# version 14
# baseline (speedup 1.0000x reference)
import numpy as np
import ml_dtypes
from contextlib import ExitStack
import concourse.bass as bass
import concourse.mybir as mybir
from concourse.bass_utils import run_bass_kernel_spmd

F32 = mybir.dt.float32
BF16 = mybir.dt.bfloat16
I32 = mybir.dt.int32
AF = mybir.ActivationFunctionType
ALU = mybir.AluOpType
BF = ml_dtypes.bfloat16
PAIRS = [[0, 1], [2, 3], [4, 5], [6, 7]]
ENGINES = ("tensor", "vector", "scalar", "gpsimd", "sync")

D = 2048
KC = 16
T = 1024


class Prog:

    def __init__(self, nc):
        self.nc = nc
        self.es = ExitStack()
        self.cmds = {e: [] for e in ENGINES}
        self.sems = {}
        self.ctr = {}
        self._waited = {e: {} for e in ENGINES}
        self._selfinc = {e: {} for e in ENGINES}
        self._async = set()

    def sem(self, name):
        if name not in self.sems:
            self.sems[name] = self.es.enter_context(self.nc.semaphore(name))
            self.ctr[name] = 0
        return self.sems[name]

    def sb(self, name, shape, dtype):
        return self.es.enter_context(self.nc.sbuf_tensor(name, shape, dtype))

    def psum(self, name, shape, dtype=F32):
        return self.es.enter_context(self.nc.psum_tensor(name, shape, dtype))

    def emit(self, eng, fn, waits=(), inc=None, inc_n=1):
        w = []
        for s, v in waits:
            if v is None or v <= 0:
                continue
            self.sem(s)
            if self._waited[eng].get(s, 0) >= v:
                continue
            self._waited[eng][s] = v
            w.append((s, v))
        after = None
        if inc is not None:
            self.sem(inc)
            self.ctr[inc] += inc_n
            after = self.ctr[inc]
            if inc_n != 1 or inc.startswith("d"):
                self._async.add(inc)
        self.cmds[eng].append((fn, tuple(w), (inc, inc_n) if inc else None))
        return after

    def finalize(self):
        nc = self.nc

        def runner(eng_name):
            def run(engobj):
                for fn, waits, inc in self.cmds[eng_name]:
                    for s, v in waits:
                        engobj.wait_ge(self.sems[s], v)
                    if fn is None:
                        continue
                    inst = fn(engobj)
                    if inc is not None:
                        inst.then_inc(self.sems[inc[0]], inc[1])
            return run

        with nc.Block() as block:
            block.tensor(runner("tensor"))
            block.vector(runner("vector"))
            block.scalar(runner("scalar"))
            block.gpsimd(runner("gpsimd"))
            block.sync(runner("sync"))
        self.es.close()


class Bank:

    def __init__(self, p, name, shape=(128, 512), dtype=F32, scope=None):
        alloc = scope.enter_context if scope is not None else p.es.enter_context
        self.pt = alloc(p.nc.psum_tensor(name, list(shape), dtype))
        self.readers = []

    def acquire(self):
        w, self.readers = self.readers, []
        return w

    def reader(self, sem, val):
        self.readers.append((sem, val))



def frearr(ap):
    return ap.rearrange("(kc kp) n -> kp kc n", kp=128)


class WLoader:

    def __init__(self, p, name, nbuf=2, nk=KC, S=512, scope=None):
        self.p = p
        self.S = S
        self.nk = nk
        alloc = (scope.enter_context if scope is not None else p.es.enter_context)
        self.bufs = [alloc(p.nc.sbuf_tensor(f"{name}{i}", [128, nk, S], BF16))
                     for i in range(nbuf)]
        self.n = nbuf
        self.sem = [f"d_{name}{i}" for i in range(nbuf)]
        for s in self.sem:
            p.sem(s)
        self.last_reader = [None] * nbuf
        self.i = 0

    extra_wait = None
    queue = "gpsimd"

    def load(self, dram_ap, col0, ncols=None, nk=None):
        ncols = ncols or self.S
        nk = nk or self.nk
        i = self.i
        self.i = (self.i + 1) % self.n
        buf = self.bufs[i]
        waits = [self.last_reader[i]] if self.last_reader[i] else []
        if self.extra_wait is not None:
            waits = waits + [self.extra_wait]
        src = dram_ap[:, col0:col0 + ncols].rearrange("(kc kp) n -> kp kc n", kp=128)
        done = self.p.emit(
            self.queue,
            lambda g, buf=buf, src=src, nc_=ncols, nk=nk: g.dma_start(
                buf[:, 0:nk, 0:nc_], src),
            waits=waits, inc=self.sem[i], inc_n=16)
        return i, buf, (self.sem[i], done)

    def consumed(self, i, sem, val):
        self.last_reader[i] = (sem, val)



CONST_SPECS = (("onescol", [128, 1], BF16), ("onesrow", [1, 128], BF16),
               ("ident", [128, 128], BF16), ("onescol16", [128, 16, 16], BF16),
               ("rowsel16", [16, 16, 128], BF16), ("fconsts", [128, 8], F32),
               ("identf", [128, 128], F32), ("id32f", [32, 32], F32),
               ("id32b", [32, 32], BF16), ("ones32", [32, 256], F32),
               ("e32b", [32, 2048], BF16), ("mask01", [128, 2, 256], BF16))
FCONST_VALS = (1e-5, 1.0, 0.5, 2.0)


def make_consts():
    c = {}
    c["onescol"] = np.ones((128, 1), BF)
    c["onesrow"] = np.ones((1, 128), BF)
    c["ident"] = np.eye(128, dtype=np.float32).astype(BF)
    oc16 = np.zeros((128, 16, 16), np.float32)
    for h in range(16):
        oc16[:, h, h] = 1.0
    c["onescol16"] = oc16.astype(BF)
    rs16 = np.zeros((16, 16, 128), np.float32)
    for h in range(16):
        rs16[h, h, :] = 1.0
    c["rowsel16"] = rs16.astype(BF)
    fc = np.zeros((128, 8), np.float32)
    fc[:, :len(FCONST_VALS)] = np.asarray(FCONST_VALS)[None, :]
    c["fconsts"] = fc
    c["identf"] = np.eye(128, dtype=np.float32)
    c["id32f"] = np.eye(32, dtype=np.float32)
    c["id32b"] = np.eye(32, dtype=np.float32).astype(BF)
    c["ones32"] = np.ones((32, 256), np.float32)
    e32 = np.zeros((32, 2048), np.float32)
    for h in range(32):
        e32[h, h * 64:(h + 1) * 64] = 1.0
    c["e32b"] = e32.astype(BF)
    m01 = np.zeros((128, 2, 256), np.float32)
    for jt in range(2):
        for j in range(128):
            m01[j, jt, jt * 128 + j:] = 1.0
    c["mask01"] = m01.astype(BF)
    return c


_SCOPED_CONST_N = [0]


def scoped_consts(nc, p, CON, names, scope):
    _SCOPED_CONST_N[0] += 1
    sem = f"dC{_SCOPED_CONST_N[0]}"
    done = 0
    spec = {n: (s, d) for n, s, d in CONST_SPECS}
    for name in names:
        shape, dt = spec[name]
        dram = nc.dram_tensor(name, shape, dt, kind="ExternalInput")
        CON[name] = scope.enter_context(nc.sbuf_tensor("c_" + name, shape, dt))
        done = p.emit("sync",
                      lambda s, h=CON[name], d=dram: s.dma_start(h[...], d.ap()),
                      inc=sem, inc_n=16)
    CON["_ready"] = (sem, done)
    return CON


def setup_consts(nc, p, names=None):
    CON = {}
    done = 0
    for name, shape, dt in CONST_SPECS:
        if names is not None and name not in names:
            continue
        dram = nc.dram_tensor(name, shape, dt, kind="ExternalInput")
        CON[name] = p.sb("c_" + name, shape, dt)
        done = p.emit("sync",
                      lambda s, h=CON[name], d=dram: s.dma_start(h[...], d.ap()),
                      inc="dIO", inc_n=16)
    CON["_ready"] = ("dIO", done)
    if "fconsts" in CON:
        for i, val in enumerate(FCONST_VALS):
            nc.const_aps.register(F32, val, CON["fconsts"][:, i:i + 1])
    return CON



def emit_l0_attn(p, io, upto=99):
    nc = p.nc
    SB_X, CON = io["SB_X"], io["CON"]
    scope = ExitStack()
    sb = lambda n, s, d: scope.enter_context(nc.sbuf_tensor(n, s, d))
    wl = WLoader(p, "wl0a", scope=scope)
    scoped_consts(nc, p, CON, ("onescol16", "rowsel16"), scope)
    bank = [Bank(p, "l0_psA", scope=scope), Bank(p, "l0_psB", scope=scope)]
    xready = io["x_ready"]

    scope_kv = ExitStack()
    sbk = lambda n, s, d: scope_kv.enter_context(nc.sbuf_tensor(n, s, d))
    SB_KT = sbk("l0_kt", [128, 8, D], BF16)
    SB_VT = sbk("l0_vt", [128, 8, D], BF16)

    def tokmajor_proj(wdram, SB_OUT, relu_scale):
        for j in range(4):
            bi, wbuf, wready = wl.load(wdram, j * 512)
            last_mm = 0
            for tt in range(8):
                bk = bank[(j * 8 + tt) % 2]
                waits = [wready, xready] + bk.acquire()
                for kc in range(KC):
                    last_mm = p.emit(
                        "tensor",
                        lambda t, bk=bk, kc=kc, tt=tt, wbuf=wbuf: t.matmul(
                            bk.pt[:, :], SB_X[:, kc, tt * 128:(tt + 1) * 128],
                            wbuf[:, kc, :],
                            start=(kc == 0), stop=(kc == KC - 1)),
                        waits=waits if kc == 0 else (),
                        inc="sT" if kc == KC - 1 else None)
                ev = p.emit(
                    "scalar",
                    lambda s, bk=bk, tt=tt, j=j, SB_OUT=SB_OUT: s.activation(
                        SB_OUT[:, tt, j * 512:(j + 1) * 512], bk.pt[:, :],
                        AF.Relu, scale=relu_scale),
                    waits=[("sT", last_mm)], inc="sA")
                bk.reader("sA", ev)
            wl.consumed(bi, "sT", last_mm)
        return ev

    SCALE = 1.0 / np.sqrt(128.0)
    tokmajor_proj(io["Wk"], SB_KT, SCALE)
    if upto <= 0:
        scope_kv.close(); scope.close(); return

    def tokmajor_proj_plain(wdram, SB_OUT):
        for j in range(4):
            bi, wbuf, wready = wl.load(wdram, j * 512)
            last_mm = 0
            for tt in range(8):
                bk = bank[(j * 8 + tt) % 2]
                waits = [wready, xready] + bk.acquire()
                for kc in range(KC):
                    last_mm = p.emit(
                        "tensor",
                        lambda t, bk=bk, kc=kc, tt=tt, wbuf=wbuf: t.matmul(
                            bk.pt[:, :], SB_X[:, kc, tt * 128:(tt + 1) * 128],
                            wbuf[:, kc, :],
                            start=(kc == 0), stop=(kc == KC - 1)),
                        waits=waits if kc == 0 else (),
                        inc="sT" if kc == KC - 1 else None)
                ev = p.emit(
                    "vector",
                    lambda v, bk=bk, tt=tt, j=j, SB_OUT=SB_OUT: v.tensor_copy(
                        SB_OUT[:, tt, j * 512:(j + 1) * 512], bk.pt[:, :]),
                    waits=[("sT", last_mm)], inc="sV")
                bk.reader("sV", ev)
            wl.consumed(bi, "sT", last_mm)

    tokmajor_proj_plain(io["Wv"], SB_VT)
    if upto <= 1:
        scope_kv.close(); scope.close(); return

    SB_KV = sbk("l0_kv", [128, KC, 128], F32)
    kvdone = 0
    for hq in range(4):
        bk = bank[hq % 2]
        waits0 = bk.acquire()
        lastmm = 0
        for hh in range(4):
            h = hq * 4 + hh
            for tt in range(8):
                lastmm = p.emit(
                    "tensor",
                    lambda t, bk=bk, hh=hh, h=h, tt=tt: t.matmul(
                        bk.pt[:, hh * 128:(hh + 1) * 128],
                        SB_KT[:, tt, h * 128:(h + 1) * 128],
                        SB_VT[:, tt, h * 128:(h + 1) * 128],
                        start=(tt == 0), stop=(tt == 7)),
                    waits=waits0 if (hh == 0 and tt == 0) else (),
                    inc="sT" if (hh == 3 and tt == 7) else None)
        kvdone = p.emit(
            "vector",
            lambda v, bk=bk, hq=hq: v.tensor_copy(
                SB_KV[:, hq * 4:(hq + 1) * 4, :].rearrange("p h v -> p (h v)"),
                bk.pt[:, :]),
            waits=[("sT", lastmm)], inc="sV")
        bk.reader("sV", kvdone)

    cc_in = nc.dram_tensor("l0_kv_in", [128, D], F32, kind="Internal")
    cc_out = nc.dram_tensor("l0_kv_out", [128, D], F32, kind="Internal")
    st1 = p.emit("sync",
                 lambda s: s.dma_start(cc_in.ap(),
                                       SB_KV[:, :, :].rearrange("p h v -> p (h v)")),
                 waits=[("sV", kvdone)], inc="dCC", inc_n=16)
    ccd = p.emit("gpsimd",
                 lambda g: g.collective_compute(
                     "AllReduce", ALU.add, replica_groups=PAIRS,
                     ins=[cc_in.ap().opt()], outs=[cc_out.ap().opt()]),
                 waits=[("dCC", st1)], inc="dCC", inc_n=1)
    scope_kv.close()
    kv_staged = ("dCC", st1)
    if upto <= 2:
        scope.close(); return

    scope_qu = ExitStack()
    SB_Q = scope_qu.enter_context(nc.sbuf_tensor("l0_q", [128, KC, T], BF16, side="right"))
    SB_U = scope_qu.enter_context(nc.sbuf_tensor("l0_u", [128, KC, T], BF16, side="right"))

    def featmajor_proj(wdram, SB_OUT, act=None, act_scale=1.0):
        last = 0
        for j in range(4):
            bi, wbuf, wready = wl.load(wdram, j * 512)
            last_mm = 0
            for oc in range(4):
                for tc in range(2):
                    bk = bank[(oc * 2 + tc) % 2]
                    waits = [wready, xready, kv_staged] + bk.acquire()
                    for kc in range(KC):
                        last_mm = p.emit(
                            "tensor",
                            lambda t, bk=bk, kc=kc, oc=oc, tc=tc, wbuf=wbuf: t.matmul(
                                bk.pt[:, :], wbuf[:, kc, oc * 128:(oc + 1) * 128],
                                SB_X[:, kc, tc * 512:(tc + 1) * 512],
                                start=(kc == 0), stop=(kc == KC - 1)),
                            waits=waits if kc == 0 else (),
                            inc="sT" if kc == KC - 1 else None)
                    fc = j * 4 + oc
                    if act is None:
                        last = p.emit(
                            "vector",
                            lambda v, bk=bk, fc=fc, tc=tc, SB_OUT=SB_OUT: v.tensor_copy(
                                SB_OUT[:, fc, tc * 512:(tc + 1) * 512], bk.pt[:, :]),
                            waits=[("sT", last_mm)], inc="sV")
                        bk.reader("sV", last)
                    else:
                        last = p.emit(
                            "scalar",
                            lambda s, bk=bk, fc=fc, tc=tc, SB_OUT=SB_OUT, act=act,
                                   asc=act_scale: s.activation(
                                SB_OUT[:, fc, tc * 512:(tc + 1) * 512], bk.pt[:, :],
                                act, scale=asc),
                            waits=[("sT", last_mm)], inc="sA")
                        bk.reader("sA", last)
            wl.consumed(bi, "sT", last_mm)
        return last

    featmajor_proj(io["Wq"], SB_Q, act=AF.Relu, act_scale=SCALE)
    featmajor_proj(io["Wu"], SB_U)
    if upto <= 3:
        scope_qu.close(); scope.close(); return

    SB_KVb = sb("l0_kvb", [128, KC, 128], BF16)
    kvb_d = p.emit("gpsimd",
                   lambda g: g.dma_start(
                       SB_KVb[:, :, :].rearrange("p h v -> p (h v)"), cc_out.ap()),
                   waits=[("dCC", ccd)], inc="dKV", inc_n=16)

    SB_O = sb("l0_o", [128, KC, T], BF16)
    SB_SQ = sb("l0_sq", [128, T], BF16)
    psR = [Bank(p, "l0_psR0", (16, 512), scope=scope),
           Bank(p, "l0_psR1", (16, 512), scope=scope)]
    ssmm = [0, 0]
    for h in range(16):
        for tc in range(2):
            bk = bank[tc]
            waits = [("dKV", kvb_d)] + bk.acquire()
            mm = p.emit(
                "tensor",
                lambda t, bk=bk, h=h, tc=tc: t.matmul(
                    bk.pt[:, :], SB_KVb[:, h, :], SB_Q[:, h, tc * 512:(tc + 1) * 512],
                    start=True, stop=True),
                waits=waits, inc="sT")
            ev = p.emit(
                "vector",
                lambda v, bk=bk, h=h, tc=tc: v.tensor_copy(
                    SB_O[:, h, tc * 512:(tc + 1) * 512], bk.pt[:, :]),
                waits=[("sT", mm)], inc="sV")
            bk.reader("sV", ev)
            if upto <= 3.2:
                ssmm[tc] = 0
                continue
            sq = p.emit(
                "scalar",
                lambda s, h=h, tc=tc: s.activation(
                    SB_SQ[:, tc * 512:(tc + 1) * 512],
                    SB_O[:, h, tc * 512:(tc + 1) * 512], AF.Square),
                waits=[("sV", ev)], inc="sA")
            if upto <= 3.5:
                ssmm[tc] = 0
                continue
            ssmm[tc] = p.emit(
                "tensor",
                lambda t, h=h, tc=tc: t.matmul(
                    psR[tc].pt[:, :], CON["onescol16"][:, h, :],
                    SB_SQ[:, tc * 512:(tc + 1) * 512],
                    start=(h == 0), stop=(h == 15), skip_group_check=True),
                waits=[("sA", sq)], inc="sT")

    if upto <= 3.7:
        scope_qu.close(); scope.close(); return
    SB_SS = sb("l0_ss", [16, T], F32)
    s1 = p.emit("vector", lambda v: v.tensor_copy(SB_SS[:, 0:512], psR[0].pt[:, :]),
                waits=[("sT", ssmm[0])], inc="sV")
    s2 = p.emit("vector", lambda v: v.tensor_copy(SB_SS[:, 512:1024], psR[1].pt[:, :]),
                waits=[("sT", ssmm[1])], inc="sV")
    psR[0].reader("sV", s1)
    psR[1].reader("sV", s2)
    g1 = p.emit("vector", lambda v: v.tensor_scalar(
        SB_SS[:, :], SB_SS[:, :], 1.0 / 128.0, 1e-24, ALU.mult, ALU.max),
        waits=[("sV", s2)], inc="sV")
    g2 = p.emit("scalar", lambda s: s.activation(SB_SS[:, :], SB_SS[:, :], AF.Ln),
                waits=[("sV", g1)], inc="sA")
    g3 = p.emit("scalar", lambda s: s.activation(SB_SS[:, :], SB_SS[:, :], AF.Exp,
                                                 scale=-0.5),
                waits=[("sA", g2)], inc="sA")
    SB_SSb = sb("l0_ssb", [16, T], BF16)
    g4 = p.emit("vector", lambda v: v.tensor_copy(SB_SSb[:, :], SB_SS[:, :]),
                waits=[("sA", g3)], inc="sV")
    if upto <= 4:
        scope_qu.close(); scope.close(); return

    SB_Y = SB_O
    SB_T1 = sb("l0_t1", [128, T], BF16)
    for h in range(16):
        sil = p.emit(
            "scalar",
            lambda s, h=h: s.activation(SB_T1[:, :], SB_U[:, h, :], AF.Silu),
            inc="sA")
        t2 = p.emit(
            "vector",
            lambda v, h=h: v.tensor_mul(SB_T1[:, :], SB_O[:, h, :], SB_T1[:, :]),
            waits=[("sA", sil)], inc="sV")
        for tc in range(2):
            bk = bank[tc]
            waits = [("sV", g4)] + bk.acquire()
            bc = p.emit(
                "tensor",
                lambda t, bk=bk, h=h, tc=tc: t.matmul(
                    bk.pt[:, :], CON["rowsel16"][:, h, :],
                    SB_SSb[:, tc * 512:(tc + 1) * 512],
                    start=True, stop=True),
                waits=waits, inc="sT")
            y = p.emit(
                "vector",
                lambda v, bk=bk, h=h, tc=tc: v.tensor_mul(
                    SB_Y[:, h, tc * 512:(tc + 1) * 512],
                    SB_T1[:, tc * 512:(tc + 1) * 512], bk.pt[:, :]),
                waits=[("sT", bc), ("sV", t2)], inc="sV")
            bk.reader("sV", y)
        p.emit("scalar", None, waits=[("sV", y)])

    if upto <= 5:
        scope_qu.close(); scope.close(); return
    scope_qu.close()
    SB_S = scope.enter_context(nc.sbuf_tensor("l0_s", [128, KC, T], BF16, side="right"))
    sdone = 0
    for j in range(4):
        bi, wbuf, wready = wl.load(io["Wo"], j * 512)
        last_mm = 0
        for oc in range(4):
            for tc in range(2):
                bk = bank[(oc * 2 + tc) % 2]
                waits = [wready] + bk.acquire()
                for kc in range(KC):
                    last_mm = p.emit(
                        "tensor",
                        lambda t, bk=bk, kc=kc, oc=oc, tc=tc, wbuf=wbuf: t.matmul(
                            bk.pt[:, :], wbuf[:, kc, oc * 128:(oc + 1) * 128],
                            SB_Y[:, kc, tc * 512:(tc + 1) * 512],
                            start=(kc == 0), stop=(kc == KC - 1)),
                        waits=waits if kc == 0 else (),
                        inc="sT" if kc == KC - 1 else None)
                fc = j * 4 + oc
                sdone = p.emit(
                    "vector",
                    lambda v, bk=bk, fc=fc, tc=tc: v.tensor_add(
                        SB_S[:, fc, tc * 512:(tc + 1) * 512], bk.pt[:, :],
                        SB_X[:, fc, tc * 512:(tc + 1) * 512]),
                    waits=[("sT", last_mm)], inc="sV")
                bk.reader("sV", sdone)
        wl.consumed(bi, "sT", last_mm)

    if upto <= 6:
        io["x_ready"] = ("sV", sdone); scope.close(); return
    emit_ln(p, io, SB_S, SB_X, ("sV", sdone), scope, "ln0", bank)
    scope.close()


def emit_ln(p, io, SB_IN, SB_OUT, in_ready, scope, tag, bank, out_f32=None):
    nc = p.nc
    CON = io["CON"]
    sb = lambda n, s, d: scope.enter_context(nc.sbuf_tensor(n, s, d))
    rows = [Bank(p, f"{tag}_r{i}", (1, 512), scope=scope) for i in range(4)]
    SB_SQ = sb(f"{tag}_sq", [128, T], BF16)
    ROWS = sb(f"{tag}_rows", [1, 4 * T], F32)
    smm = [0, 0]
    qmm = [0, 0]
    sqd = 0
    prev_q = 0
    for fc in range(KC):
        sqd = p.emit("scalar",
                     lambda s, fc=fc: s.activation(SB_SQ[:, :], SB_IN[:, fc, :],
                                                   AF.Square),
                     waits=[in_ready, ("sT", prev_q)], inc="sA")
        for tc in range(2):
            smm[tc] = p.emit("tensor",
                             lambda t, fc=fc, tc=tc: t.matmul(
                                 rows[tc].pt[:, :], CON["onescol"][:, :],
                                 SB_IN[:, fc, tc * 512:(tc + 1) * 512],
                                 start=(fc == 0), stop=(fc == KC - 1),
                                 skip_group_check=True),
                             waits=[in_ready] if fc == 0 else (),
                             inc="sT")
            qmm[tc] = p.emit("tensor",
                             lambda t, fc=fc, tc=tc: t.matmul(
                                 rows[2 + tc].pt[:, :], CON["onescol"][:, :],
                                 SB_SQ[:, tc * 512:(tc + 1) * 512],
                                 start=(fc == 0), stop=(fc == KC - 1),
                                 skip_group_check=True),
                             waits=[("sA", sqd)],
                             inc="sT")
        prev_q = qmm[1]
    e1 = 0
    for i, off in ((0, 0), (1, 512), (2, 1024), (3, 1536)):
        e1 = p.emit("vector",
                    lambda v, i=i, off=off: v.tensor_copy(
                        ROWS[:, off:off + 512], rows[i].pt[:, :]),
                    waits=[("sT", max(smm + qmm))], inc="sV")
        rows[i].reader("sV", e1)
    e2 = p.emit("vector", lambda v: v.tensor_scalar(
        ROWS[:, 0:2048], ROWS[:, 0:2048], 1.0 / 2048.0, None, ALU.mult),
        waits=[("sV", e1)], inc="sV")
    e3 = p.emit("vector", lambda v: v.tensor_mul(
        ROWS[:, 2048:3072], ROWS[:, 0:1024], ROWS[:, 0:1024]),
        waits=[("sV", e2)], inc="sV")
    e4 = p.emit("vector", lambda v: v.tensor_sub(
        ROWS[:, 2048:3072], ROWS[:, 1024:2048], ROWS[:, 2048:3072]),
        waits=[("sV", e3)], inc="sV")
    e5 = p.emit("scalar", lambda s: s.activation(
        ROWS[:, 2048:3072], ROWS[:, 2048:3072], AF.Ln, bias=1e-5),
        waits=[("sV", e4)], inc="sA")
    e6 = p.emit("scalar", lambda s: s.activation(
        ROWS[:, 2048:3072], ROWS[:, 2048:3072], AF.Exp, scale=-0.5),
        waits=[("sA", e5)], inc="sA")
    e7 = p.emit("vector", lambda v: v.tensor_mul(
        ROWS[:, 3072:4096], ROWS[:, 0:1024], ROWS[:, 2048:3072]),
        waits=[("sA", e6)], inc="sV")
    e8 = p.emit("vector", lambda v: v.tensor_scalar(
        ROWS[:, 3072:4096], ROWS[:, 3072:4096], -1.0, None, ALU.mult),
        waits=[("sV", e7)], inc="sV")
    RB = sb(f"{tag}_rb", [1, 2 * T], BF16)
    e9 = p.emit("vector", lambda v: v.tensor_copy(RB[:, :], ROWS[:, 2048:4096]),
                waits=[("sV", e8)], inc="sV")
    BCR = sb(f"{tag}_bcr", [128, T], BF16)
    BCC = sb(f"{tag}_bcc", [128, T], BF16)
    bcdone = 0
    for ri, BC in ((0, BCR), (1, BCC)):
        for tc in range(2):
            bk = bank[tc]
            waits = [("sV", e9)] + bk.acquire()
            mm = p.emit("tensor",
                        lambda t, bk=bk, ri=ri, tc=tc: t.matmul(
                            bk.pt[:, :], CON["onesrow"][:, :],
                            RB[:, ri * T + tc * 512: ri * T + (tc + 1) * 512],
                            start=True, stop=True),
                        waits=waits, inc="sT")
            bcdone = p.emit("vector",
                            lambda v, bk=bk, BC=BC, tc=tc: v.tensor_copy(
                                BC[:, tc * 512:(tc + 1) * 512], bk.pt[:, :]),
                            waits=[("sT", mm)], inc="sV")
            bk.reader("sV", bcdone)
    last = 0
    for fc in range(KC):
        tgt = (SB_OUT if out_f32 is None else out_f32)
        last = p.emit("vector",
                      lambda v, fc=fc: v.tensor_mul(SB_SQ[:, :], SB_IN[:, fc, :],
                                                    BCR[:, :]),
                      waits=[("sV", bcdone), ("sV", last)], inc="sV")
        last = p.emit("vector",
                      lambda v, fc=fc, tgt=tgt: v.tensor_add(
                          tgt[:, fc, :], SB_SQ[:, :], BCC[:, :]),
                      waits=[("sV", last)], inc="sV")
    io["x_ready"] = ("sV", last)
    return last



def build_l0_attn(upto=99):
    nc = bass.Bass("TRN2", target_bir_lowering=False)
    x_in = nc.dram_tensor("x_in", [D, T], F32, kind="ExternalInput")
    wd = {}
    for w in ("Wq", "Wk", "Wv", "Wu", "Wo"):
        wd[w] = nc.dram_tensor(w, [D, D], F32, kind="ExternalInput")
    x_out = nc.dram_tensor("x_out", [KC, 128, T], BF16, kind="ExternalOutput")

    p = Prog(nc)
    for s in ("sT", "sV", "sA", "dIO", "dCC", "dX"):
        p.sem(s)
    CON = setup_consts(nc, p, names=("onescol", "onesrow", "fconsts"))
    SB_X = p.sb("sb_x", [128, KC, T], BF16)
    xl = p.emit("gpsimd", lambda g: g.dma_start(SB_X[:, :, :], frearr(x_in.ap())),
                inc="dX", inc_n=16)
    p.emit("tensor", None, waits=[CON["_ready"]])
    io = {"SB_X": SB_X, "CON": CON,
          "x_ready": ("dX", xl)}
    for w in wd:
        io[w] = wd[w].ap()
    emit_l0_attn(p, io, upto=upto)
    xs, xv = io["x_ready"]
    out = p.emit("sync", lambda s: s.dma_start(
        x_out.ap().rearrange("kc kp n -> kp kc n"), SB_X[:, :, :]),
        waits=[(xs, xv)], inc="dIO", inc_n=16)
    p.emit("sync", None, waits=[("dIO", out)])
    p.finalize()
    return nc





def emit_l0_ffn(p, io):
    nc = p.nc
    SB_X, CON = io["SB_X"], io["CON"]
    scope = ExitStack()
    sb = lambda n, s, d: scope.enter_context(nc.sbuf_tensor(n, s, d))
    wl = WLoader(p, "wl0f", scope=scope)
    GSC, USC, DSC = io["gs"], io["us"], io["ds"]
    bank = [Bank(p, f"ff0_ps{i}", scope=scope) for i in range(4)]
    scope_rows = ExitStack()
    rows = [Bank(p, f"ff0_r{i}", (1, 512), scope=scope_rows) for i in range(4)]
    xready = io["x_ready"]

    scope_acc = ExitStack()
    ACC = scope_acc.enter_context(nc.sbuf_tensor("ff0_acc", [128, KC, T], F32, side="right"))
    SB_H = sb("ff0_h", [128, KC, T], BF16)
    SB_G = sb("ff0_g", [128, 512], BF16)
    NR = sb("ff0_nr", [1, T], BF16)
    SHB = sb("ff0_shb", [128, T], BF16)

    smm = [0, 0]
    for fc in range(KC):
        for tc in range(2):
            smm[tc] = p.emit("tensor",
                             lambda t, fc=fc, tc=tc: t.matmul(
                                 rows[tc].pt[:, :], CON["onescol"][:, :],
                                 SB_X[:, fc, tc * 512:(tc + 1) * 512],
                                 start=(fc == 0), stop=(fc == KC - 1),
                                 skip_group_check=True),
                             waits=[xready] if fc == 0 else (),
                             inc="sT")
    nr = 0
    for tc in range(2):
        nr = p.emit("vector",
                    lambda v, tc=tc: v.tensor_scalar(
                        NR[:, tc * 512:(tc + 1) * 512], rows[tc].pt[:, :],
                        -128.0, None, ALU.mult),
                    waits=[("sT", smm[tc])], inc="sV")
        rows[tc].reader("sV", nr)

    IG = 4
    acc_last = [[0, 0] for _ in range(KC)]
    h_last = {}
    shmm = [0, 0]
    for ig in range(4):
        for isl in range(4):
            gb, gbuf, gready = wl.load(io["gwT"], ig * 2048 + isl * 512)
            ub, ubuf, uready = wl.load(io["uwT"], ig * 2048 + isl * 512)
            lg = lu = 0
            for tc in range(2):
                for oc in range(4):
                    ic = ig * 16 + isl * 4 + oc
                    hc = isl * 4 + oc
                    bkg, bku = bank[0 + (oc % 2)], bank[2 + (oc % 2)]
                    gmm = 0
                    wg = [gready, xready, ("sV", nr)] + bkg.acquire()
                    for kc in range(KC):
                        gmm = p.emit("tensor",
                                     lambda t, bkg=bkg, kc=kc, oc=oc, tc=tc, gbuf=gbuf: t.matmul(
                                         bkg.pt[:, :], gbuf[:, kc, oc * 128:(oc + 1) * 128],
                                         SB_X[:, kc, tc * 512:(tc + 1) * 512],
                                         start=(kc == 0), stop=False),
                                     waits=wg if kc == 0 else (), inc=None)
                    gmm = p.emit("tensor",
                                 lambda t, bkg=bkg, tc=tc: t.matmul(
                                     bkg.pt[:, :], CON["onesrow"][:, :],
                                     NR[:, tc * 512:(tc + 1) * 512],
                                     start=False, stop=True),
                                 inc="sT")
                    lg = gmm
                    gsil = p.emit("scalar",
                                  lambda s, bkg=bkg, ic=ic: s.activation(
                                      SB_G[:, :], bkg.pt[:, :], AF.Silu,
                                      scale=GSC[:, ic:ic + 1]),
                                  waits=[("sT", gmm)], inc="sA")
                    bkg.reader("sA", gsil)
                    umm = 0
                    wu = [uready, ("sV", nr)] + bku.acquire()
                    for kc in range(KC):
                        umm = p.emit("tensor",
                                     lambda t, bku=bku, kc=kc, oc=oc, tc=tc, ubuf=ubuf: t.matmul(
                                         bku.pt[:, :], ubuf[:, kc, oc * 128:(oc + 1) * 128],
                                         SB_X[:, kc, tc * 512:(tc + 1) * 512],
                                         start=(kc == 0), stop=False),
                                     waits=wu if kc == 0 else (), inc=None)
                    umm = p.emit("tensor",
                                 lambda t, bku=bku, tc=tc: t.matmul(
                                     bku.pt[:, :], CON["onesrow"][:, :],
                                     NR[:, tc * 512:(tc + 1) * 512],
                                     start=False, stop=True),
                                 inc="sT")
                    lu = umm
                    hv = p.emit("vector",
                                lambda v, bku=bku, hc=hc, tc=tc, ic=ic: v.scalar_tensor_tensor(
                                    SB_H[:, hc, tc * 512:(tc + 1) * 512],
                                    bku.pt[:, :], USC[:, ic:ic + 1], SB_G[:, :],
                                    ALU.mult, ALU.mult),
                                waits=[("sT", umm), ("sA", gsil)], inc="sV")
                    bku.reader("sV", hv)
                    h_last[(hc, tc)] = hv
                    p.emit("scalar", None, waits=[("sV", hv)])
                    shmm[tc] = p.emit("tensor",
                                      lambda t, hc=hc, tc=tc,
                                             st=(ig == 0 and isl == 0 and oc == 0),
                                             sp=(ig == 3 and isl == 3 and oc == 3): t.matmul(
                                          rows[2 + tc].pt[:, :], CON["onescol"][:, :],
                                          SB_H[:, hc, tc * 512:(tc + 1) * 512],
                                          start=st, stop=sp,
                                          skip_group_check=True),
                                      waits=[("sV", hv)], inc="sT")
            wl.consumed(gb, "sT", lg)
            wl.consumed(ub, "sT", lu)
        for jd in range(4):
            db, dbuf, dready = wl.load(io["dwT"].rearr_rows(ig), jd * 512)
            for oc in range(4):
                for tc in range(2):
                    fc = jd * 4 + oc
                    bk = bank[tc]
                    dmm = 0
                    wd = [dready] + bk.acquire() + [("sV", h_last[(kch, tc)]) for kch in range(KC)]
                    for kch in range(KC):
                        dmm = p.emit("tensor",
                                     lambda t, bk=bk, kch=kch, oc=oc, tc=tc, dbuf=dbuf: t.matmul(
                                         bk.pt[:, :], dbuf[:, kch, oc * 128:(oc + 1) * 128],
                                         SB_H[:, kch, tc * 512:(tc + 1) * 512],
                                         start=(kch == 0), stop=(kch == KC - 1)),
                                     waits=wd if kch == 0 else (), inc="sT" if kch == KC - 1 else None)
                    if ig == 0:
                        ae = p.emit("vector",
                                    lambda v, bk=bk, fc=fc, tc=tc: v.tensor_copy(
                                        ACC[:, fc, tc * 512:(tc + 1) * 512], bk.pt[:, :]),
                                    waits=[("sT", dmm), ("sV", acc_last[fc][tc])], inc="sV")
                    else:
                        ae = p.emit("vector",
                                    lambda v, bk=bk, fc=fc, tc=tc: v.tensor_add(
                                        ACC[:, fc, tc * 512:(tc + 1) * 512], bk.pt[:, :],
                                        ACC[:, fc, tc * 512:(tc + 1) * 512]),
                                    waits=[("sT", dmm), ("sV", acc_last[fc][tc])], inc="sV")
                    acc_last[fc][tc] = ae
                    bk.reader("sV", ae)
            wl.consumed(db, "sT", dmm)

    shb = 0
    for tc in range(2):
        shb = p.emit("vector",
                     lambda v, tc=tc: v.tensor_scalar(
                         NR[:, tc * 512:(tc + 1) * 512], rows[2 + tc].pt[:, :],
                         128.0, None, ALU.mult),
                     waits=[("sT", shmm[tc]), ("sT", p.ctr["sT"])], inc="sV")
        rows[2 + tc].reader("sV", shb)
    shbd = 0
    for tc in range(2):
        bk = bank[2 + tc]
        waits = [("sV", shb)] + bk.acquire()
        mm = p.emit("tensor",
                    lambda t, bk=bk, tc=tc: t.matmul(
                        bk.pt[:, :], CON["onesrow"][:, :],
                        NR[:, tc * 512:(tc + 1) * 512], start=True, stop=True),
                    waits=waits, inc="sT")
        shbd = p.emit("vector",
                      lambda v, bk=bk, tc=tc: v.tensor_copy(
                          SHB[:, tc * 512:(tc + 1) * 512], bk.pt[:, :]),
                      waits=[("sT", mm)], inc="sV")
        bk.reader("sV", shbd)

    SB_S = sb("ff0_s", [128, KC, T], BF16)
    sdone = 0
    for fc in range(KC):
        t1 = p.emit("vector",
                    lambda v, fc=fc: v.tensor_sub(ACC[:, fc, :], ACC[:, fc, :], SHB[:, :]),
                    waits=[("sV", max(shbd, acc_last[fc][0], acc_last[fc][1]))], inc="sV")
        sdone = p.emit("vector",
                       lambda v, fc=fc: v.scalar_tensor_tensor(
                           SB_S[:, fc, :], ACC[:, fc, :], DSC[:, fc:fc + 1],
                           SB_X[:, fc, :], ALU.mult, ALU.add),
                       waits=[("sV", t1)], inc="sV")
    scope_rows.close()
    scope_acc.close()
    emit_ln(p, io, SB_S, SB_X, ("sV", sdone), scope, "ln1", bank)
    scope.close()


class RowSlicedAP:
    def __init__(self, ap, r0, rows):
        self.ap = ap
        self.r0 = r0
        self.rows = rows

    def __getitem__(self, key):
        _, colsl = key
        return self.ap[self.r0:self.r0 + self.rows, colsl]


def build_l0_ffn():
    nc = bass.Bass("TRN2", target_bir_lowering=False)
    x_in = nc.dram_tensor("x_in", [KC, 128, T], BF16, kind="ExternalInput")
    gwT = nc.dram_tensor("gwT", [D, 8192], I32, kind="ExternalInput")
    uwT = nc.dram_tensor("uwT", [D, 8192], I32, kind="ExternalInput")
    dwT = nc.dram_tensor("dwT", [8192, D], I32, kind="ExternalInput")
    gs = nc.dram_tensor("gs", [128, 64], F32, kind="ExternalInput")
    us = nc.dram_tensor("us", [128, 64], F32, kind="ExternalInput")
    ds = nc.dram_tensor("ds", [128, 16], F32, kind="ExternalInput")
    x_out = nc.dram_tensor("x_out", [KC, 128, T], BF16, kind="ExternalOutput")

    p = Prog(nc)
    for s in ("sT", "sV", "sA", "dIO", "dX"):
        p.sem(s)
    CON = setup_consts(nc, p, names=("onescol", "onesrow", "fconsts"))
    SB_X = p.sb("sb_x", [128, KC, T], BF16)
    SBgs = p.sb("sb_gs", [128, 64], F32)
    SBus = p.sb("sb_us", [128, 64], F32)
    SBds = p.sb("sb_ds", [128, 16], F32)
    xl = p.emit("sync", lambda s: s.dma_start(
        SB_X[:, :, :], x_in.ap().rearrange("kc kp n -> kp kc n")), inc="dX", inc_n=16)
    for src_d, dst in ((gs, SBgs), (us, SBus), (ds, SBds)):
        xl = p.emit("sync", lambda s, src_d=src_d, dst=dst: s.dma_start(dst[...], src_d.ap()),
                    inc="dX", inc_n=16)
    p.emit("tensor", None, waits=[CON["_ready"], ("dX", xl)])

    class DwT:
        def __init__(self, ap):
            self.ap_ = ap
        def rearr_rows(self, ig):
            class V:
                def __init__(s2, ap, r0):
                    s2.ap = ap
                    s2.r0 = r0
                def __getitem__(s2, key):
                    _, colsl = key
                    return s2.ap[s2.r0:s2.r0 + 2048, colsl]
            return V(self.ap_, ig * 2048)

    io = {"SB_X": SB_X, "CON": CON, "x_ready": ("dX", xl),
          "gwT": gwT.ap(), "uwT": uwT.ap(), "dwT": DwT(dwT.ap()),
          "gs": SBgs, "us": SBus, "ds": SBds}
    emit_l0_ffn(p, io)
    xs, xv = io["x_ready"]
    out = p.emit("sync", lambda s: s.dma_start(
        x_out.ap().rearrange("kc kp n -> kp kc n"), SB_X[:, :, :]),
        waits=[(xs, xv)], inc="dIO", inc_n=16)
    p.emit("sync", None, waits=[("dIO", out)])
    p.finalize()
    return nc





def emit_l1_attn(p, io):
    nc = p.nc
    CON = io["CON"]
    SB_X = io["SB_X"]
    scope = ExitStack()
    sb = lambda n, s, d: scope.enter_context(nc.sbuf_tensor(n, s, d))
    wl = WLoader(p, "wl1a", S=256, scope=scope)
    scoped_consts(nc, p, CON, ("ident", "identf", "id32f", "id32b", "ones32",
                               "e32b", "mask01"), scope)
    scope_ps = ExitStack()

    bank = [Bank(p, "m_psA", scope=scope_ps), Bank(p, "m_psB", scope=scope_ps)]
    psY = Bank(p, "m_psY", (128, 1024), scope=scope_ps)
    psS = Bank(p, "m_psS", (128, 512), scope=scope_ps)
    psTB = Bank(p, "m_psTB", (128, 128), BF16, scope=scope_ps)
    psRow = Bank(p, "m_psRow", (32, 512), scope=scope_ps)
    psZ = Bank(p, "m_psZ", (128, 512), scope=scope_ps)

    S = sb("m_S", [128, 32, 64], F32)
    S_bf = sb("m_Sbf", [128, 32, 64], BF16)
    TL = sb("m_TL", [128, 24, 3], BF16)
    CDB = sb("m_cdbc", [32, 8], BF16)
    DT2 = sb("m_dt2", [32, T], F32)
    SB_DT = sb("m_dt", [32, T], F32)
    DACS = sb("m_dacs", [32, T], F32)
    WR = sb("m_wr", [32, T], F32)
    EDACS = sb("m_edacs", [32, T], BF16)
    STK = sb("m_stk", [96, 256], F32)
    TTf = sb("m_ttf", [128, 2, 96], F32)
    CBT = sb("m_cbt", [128, 2, 256], BF16)
    XT = sb("m_xt", [128, 2, 512], BF16)
    BTT = sb("m_btt", [128, 2, 128], BF16)
    XDb = sb("m_xd", [128, 2, 2, 64], BF16)
    XSb = sb("m_xs", [128, 2, 2, 64], BF16)
    EM = sb("m_em", [128, 2, 256], F32)
    EMB = sb("m_emb", [128, 2, 256], BF16)
    MTb = sb("m_mt", [128, 2, 2, 256], BF16)
    EBB = sb("m_ebb", [128, 2, 256], BF16)
    CTb = sb("m_ct", [128, 2, 256], BF16)
    DXb = sb("m_dxs", [128, 256], BF16)
    SQt = sb("m_sqt", [128, 2, 256], BF16)
    CDBT = sb("m_cdbt", [128, 8, 64], BF16)
    SSQ = sb("m_ssq", [1, T], BF16)
    CV = sb("m_cv", [128, 512], F32)
    RAW = sb("m_raw", [128, 2, 1027], BF16)
    ACOL = sb("m_acol", [32, 1], F32)
    XH = SB_X
    scope_xbc = ExitStack()
    XBC = scope_xbc.enter_context(nc.sbuf_tensor("m_xbc", [128, 24, T], BF16, side="right"))
    SB_Y = sb("m_yb", [128, KC, T], BF16)
    ZS = sb("m_zsil", [128, 512], BF16)
    OPT = sb("m_opt", [128, 512], BF16)

    ID32F, ID32B = CON["id32f"], CON["id32b"]
    contribA = nc.dram_tensor("m_contribA", [2, 1024, T], BF16, kind="Internal")
    contribB = nc.dram_tensor("m_contribB", [2, 1025, T], BF16, kind="Internal")
    rsoutA = nc.dram_tensor("m_rsoutA", [1024, T], BF16, kind="Internal")
    rsoutB = nc.dram_tensor("m_rsoutB", [1025, T], BF16, kind="Internal")

    if io.get("fused"):
        ag_in = nc.dram_tensor("m_ag_in", [D, T], BF16, kind="Internal")
        ag_out = nc.dram_tensor("m_ag_out", [2 * D, T], BF16, kind="Internal")
        xres_sc = nc.dram_tensor("m_xres_sc", [D, T], BF16, kind="Internal")
        xs, xv = io["x_ready"]
        sp1 = p.emit("sync", lambda s: s.dma_start(
            ag_in.ap().rearrange("(kc kp) n -> kp kc n", kp=128), SB_X[:, :, :]),
            waits=[(xs, xv)], inc="dOU", inc_n=16)
        sp2 = p.emit("sync", lambda s: s.dma_start(
            xres_sc.ap().rearrange("(kc kp) n -> kp kc n", kp=128), SB_X[:, :, :]),
            waits=[(xs, xv)], inc="dOU", inc_n=16)
        agd = p.emit("gpsimd", lambda g: g.collective_compute(
            "AllGather", ALU.bypass, replica_groups=PAIRS,
            ins=[ag_in.ap().opt()], outs=[ag_out.ap().opt()]),
            waits=[("dOU", sp2)], inc="dCC", inc_n=1)
        io["x_half_ap"] = lambda H: ag_out.ap()[H * D:(H + 1) * D, :].rearrange(
            "(kc kp) n -> kp kc n", kp=128)
        io["x_res"] = xres_sc.ap().rearrange("(kc kp) n -> kp kc n", kp=128)
        io["ag_ready"] = ("dCC", agd)
    else:
        io["x_half_ap"] = lambda H: io["x_full"][:, :, H * T:(H + 1) * T]
        io["ag_ready"] = ("sP", 0)
    z0 = p.emit("gpsimd", lambda g: g.memset(
        S[:, :, :].rearrange("p h q -> p (h q)"), 0.0), inc="sP")
    z0 = p.emit("gpsimd", lambda g: g.memset(
        S_bf[:, :, :].rearrange("p h q -> p (h q)"), 0.0), waits=[("sP", z0)], inc="sP")
    z0 = p.emit("gpsimd", lambda g: g.memset(
        TL[:, :, :].rearrange("p h q -> p (h q)"), 0.0), waits=[("sP", z0)], inc="sP")
    aready = io["aux_ready"]
    a1 = p.emit("scalar", lambda s: s.activation(ACOL[:, :], io["alog"][:, :], AF.Exp),
                waits=[aready, CON["_ready"]], inc="sA")
    a2 = p.emit("vector", lambda v: v.tensor_scalar(ACOL[:, :], ACOL[:, :], -1.0,
                                                    None, ALU.mult),
                waits=[("sA", a1)], inc="sV")
    initdone = ("sV", a2)
    sbf_last = ("sP", z0)
    io["_su_prev"] = 0
    s_init = ("sP", z0)
    op_dma = 0
    xds_prev = {}
    em_guard = {}
    emb_guard = {}

    for H in range(2):
        halfw = [("sT", p.ctr["sT"]), ("sV", p.ctr["sV"]), ("sA", p.ctr["sA"]),
                 ("dOU", p.ctr.get("dOU", 0))]
        xl = p.emit("sync", lambda s, H=H: s.dma_start(
            XH[:, :, :], io["x_half_ap"](H)),
            waits=halfw + [io["ag_ready"]], inc="dXH", inc_n=16)
        xready = ("dXH", xl)

        raw_ev = {0: 0, 1: 0}
        cs = 0
        for j in range(12):
            bi, wbuf, wready = wl.load(io["ipw_xbc"], j * 256)
            last_mm = 0
            for oc in range(2):
                xc = j * 2 + oc
                rb = xc % 2
                ev = 0
                for tc2 in range(2):
                    bk = bank[tc2]
                    waits = [wready, xready, ("sV", raw_ev[rb])] + bk.acquire() + halfw
                    for kc in range(KC):
                        last_mm = p.emit(
                            "tensor",
                            lambda t, bk=bk, kc=kc, oc=oc, tc2=tc2, wbuf=wbuf: t.matmul(
                                bk.pt[:, :], wbuf[:, kc, oc * 128:(oc + 1) * 128],
                                XH[:, kc, tc2 * 512:(tc2 + 1) * 512],
                                start=(kc == 0), stop=(kc == KC - 1)),
                            waits=waits if kc == 0 else (),
                            inc="sT" if kc == KC - 1 else None)
                    ev = p.emit("vector",
                                lambda v, bk=bk, rb=rb, tc2=tc2: v.tensor_copy(
                                    RAW[:, rb, 3 + tc2 * 512: 3 + (tc2 + 1) * 512],
                                    bk.pt[:, :]),
                                waits=[("sT", last_mm), ("sV", raw_ev[rb])], inc="sV")
                    bk.reader("sV", ev)
                c1 = p.emit("vector",
                            lambda v, rb=rb, xc=xc: v.tensor_copy(
                                RAW[:, rb, 0:3], TL[:, xc, :]),
                            waits=[("sV", ev)], inc="sV")
                for hh2 in range(2):
                    o0 = hh2 * 512
                    c1 = p.emit("vector",
                                lambda v, rb=rb, xc=xc, o0=o0: v.tensor_scalar(
                                    CV[:, :], RAW[:, rb, 3 + o0:515 + o0],
                                    io["convw"][:, xc, 3:4],
                                    io["convb"][:, xc:xc + 1], ALU.mult, ALU.add),
                                waits=[("sV", c1), ("sA", cs), io["cw_ready"]],
                                inc="sV")
                    for dd in range(3):
                        c1 = p.emit("vector",
                                    lambda v, rb=rb, xc=xc, dd=dd, o0=o0:
                                    v.scalar_tensor_tensor(
                                        CV[:, :], RAW[:, rb, dd + o0:512 + dd + o0],
                                        io["convw"][:, xc, dd:dd + 1], CV[:, :],
                                        ALU.mult, ALU.add),
                                    waits=[("sV", c1)], inc="sV")
                    cs = p.emit("scalar",
                                lambda s, xc=xc, o0=o0: s.activation(
                                    XBC[:, xc, o0:o0 + 512], CV[:, :], AF.Silu),
                                waits=[("sV", c1)], inc="sA")
                    c1 = p.emit("vector", None, waits=[("sA", cs)], inc=None) or c1
                c2 = p.emit("vector",
                            lambda v, rb=rb, xc=xc: v.tensor_copy(
                                TL[:, xc, :], RAW[:, rb, 1024:1027]),
                            waits=[("sV", c1)], inc="sV")
                raw_ev[rb] = c2
            wl.consumed(bi, "sT", last_mm)
        xbc_done = ("sA", cs)

        s2 = 0
        for tc2 in range(2):
            waits = [io["dtw_ready"], xready] + psRow.acquire() + halfw
            dt_mm = 0
            for kc in range(KC):
                dt_mm = p.emit("tensor",
                               lambda t, kc=kc, tc2=tc2: t.matmul(
                                   psRow.pt[:, :], io["SB_DTW"][:, kc, :],
                                   XH[:, kc, tc2 * 512:(tc2 + 1) * 512],
                                   start=(kc == 0), stop=(kc == KC - 1)),
                               waits=waits if kc == 0 else (),
                               inc="sT" if kc == KC - 1 else None)
            s1 = p.emit("scalar",
                        lambda s, tc2=tc2: s.activation(
                            SB_DT[:, tc2 * 512:(tc2 + 1) * 512], psRow.pt[:, :],
                            AF.Relu, bias=io["dtb"][:, :]),
                        waits=[("sT", dt_mm), aready] + halfw, inc="sA")
            s2 = p.emit("scalar",
                        lambda s, tc2=tc2: s.activation(
                            DT2[:, tc2 * 512:(tc2 + 1) * 512], psRow.pt[:, :],
                            AF.Abs, bias=io["dtb"][:, :]),
                        waits=[("sA", s1)], inc="sA")
            psRow.reader("sA", s2)
        s3 = p.emit("scalar", lambda s: s.activation(WR[:, :], DT2[:, :], AF.Exp,
                                                     scale=-1.0),
                    waits=[("sA", s2)] + halfw, inc="sA")
        s4 = p.emit("scalar", lambda s: s.activation(DT2[:, :], WR[:, :], AF.Ln,
                                                     bias=1.0),
                    waits=[("sA", s3)], inc="sA")
        s5 = p.emit("vector", lambda v: v.tensor_add(SB_DT[:, :], SB_DT[:, :],
                                                     DT2[:, :]),
                    waits=[("sA", s4)] + halfw, inc="sV")
        s6 = p.emit("vector", lambda v: v.tensor_scalar(DT2[:, :], SB_DT[:, :],
                                                        ACOL[:, :], None, ALU.mult),
                    waits=[("sV", s5), initdone], inc="sV")
        s7 = 0
        for z in range(4):
            s7 = p.emit("vector",
                        lambda v, z=z: v.tensor_tensor_scan(
                            DACS[:, z * 256:(z + 1) * 256], CON["ones32"][:, :],
                            DT2[:, z * 256:(z + 1) * 256], 0.0, ALU.mult, ALU.add),
                        waits=[("sV", s6)] + halfw, inc="sV")
        s8 = p.emit("scalar", lambda s: s.activation(EDACS[:, :], DACS[:, :], AF.Exp),
                    waits=[("sV", s7)] + halfw, inc="sA")
        wr_done = 0
        cd_done = 0
        for z in range(4):
            zz = H * 4 + z
            w1 = p.emit("vector",
                        lambda v, z=z: v.tensor_scalar(
                            DT2[:, z * 256:(z + 1) * 256],
                            DACS[:, z * 256:(z + 1) * 256],
                            DACS[:, z * 256 + 255:z * 256 + 256], None, ALU.subtract),
                        waits=[("sV", s7)], inc="sV")
            w2 = p.emit("scalar",
                        lambda s, z=z: s.activation(WR[:, z * 256:(z + 1) * 256],
                                                    DT2[:, z * 256:(z + 1) * 256],
                                                    AF.Exp, scale=-1.0),
                        waits=[("sV", w1)] + halfw, inc="sA")
            wr_done = p.emit("vector",
                             lambda v, z=z: v.tensor_mul(
                                 WR[:, z * 256:(z + 1) * 256],
                                 WR[:, z * 256:(z + 1) * 256],
                                 SB_DT[:, z * 256:(z + 1) * 256]),
                             waits=[("sA", w2)], inc="sV")
            cd_done = p.emit("scalar",
                             lambda s, z=z, zz=zz: s.activation(
                                 CDB[:, zz:zz + 1],
                                 DACS[:, z * 256 + 255:z * 256 + 256], AF.Exp),
                             waits=[("sV", s7), ("sA", w2)], inc="sA")

        NWCOL = io["nw"]
        IPWZ = io["ipw_z"]
        def ztask(j, sq_ready):
            bi, wbuf, wready = wl.load(IPWZ, j * 256)
            last_mm = 0
            gv = io.get("_gv", 0)
            for oc in range(2):
                zc = j * 2 + oc
                bk = psZ
                waits = [wready] + bk.acquire()
                for kc in range(KC):
                    last_mm = p.emit(
                        "tensor",
                        lambda t, bk=bk, kc=kc, oc=oc, wbuf=wbuf: t.matmul(
                            bk.pt[:, :], wbuf[:, kc, oc * 128:(oc + 1) * 128],
                            XH[:, kc, 0:512],
                            start=(kc == 0), stop=(kc == KC - 1)),
                        waits=waits if kc == 0 else (),
                        inc="sT" if kc == KC - 1 else None)
                zs = p.emit("scalar",
                            lambda s, bk=bk: s.activation(ZS[:, :], bk.pt[:, :],
                                                          AF.Silu),
                            waits=[("sT", last_mm), ("sV", gv)], inc="sA")
                bk.reader("sA", zs)
                gv = p.emit("vector",
                            lambda v, zc=zc: v.scalar_tensor_tensor(
                                SB_Y[:, zc, 0:512],
                                SB_Y[:, zc, 0:512],
                                NWCOL[:, zc:zc + 1], ZS[:, :],
                                ALU.mult, ALU.mult),
                            waits=[("sA", zs), ("sV", sq_ready), io["cw_ready"]],
                            inc="sV")
            wl.consumed(bi, "sT", last_mm)
            io["_gv"] = gv

        sq_mm = 0
        sqe_h01 = 0
        for z in range(4):
            zz = H * 4 + z
            k1 = p.emit("vector",
                        lambda v, z=z: v.tensor_scalar(
                            STK[0:32, :], DACS[:, z * 256:(z + 1) * 256], -1.0, None,
                            ALU.mult),
                        waits=[("sV", wr_done), ("sA", cd_done)], inc="sV")
            k2 = p.emit("vector",
                        lambda v, z=z: v.tensor_copy(
                            STK[32:64, :], SB_DT[:, z * 256:(z + 1) * 256]),
                        waits=[("sV", k1)], inc="sV")
            k3 = p.emit("vector",
                        lambda v, z=z: v.tensor_copy(
                            STK[64:96, :], WR[:, z * 256:(z + 1) * 256]),
                        waits=[("sV", k2)], inc="sV")
            tt_ev = 0
            for jt in range(2):
                w0 = [("sV", k3)] + bank[0].acquire()
                tmm = p.emit("tensor",
                             lambda t, jt=jt: t.transpose(
                                 bank[0].pt[0:128, 0:96],
                                 STK[:, jt * 128:(jt + 1) * 128],
                                 CON["identf"][0:96, 0:96]),
                             waits=w0, inc="sT")
                tt_ev = p.emit("vector",
                               lambda v, jt=jt: v.tensor_copy(
                                   TTf[:, jt, :], bank[0].pt[0:128, 0:96]),
                               waits=[("sT", tmm)], inc="sV")
                bank[0].reader("sV", tt_ev)
            last_ymm = 0
            last_smm = 0
            for g in range(4):
                bkc = bank[1]
                w0 = bkc.acquire() + [("sA", cd_done)]
                cmm = p.emit("tensor",
                             lambda t, g=g, zz=zz, bkc=bkc: t.matmul(
                                 bkc.pt[:, :],
                                 bass.AP(CDB, zz, [[8, 32], [0, 128]]),
                                 CON["e32b"][:, g * 512:(g + 1) * 512],
                                 start=True, stop=True),
                             waits=w0, inc="sT")
                cdt_ev = p.emit("vector",
                                lambda v, bkc=bkc: v.tensor_copy(
                                    CDBT[:, :, :].rearrange("p h q -> p (h q)"),
                                    bkc.pt[:, :]),
                                waits=[("sT", cmm), ("sV", io.get("_su_prev", 0))],
                                inc="sV")
                bkc.reader("sV", cdt_ev)
                cb_ev = 0
                for jt in range(2):
                    bk = bank[jt]
                    w0 = [xbc_done] + bk.acquire()
                    cmm = p.emit("tensor",
                                 lambda t, bk=bk, g=g, z=z, jt=jt: t.matmul(
                                     bk.pt[:, 0:256],
                                     XBC[:, 16 + g,
                                         z * 256 + jt * 128:z * 256 + (jt + 1) * 128],
                                     XBC[:, 20 + g, z * 256:(z + 1) * 256],
                                     start=True, stop=True),
                                 waits=w0, inc="sT")
                    cb_ev = p.emit("vector",
                                   lambda v, bk=bk, jt=jt: v.tensor_mul(
                                       CBT[:, jt, :], bk.pt[:, 0:256],
                                       CON["mask01"][:, jt, :]),
                                   waits=[("sT", cmm),
                                          ("sT", max(xds_prev.values() or [0]))],
                                   inc="sV")
                    bk.reader("sV", cb_ev)
                bt_ev = 0
                for jt in range(2):
                    w0 = psTB.acquire()
                    tmm = p.emit("tensor",
                                 lambda t, g=g, z=z, jt=jt: t.transpose(
                                     psTB.pt[:, :],
                                     XBC[:, 16 + g,
                                         z * 256 + jt * 128:z * 256 + (jt + 1) * 128],
                                     CON["ident"][:, :]),
                                 waits=w0 + [xbc_done], inc="sT")
                    bt_ev = p.emit("vector",
                                   lambda v, jt=jt: v.tensor_copy(
                                       BTT[:, jt, :], psTB.pt[:, :]),
                                   waits=[("sT", tmm)], inc="sV")
                    psTB.reader("sV", bt_ev)
                xt_ev = 0
                for q in range(4):
                    for jt in range(2):
                        w0 = psTB.acquire()
                        tmm = p.emit("tensor",
                                     lambda t, g=g, z=z, q=q, jt=jt: t.transpose(
                                         psTB.pt[:, :],
                                         XBC[:, 4 * g + q,
                                             z * 256 + jt * 128:z * 256 + (jt + 1) * 128],
                                         CON["ident"][:, :]),
                                     waits=w0 + [xbc_done], inc="sT")
                        xt_ev = p.emit("vector",
                                       lambda v, q=q, jt=jt: v.tensor_copy(
                                           XT[:, jt, q * 128:(q + 1) * 128],
                                           psTB.pt[:, :]),
                                       waits=[("sT", tmm)], inc="sV")
                        psTB.reader("sV", xt_ev)
                for rp in range(4):
                    prs = (2 * rp, 2 * rp + 1)
                    xd_ev = {}
                    for r in prs:
                        hl = g * 8 + r
                        pa = r % 2
                        ev = 0
                        for jt in range(2):
                            ev = p.emit("vector",
                                        lambda v, jt=jt, r=r, hl=hl, pa=pa: v.tensor_scalar(
                                            XDb[:, pa, jt, :],
                                            XT[:, jt, r * 64:(r + 1) * 64],
                                            TTf[:, jt, 32 + hl:32 + hl + 1], None,
                                            ALU.mult),
                                        waits=[("sV", xt_ev), ("sV", tt_ev),
                                               ("sT", xds_prev.get(pa, 0))], inc="sV")
                            ev = p.emit("vector",
                                        lambda v, jt=jt, r=r, hl=hl, pa=pa: v.tensor_scalar(
                                            XSb[:, pa, jt, :],
                                            XT[:, jt, r * 64:(r + 1) * 64],
                                            TTf[:, jt, 64 + hl:64 + hl + 1], None,
                                            ALU.mult),
                                        waits=[("sV", ev)], inc="sV")
                        xd_ev[r] = ev
                    d1 = {}
                    eb = {}
                    for r in prs:
                        hl = g * 8 + r
                        pa = r % 2
                        bkE = bank[pa]
                        w0 = bkE.acquire()
                        d1[r] = p.emit("tensor",
                                       lambda t, hl=hl, z=z, bkE=bkE: t.matmul(
                                           bkE.pt[:, 0:256],
                                           bass.AP(ID32F, hl, [[32, 32], [0, 128]]),
                                           DACS[:, z * 256:(z + 1) * 256],
                                           start=True, stop=True),
                                       waits=w0 + [("sV", s7)], inc="sT")
                        eb[r] = p.emit("tensor",
                                       lambda t, hl=hl, z=z, bkE=bkE: t.matmul(
                                           bkE.pt[:, 256:512],
                                           bass.AP(ID32B, hl, [[32, 32], [0, 128]]),
                                           EDACS[:, z * 256:(z + 1) * 256],
                                           start=True, stop=True),
                                       waits=[("sA", s8)], inc="sT")
                    mt_ev = {}
                    for jt in range(2):
                        e1 = {}
                        for r in prs:
                            hl = g * 8 + r
                            pa = r % 2
                            e1[r] = p.emit("vector",
                                           lambda v, jt=jt, hl=hl, pa=pa: v.tensor_scalar(
                                               EM[:, pa, :], bank[pa].pt[:, 0:256],
                                               TTf[:, jt, hl:hl + 1], 0.0,
                                               ALU.add, ALU.min),
                                           waits=[("sT", d1[r]), ("sV", tt_ev),
                                                  ("sA", em_guard.get(pa, 0))],
                                           inc="sV")
                        e2 = {}
                        for r in prs:
                            pa = r % 2
                            e2[r] = p.emit("scalar",
                                           lambda s, pa=pa: s.activation(
                                               EMB[:, pa, :], EM[:, pa, :], AF.Exp),
                                           waits=[("sV", e1[r]),
                                                  ("sV", emb_guard.get(pa, 0))],
                                           inc="sA")
                            em_guard[pa] = e2[r]
                        for r in prs:
                            pa = r % 2
                            mt_ev[r] = p.emit("vector",
                                              lambda v, jt=jt, pa=pa: v.tensor_mul(
                                                  MTb[:, pa, jt, :], EMB[:, pa, :],
                                                  CBT[:, jt, :]),
                                              waits=[("sA", e2[r]), ("sV", cb_ev)],
                                              inc="sV")
                            emb_guard[pa] = mt_ev[r]
                            bank[pa].reader("sV", e1[r])
                    ct = {}
                    for r in prs:
                        pa = r % 2
                        ebe = p.emit("vector",
                                     lambda v, pa=pa: v.tensor_copy(
                                         EBB[:, pa, :], bank[pa].pt[:, 256:512]),
                                     waits=[("sT", eb[r])], inc="sV")
                        bank[pa].reader("sV", ebe)
                        ct[r] = p.emit("vector",
                                       lambda v, g=g, z=z, pa=pa: v.tensor_mul(
                                           CTb[:, pa, :],
                                           XBC[:, 20 + g, z * 256:(z + 1) * 256],
                                           EBB[:, pa, :]),
                                       waits=[("sV", ebe)], inc="sV")
                    for r in prs:
                        hl = g * 8 + r
                        pa = r % 2
                        p2, base = r // 2, (r % 2) * 64
                        wY = psY.acquire() if (r == 0) else []
                        p.emit("tensor",
                               lambda t, p2=p2, base=base, pa=pa: t.matmul(
                                   psY.pt[base:base + 64, p2 * 256:(p2 + 1) * 256],
                                   XDb[:, pa, 0, :], MTb[:, pa, 0, :],
                                   start=True, stop=False),
                               waits=wY + [("sV", xd_ev[r]), ("sV", mt_ev[r])],
                               inc=None)
                        p.emit("tensor",
                               lambda t, p2=p2, base=base, pa=pa: t.matmul(
                                   psY.pt[base:base + 64, p2 * 256:(p2 + 1) * 256],
                                   XDb[:, pa, 1, :], MTb[:, pa, 1, :],
                                   start=False, stop=False),
                               inc=None)
                        last_ymm = p.emit("tensor",
                                          lambda t, p2=p2, base=base, hl=hl, pa=pa: t.matmul(
                                              psY.pt[base:base + 64,
                                                     p2 * 256:(p2 + 1) * 256],
                                              S_bf[:, hl, :], CTb[:, pa, :],
                                              start=False, stop=True),
                                          waits=[("sV", ct[r]), sbf_last], inc="sT")
                        wS = psS.acquire() if r == 0 else []
                        p.emit("tensor",
                               lambda t, r=r, pa=pa: t.matmul(
                                   psS.pt[:, r * 64:(r + 1) * 64],
                                   BTT[:, 0, :], XSb[:, pa, 0, :],
                                   start=True, stop=False),
                               waits=wS + [("sV", xd_ev[r]), ("sV", bt_ev)], inc=None)
                        last_smm = p.emit("tensor",
                                          lambda t, r=r, pa=pa: t.matmul(
                                              psS.pt[:, r * 64:(r + 1) * 64],
                                              BTT[:, 1, :], XSb[:, pa, 1, :],
                                              start=False, stop=True),
                                          inc="sT")
                        xds_prev[pa] = last_smm
                yv = 0
                dxe = 0
                for q in range(4):
                    dxe = p.emit("scalar",
                                 lambda s, g=g, q=q, z=z: s.activation(
                                     DXb[:, :],
                                     XBC[:, 4 * g + q, z * 256:(z + 1) * 256],
                                     AF.Copy,
                                     scale=io["dcol"][:, 4 * g + q:4 * g + q + 1]),
                                 waits=[xbc_done, ("sV", yv), io["cw_ready"]],
                                 inc="sA")
                    yv = p.emit("vector",
                                lambda v, g=g, q=q, z=z: v.tensor_add(
                                    SB_Y[:, 4 * g + q, z * 256:(z + 1) * 256],
                                    psY.pt[:, q * 256:(q + 1) * 256], DXb[:, :]),
                                waits=[("sT", last_ymm), ("sA", dxe)], inc="sV")
                psY.reader("sV", yv)
                su = p.emit("vector",
                            lambda v, g=g: v.tensor_mul(
                                S[:, g * 8:(g + 1) * 8, :].rearrange("p h q -> p (h q)"),
                                S[:, g * 8:(g + 1) * 8, :].rearrange("p h q -> p (h q)"),
                                CDBT[:, :, :].rearrange("p h q -> p (h q)")),
                            waits=[("sV", cdt_ev), ("sT", last_smm),
                                   ("sV", io.get("_su_prev", 0)), s_init], inc="sV")
                su = p.emit("vector",
                            lambda v, g=g: v.tensor_add(
                                S[:, g * 8:(g + 1) * 8, :].rearrange("p h q -> p (h q)"),
                                psS.pt[:, :],
                                S[:, g * 8:(g + 1) * 8, :].rearrange("p h q -> p (h q)")),
                            waits=[("sV", su)], inc="sV")
                psS.reader("sV", su)
                io["_su_prev"] = su
                if z >= 2:
                    ztask((z - 2) * 4 + g, sqe_h01)
            sbf = p.emit("vector",
                         lambda v: v.tensor_copy(
                             S_bf[:, :, :].rearrange("p h q -> p (h q)"),
                             S[:, :, :].rearrange("p h q -> p (h q)")),
                         waits=[("sV", io["_su_prev"]), ("sT", last_ymm)], inc="sV")
            sbf_last = ("sV", sbf)
            sq_mm = 0
            sq_mm_prev = {0: 0, 1: 0}
            wR = psRow.acquire()
            for fc in range(KC):
                qa = fc % 2
                sqd = p.emit("scalar",
                             lambda s, fc=fc, z=z, qa=qa: s.activation(
                                 SQt[:, qa, :], SB_Y[:, fc, z * 256:(z + 1) * 256],
                                 AF.Square),
                             waits=[("sV", yv), ("sT", sq_mm_prev[qa])], inc="sA")
                sq_mm = p.emit("tensor",
                               lambda t, fc=fc, qa=qa: t.matmul(
                                   psRow.pt[0:1, 0:256], CON["onescol"][:, :],
                                   SQt[:, qa, :],
                                   start=(fc == 0), stop=(fc == KC - 1),
                                   skip_group_check=True),
                               waits=[("sA", sqd)] + (wR if fc == 0 else []),
                               inc="sT")
                sq_mm_prev[qa] = sq_mm
            sqe = p.emit("vector",
                         lambda v, z=z: v.tensor_copy(
                             SSQ[:, z * 256:(z + 1) * 256], psRow.pt[0:1, 0:256]),
                         waits=[("sT", sq_mm)], inc="sV")
            psRow.reader("sV", sqe)
            if z == 1:
                sqe_h01 = sqe

        gv = io.get("_gv", 0)
        for j in range(8):
            bi, wbuf, wready = wl.load(io["ipw_z"], j * 256)
            last_mm = 0
            for oc in range(2):
                zc = j * 2 + oc
                for tc2 in (1,):
                    bk = bank[tc2]
                    waits = [wready, xready] + bk.acquire()
                    for kc in range(KC):
                        last_mm = p.emit(
                            "tensor",
                            lambda t, bk=bk, kc=kc, oc=oc, tc2=tc2, wbuf=wbuf: t.matmul(
                                bk.pt[:, :], wbuf[:, kc, oc * 128:(oc + 1) * 128],
                                XH[:, kc, tc2 * 512:(tc2 + 1) * 512],
                                start=(kc == 0), stop=(kc == KC - 1)),
                            waits=waits if kc == 0 else (),
                            inc="sT" if kc == KC - 1 else None)
                    zs = p.emit("scalar",
                                lambda s, bk=bk: s.activation(ZS[:, :], bk.pt[:, :],
                                                              AF.Silu),
                                waits=[("sT", last_mm), ("sV", gv)], inc="sA")
                    bk.reader("sA", zs)
                    gv = p.emit("vector",
                                lambda v, zc=zc, tc2=tc2: v.scalar_tensor_tensor(
                                    SB_Y[:, zc, tc2 * 512:(tc2 + 1) * 512],
                                    SB_Y[:, zc, tc2 * 512:(tc2 + 1) * 512],
                                    io["nw"][:, zc:zc + 1], ZS[:, :],
                                    ALU.mult, ALU.mult),
                                waits=[("sA", zs), ("sV", sqe), io["cw_ready"]],
                                inc="sV")
            wl.consumed(bi, "sT", last_mm)

        for j in range(8):
            bi, wbuf, wready = wl.load(io["opw"], j * 256)
            last_mm = 0
            for oc in range(2):
                for tc2 in range(2):
                    bk = bank[tc2]
                    waits = [wready, ("sV", gv)] + bk.acquire()
                    for kc in range(KC):
                        last_mm = p.emit(
                            "tensor",
                            lambda t, bk=bk, kc=kc, oc=oc, tc2=tc2, wbuf=wbuf: t.matmul(
                                bk.pt[:, :], wbuf[:, kc, oc * 128:(oc + 1) * 128],
                                SB_Y[:, kc, tc2 * 512:(tc2 + 1) * 512],
                                start=(kc == 0), stop=(kc == KC - 1)),
                            waits=waits if kc == 0 else (),
                            inc="sT" if kc == KC - 1 else None)
                    ope = p.emit("vector",
                                 lambda v, bk=bk: v.tensor_copy(OPT[:, :], bk.pt[:, :]),
                                 waits=[("sT", last_mm), ("dOU", op_dma)], inc="sV")
                    bk.reader("sV", ope)
                    row0 = j * 256 + oc * 128
                    tgt = contribA if row0 < 1024 else contribB
                    roff = row0 if row0 < 1024 else row0 - 1024
                    op_dma = p.emit("sync",
                                    lambda s, H=H, roff=roff, tc2=tc2, tgt=tgt:
                                    s.dma_start(
                                        tgt.ap()[H, roff:roff + 128,
                                                 tc2 * 512:(tc2 + 1) * 512],
                                        OPT[:, :]),
                                    waits=[("sV", ope)], inc="dOU", inc_n=16)
                    if H == 1 and row0 == 896 and tc2 == 1:
                        io["_rsA_ready"] = op_dma
            wl.consumed(bi, "sT", last_mm)
        op_dma = p.emit("sync",
                        lambda s, H=H: s.dma_start(contribB.ap()[H, 1024:1025, :],
                                                   SSQ[:, :]),
                        waits=[("sV", sqe)], inc="dOU", inc_n=16)

    ccA = p.emit("gpsimd",
                 lambda g: g.collective_compute(
                     "ReduceScatter", ALU.add, replica_groups=PAIRS,
                     ins=[contribA.ap().rearrange("a b n -> (a b) n").opt()],
                     outs=[rsoutA.ap().opt()]),
                 waits=[("dOU", io["_rsA_ready"])], inc="dCC", inc_n=1)
    cc = p.emit("gpsimd",
                lambda g: g.collective_compute(
                    "ReduceScatter", ALU.add, replica_groups=PAIRS,
                    ins=[contribB.ap().rearrange("a b n -> (a b) n").opt()],
                    outs=[rsoutB.ap().opt()]),
                waits=[("dOU", op_dma)], inc="dCC", inc_n=1)
    scope_xbc.close()
    O2T = sb("m_o2t", [128, 2, T], BF16)
    XRT = sb("m_xrt", [128, 2, T], BF16)
    RROW = sb("m_rrow", [1, T], F32)
    RROWB = sb("m_rrowb", [1, T], BF16)
    RB = sb("m_rb", [128, T], BF16)
    rb2 = p.emit("gpsimd",
                 lambda g: g.dma_start(RROW[:, :], rsoutB.ap()[1024:1025, :]),
                 waits=[("dCC", cc)], inc="dKV", inc_n=16)
    r1 = p.emit("scalar", lambda s: s.activation(RROW[:, :], RROW[:, :], AF.Ln,
                                                 bias=1e-5, scale=1.0 / 4096.0),
                waits=[("dKV", rb2)], inc="sA")
    r2 = p.emit("scalar", lambda s: s.activation(RROW[:, :], RROW[:, :], AF.Exp,
                                                 scale=-0.5),
                waits=[("sA", r1)], inc="sA")
    r3 = p.emit("vector", lambda v: v.tensor_copy(RROWB[:, :], RROW[:, :]),
                waits=[("sA", r2)], inc="sV")
    bcd = 0
    for tc2 in range(2):
        bk = bank[tc2]
        w0 = bk.acquire() + [("sV", r3)]
        mm = p.emit("tensor",
                    lambda t, bk=bk, tc2=tc2: t.matmul(
                        bk.pt[:, :], CON["onesrow"][:, :],
                        RROWB[:, tc2 * 512:(tc2 + 1) * 512], start=True, stop=True),
                    waits=w0, inc="sT")
        bcd = p.emit("vector",
                     lambda v, bk=bk, tc2=tc2: v.tensor_copy(
                         RB[:, tc2 * 512:(tc2 + 1) * 512], bk.pt[:, :]),
                     waits=[("sT", mm)], inc="sV")
        bk.reader("sV", bcd)
    SB_S = SB_Y
    sdone = 0
    tile_used = {0: 0, 1: 0}
    for fc in range(KC):
        pa = fc % 2
        rsrc = rsoutA if fc < 8 else rsoutB
        rsoff = fc * 128 if fc < 8 else (fc - 8) * 128
        ccw = ccA if fc < 8 else cc
        d1 = p.emit("sync",
                    lambda s, rsrc=rsrc, rsoff=rsoff, pa=pa: s.dma_start(
                        O2T[:, pa, :],
                        rsrc.ap()[rsoff:rsoff + 128, :]),
                    waits=[("dCC", ccw), ("sV", tile_used[pa])],
                    inc=f"dRB{pa}", inc_n=16)
        d2 = p.emit("sync",
                    lambda s, fc=fc, pa=pa: s.dma_start(
                        XRT[:, pa, :], io["x_res"][:, fc, :]),
                    inc=f"dRB{pa}", inc_n=16)
        sdone = p.emit("vector",
                       lambda v, fc=fc, pa=pa: v.tensor_mul(SB_S[:, fc, :],
                                                            O2T[:, pa, :], RB[:, :]),
                       waits=[("sV", bcd), (f"dRB{pa}", d2), ("sV", sdone)],
                       inc="sV")
        sdone = p.emit("vector",
                       lambda v, fc=fc, pa=pa: v.tensor_add(SB_S[:, fc, :],
                                                            SB_S[:, fc, :],
                                                            XRT[:, pa, :]),
                       waits=[("sV", sdone)], inc="sV")
        tile_used[pa] = sdone
    scope_ps.close()
    emit_ln(p, io, SB_S, SB_X, ("sV", sdone), scope, "ln2",
            [Bank(p, "ln2_bA", scope=scope), Bank(p, "ln2_bB", scope=scope)])
    scope.close()


def build_l1_attn():
    nc = bass.Bass("TRN2", target_bir_lowering=False)
    x_full = nc.dram_tensor("x_full", [KC, 128, 2 * T], BF16, kind="ExternalInput")
    x_res = nc.dram_tensor("x_res", [KC, 128, T], BF16, kind="ExternalInput")
    ipw_xbc = nc.dram_tensor("ipw_xbc", [D, 3072], F32, kind="ExternalInput")
    ipw_dt = nc.dram_tensor("ipw_dt", [D, 32], F32, kind="ExternalInput")
    ipw_z = nc.dram_tensor("ipw_z", [D, 2048], F32, kind="ExternalInput")
    opw = nc.dram_tensor("opw", [D, 2048], F32, kind="ExternalInput")
    convw = nc.dram_tensor("convw", [128, 24, 4], F32, kind="ExternalInput")
    convb = nc.dram_tensor("convb", [128, 24], F32, kind="ExternalInput")
    dtb = nc.dram_tensor("dtb", [32, 1], F32, kind="ExternalInput")
    alog = nc.dram_tensor("alog", [32, 1], F32, kind="ExternalInput")
    dcol = nc.dram_tensor("dcol", [128, 16], F32, kind="ExternalInput")
    nw = nc.dram_tensor("nw", [128, 16], F32, kind="ExternalInput")
    x_out = nc.dram_tensor("x_out", [KC, 128, T], BF16, kind="ExternalOutput")

    p = Prog(nc)
    for s in ("sT", "sV", "sA", "sP", "dIO", "dXH", "dKV", "dCC", "dOU", "dAX",
              "dDTW"):
        p.sem(s)
    CON = setup_consts(nc, p, names=("onescol", "onesrow", "fconsts"))
    SB_X = p.sb("sb_x", [128, KC, T], BF16)
    CWc = p.sb("sb_cw", [128, 24, 4], F32)
    CBc = p.sb("sb_cb", [128, 24], F32)
    DTBc = p.sb("sb_dtb", [32, 1], F32)
    ALOGc = p.sb("sb_alog", [32, 1], F32)
    DCOLc = p.sb("sb_dcol", [128, 16], F32)
    NWc = p.sb("sb_nw", [128, 16], F32)
    DTW = p.sb("sb_dtw", [128, KC, 32], BF16)
    al = p.emit("sync", lambda s: s.dma_start(
        CWc[:, :, :].rearrange("p a b -> p (a b)"),
        convw.ap().rearrange("p a b -> p (a b)")), inc="dAX", inc_n=16)
    al = p.emit("sync", lambda s: s.dma_start(CBc[:, :], convb.ap()),
                inc="dAX", inc_n=16)
    al = p.emit("sync", lambda s: s.dma_start(DCOLc[:, :], dcol.ap()),
                inc="dAX", inc_n=16)
    al = p.emit("sync", lambda s: s.dma_start(NWc[:, :], nw.ap()),
                inc="dAX", inc_n=16)
    dtwl = p.emit("gpsimd", lambda g: g.dma_start(
        DTW[:, :, :], ipw_dt.ap().rearrange("(kc kp) n -> kp kc n", kp=128)),
        inc="dDTW", inc_n=16)
    al = p.emit("sync", lambda s: s.dma_start(DTBc[:, :], dtb.ap()),
                inc="dAX", inc_n=16)
    al = p.emit("sync", lambda s: s.dma_start(ALOGc[:, :], alog.ap()),
                inc="dAX", inc_n=16)
    p.emit("tensor", None, waits=[CON["_ready"], ("dAX", al), ("dDTW", dtwl)])
    io = {"SB_X": SB_X, "CON": CON,
          "x_full": x_full.ap().rearrange("kc kp n -> kp kc n"),
          "x_res": x_res.ap().rearrange("kc kp n -> kp kc n"),
          "ipw_xbc": ipw_xbc.ap(), "ipw_z": ipw_z.ap(), "opw": opw.ap(),
          "convw": CWc, "convb": CBc, "dtb": DTBc, "alog": ALOGc,
          "dcol": DCOLc, "nw": NWc, "SB_DTW": DTW,
          "aux_ready": ("dAX", al), "cw_ready": ("dAX", al),
          "dtw_ready": ("dDTW", dtwl)}
    emit_l1_attn(p, io)
    xs, xv = io["x_ready"]
    out = p.emit("sync", lambda s: s.dma_start(
        x_out.ap().rearrange("kc kp n -> kp kc n"), SB_X[:, :, :]),
        waits=[(xs, xv)], inc="dIO", inc_n=16)
    p.emit("sync", None, waits=[("dIO", out)])
    p.finalize()
    return nc





L1FFN_IGS = ((0, 16), (16, 16), (32, 12))

def emit_l1_ffn(p, io):
    nc = p.nc
    SB_X, CON = io["SB_X"], io["CON"]
    scope = ExitStack()
    sb = lambda n, s, d: scope.enter_context(nc.sbuf_tensor(n, s, d))
    wl = WLoader(p, "wl1f", scope=scope)
    if io.get("wpre") is not None:
        wl.extra_wait = io["wpre"]
        wl.queue = "sync"
    GSC, USC, DSC = io["gs"], io["us"], io["ds"]
    bank = [Bank(p, f"ff1_ps{i}", scope=scope) for i in range(4)]
    xready = io["x_ready"]

    scope_acc = ExitStack()
    ACC = scope_acc.enter_context(nc.sbuf_tensor("ff1_acc", [128, KC, T], F32,
                                                 side="right"))
    SB_H = scope_acc.enter_context(nc.sbuf_tensor("ff1_h", [128, KC, T], BF16,
                                                  side="right"))
    SB_G = sb("ff1_g", [128, 512], BF16)

    acc_last = [[0, 0] for _ in range(KC)]
    for igi, (c0, nch) in enumerate(L1FFN_IGS):
        h_last = {}
        for isl in range(nch // 4):
            gb, gbuf, gready = wl.load(io["gwT"], (c0 + isl * 4) * 128)
            ub, ubuf, uready = wl.load(io["uwT"], (c0 + isl * 4) * 128)
            lg = lu = 0
            for tc in range(2):
                for oc in range(4):
                    ic = c0 + isl * 4 + oc
                    hc = isl * 4 + oc
                    bkg, bku = bank[0 + (oc % 2)], bank[2 + (oc % 2)]
                    wg = [gready, xready] + bkg.acquire()
                    for kc in range(KC):
                        lg = p.emit("tensor",
                                    lambda t, bkg=bkg, kc=kc, oc=oc, tc=tc,
                                           gbuf=gbuf: t.matmul(
                                        bkg.pt[:, :],
                                        gbuf[:, kc, oc * 128:(oc + 1) * 128],
                                        SB_X[:, kc, tc * 512:(tc + 1) * 512],
                                        start=(kc == 0), stop=(kc == KC - 1)),
                                    waits=wg if kc == 0 else (),
                                    inc="sT" if kc == KC - 1 else None)
                    gsil = p.emit("scalar",
                                  lambda s, bkg=bkg, ic=ic: s.activation(
                                      SB_G[:, :], bkg.pt[:, :], AF.Silu,
                                      scale=GSC[:, ic:ic + 1]),
                                  waits=[("sT", lg), io["sc_ready"]], inc="sA")
                    bkg.reader("sA", gsil)
                    wu = [uready] + bku.acquire()
                    for kc in range(KC):
                        lu = p.emit("tensor",
                                    lambda t, bku=bku, kc=kc, oc=oc, tc=tc,
                                           ubuf=ubuf: t.matmul(
                                        bku.pt[:, :],
                                        ubuf[:, kc, oc * 128:(oc + 1) * 128],
                                        SB_X[:, kc, tc * 512:(tc + 1) * 512],
                                        start=(kc == 0), stop=(kc == KC - 1)),
                                    waits=wu if kc == 0 else (),
                                    inc="sT" if kc == KC - 1 else None)
                    hv = p.emit("vector",
                                lambda v, bku=bku, hc=hc, tc=tc, ic=ic:
                                v.scalar_tensor_tensor(
                                    SB_H[:, hc, tc * 512:(tc + 1) * 512],
                                    bku.pt[:, :], USC[:, ic:ic + 1],
                                    SB_G[:, :], ALU.mult, ALU.mult),
                                waits=[("sT", lu), ("sA", gsil)], inc="sV")
                    bku.reader("sV", hv)
                    h_last[(hc, tc)] = hv
                    p.emit("scalar", None, waits=[("sV", hv)])
            wl.consumed(gb, "sT", lg)
            wl.consumed(ub, "sT", lu)
        for jd in range(4):
            db, dbuf, dready = wl.load(io["dwT"].rearr_rows(igi), jd * 512, nk=nch)
            dmm = 0
            for oc in range(4):
                for tc in range(2):
                    fc = jd * 4 + oc
                    bk = bank[tc]
                    wd = [dready] + bk.acquire() + [("sV", h_last[(kch, tc)])
                                                    for kch in range(nch)]
                    for kch in range(nch):
                        dmm = p.emit("tensor",
                                     lambda t, bk=bk, kch=kch, oc=oc, tc=tc,
                                            dbuf=dbuf, nch=nch: t.matmul(
                                         bk.pt[:, :],
                                         dbuf[:, kch, oc * 128:(oc + 1) * 128],
                                         SB_H[:, kch, tc * 512:(tc + 1) * 512],
                                         start=(kch == 0), stop=(kch == nch - 1)),
                                     waits=wd if kch == 0 else (),
                                     inc="sT" if kch == nch - 1 else None)
                    if igi == 0:
                        ae = p.emit("vector",
                                    lambda v, bk=bk, fc=fc, tc=tc: v.tensor_copy(
                                        ACC[:, fc, tc * 512:(tc + 1) * 512],
                                        bk.pt[:, :]),
                                    waits=[("sT", dmm), ("sV", acc_last[fc][tc])],
                                    inc="sV")
                    else:
                        ae = p.emit("vector",
                                    lambda v, bk=bk, fc=fc, tc=tc: v.tensor_add(
                                        ACC[:, fc, tc * 512:(tc + 1) * 512],
                                        bk.pt[:, :],
                                        ACC[:, fc, tc * 512:(tc + 1) * 512]),
                                    waits=[("sT", dmm), ("sV", acc_last[fc][tc])],
                                    inc="sV")
                    acc_last[fc][tc] = ae
                    bk.reader("sV", ae)
            wl.consumed(db, "sT", dmm)

    SB_S = sb("ff1_s", [128, KC, T], BF16)
    sdone = 0
    for fc in range(KC):
        sdone = p.emit("vector",
                       lambda v, fc=fc: v.scalar_tensor_tensor(
                           SB_S[:, fc, :], ACC[:, fc, :], DSC[:, fc:fc + 1],
                           SB_X[:, fc, :], ALU.mult, ALU.add),
                       waits=[("sV", max(acc_last[fc])), ("sV", sdone),
                              io["sc_ready"]], inc="sV")
    scope_acc.close()
    OUTF = scope.enter_context(nc.sbuf_tensor("ff1_of", [128, KC, T], F32,
                                              side="right"))
    emit_ln(p, io, SB_S, SB_X, ("sV", sdone), scope, "ln3", bank[:2], out_f32=OUTF)
    io["OUTF"] = OUTF
    scope.close()


def build_l1_ffn():
    nc = bass.Bass("TRN2", target_bir_lowering=False)
    x_in = nc.dram_tensor("x_in", [KC, 128, T], BF16, kind="ExternalInput")
    gwT = nc.dram_tensor("gwT", [D, 5632], F32, kind="ExternalInput")
    uwT = nc.dram_tensor("uwT", [D, 5632], F32, kind="ExternalInput")
    dwT = nc.dram_tensor("dwT", [5632, D], F32, kind="ExternalInput")
    gs = nc.dram_tensor("gs", [128, 44], F32, kind="ExternalInput")
    us = nc.dram_tensor("us", [128, 44], F32, kind="ExternalInput")
    ds = nc.dram_tensor("ds", [128, 16], F32, kind="ExternalInput")
    x_out = nc.dram_tensor("x_out", [KC, 128, T], F32, kind="ExternalOutput")

    p = Prog(nc)
    for s in ("sT", "sV", "sA", "dIO", "dX"):
        p.sem(s)
    CON = setup_consts(nc, p, names=("onescol", "onesrow", "fconsts"))
    SB_X = p.sb("sb_x", [128, KC, T], BF16)
    SBgs = p.sb("sb_gs", [128, 44], F32)
    SBus = p.sb("sb_us", [128, 44], F32)
    SBds = p.sb("sb_ds", [128, 16], F32)
    xl = p.emit("sync", lambda s: s.dma_start(
        SB_X[:, :, :], x_in.ap().rearrange("kc kp n -> kp kc n")), inc="dX", inc_n=16)
    for src_d, dst in ((gs, SBgs), (us, SBus), (ds, SBds)):
        xl = p.emit("sync", lambda s, src_d=src_d, dst=dst: s.dma_start(
            dst[...], src_d.ap()), inc="dX", inc_n=16)
    p.emit("tensor", None, waits=[CON["_ready"], ("dX", xl)])

    class DwT:
        def __init__(self, ap):
            self.ap_ = ap
        def rearr_rows(self, igi):
            r0 = L1FFN_IGS[igi][0] * 128
            rows = L1FFN_IGS[igi][1] * 128
            class V:
                def __init__(s2, ap, r0, rows):
                    s2.ap = ap
                    s2.r0 = r0
                    s2.rows = rows
                def __getitem__(s2, key):
                    _, colsl = key
                    return s2.ap[s2.r0:s2.r0 + s2.rows, colsl]
            return V(self.ap_, r0, rows)

    io = {"SB_X": SB_X, "CON": CON, "x_ready": ("dX", xl),
          "gwT": gwT.ap(), "uwT": uwT.ap(), "dwT": DwT(dwT.ap()),
          "gs": SBgs, "us": SBus, "ds": SBds, "sc_ready": ("dX", xl)}
    emit_l1_ffn(p, io)
    OUTF = io["OUTF"]
    xs, xv = io["x_ready"]
    out = p.emit("sync", lambda s: s.dma_start(
        x_out.ap().rearrange("kc kp n -> kp kc n"), OUTF[:, :, :]),
        waits=[(xs, xv)], inc="dIO", inc_n=16)
    p.emit("sync", None, waits=[("dIO", out)])
    p.finalize()
    return nc





def block_barrier(p):
    snap = [(s, p.ctr.get(s, 0)) for s in ("sT", "sV", "sA", "sP")]
    p.emit("gpsimd", None, waits=snap)
    p.emit("sync", None, waits=snap)
    return snap


def build_fused():
    nc = bass.Bass("TRN2", target_bir_lowering=False)
    x_in = nc.dram_tensor("x_in", [D, T], F32, kind="ExternalInput")
    wd = {}
    for w in ("Wq", "Wk", "Wv", "Wu", "Wo"):
        wd[w] = nc.dram_tensor(w, [D, D], F32, kind="ExternalInput")
    U8 = mybir.dt.uint8
    gwT0 = nc.dram_tensor("gwT0", [D, 8192], U8, kind="ExternalInput")
    uwT0 = nc.dram_tensor("uwT0", [D, 8192], U8, kind="ExternalInput")
    dwT0 = nc.dram_tensor("dwT0", [8192, D], U8, kind="ExternalInput")
    gs0 = nc.dram_tensor("gs0", [128, 64], F32, kind="ExternalInput")
    us0 = nc.dram_tensor("us0", [128, 64], F32, kind="ExternalInput")
    ds0 = nc.dram_tensor("ds0", [128, 16], F32, kind="ExternalInput")
    ipw_xbc = nc.dram_tensor("ipw_xbc", [D, 3072], F32, kind="ExternalInput")
    ipw_dt = nc.dram_tensor("ipw_dt", [D, 32], F32, kind="ExternalInput")
    ipw_z = nc.dram_tensor("ipw_z", [D, 2048], F32, kind="ExternalInput")
    opw = nc.dram_tensor("opw", [D, 2048], F32, kind="ExternalInput")
    convw = nc.dram_tensor("convw", [128, 24, 4], F32, kind="ExternalInput")
    convb = nc.dram_tensor("convb", [128, 24], F32, kind="ExternalInput")
    dtb = nc.dram_tensor("dtb", [32, 1], F32, kind="ExternalInput")
    alog = nc.dram_tensor("alog", [32, 1], F32, kind="ExternalInput")
    dcol = nc.dram_tensor("dcol", [128, 16], F32, kind="ExternalInput")
    nw = nc.dram_tensor("nw", [128, 16], F32, kind="ExternalInput")
    gwT1 = nc.dram_tensor("gwT1", [D, 5632], F32, kind="ExternalInput")
    uwT1 = nc.dram_tensor("uwT1", [D, 5632], F32, kind="ExternalInput")
    dwT1 = nc.dram_tensor("dwT1", [5632, D], F32, kind="ExternalInput")
    gs1 = nc.dram_tensor("gs1", [128, 44], F32, kind="ExternalInput")
    us1 = nc.dram_tensor("us1", [128, 44], F32, kind="ExternalInput")
    ds1 = nc.dram_tensor("ds1", [128, 16], F32, kind="ExternalInput")
    x_out = nc.dram_tensor("x_out", [KC, 128, T], F32, kind="ExternalOutput")

    p = Prog(nc)
    for s in ("sT", "sV", "sA", "sP", "dIO", "dX", "dXH", "dKV", "dCC", "dOU",
              "dAX", "dDTW"):
        p.sem(s)
    CON = setup_consts(nc, p, names=("onescol", "onesrow", "fconsts"))
    SB_X = p.sb("sb_x", [128, KC, T], BF16)
    SBgs0 = p.sb("sb_gs0", [128, 64], F32)
    SBus0 = p.sb("sb_us0", [128, 64], F32)
    SBds0 = p.sb("sb_ds0", [128, 16], F32)
    CWc = p.sb("sb_cw", [128, 24, 4], F32)
    CBc = p.sb("sb_cb", [128, 24], F32)
    DTBc = p.sb("sb_dtb", [32, 1], F32)
    ALOGc = p.sb("sb_alog", [32, 1], F32)
    DCOLc = p.sb("sb_dcol", [128, 16], F32)
    NWc = p.sb("sb_nw", [128, 16], F32)
    DTW = p.sb("sb_dtw", [128, KC, 32], BF16)
    SBgs1 = p.sb("sb_gs1", [128, 44], F32)
    SBus1 = p.sb("sb_us1", [128, 44], F32)
    SBds1 = p.sb("sb_ds1", [128, 16], F32)
    al = 0
    for src_d, dst in ((gs0, SBgs0), (us0, SBus0), (ds0, SBds0),
                       (convb, CBc), (dcol, DCOLc), (nw, NWc),
                       (dtb, DTBc), (alog, ALOGc), (gs1, SBgs1), (us1, SBus1),
                       (ds1, SBds1)):
        al = p.emit("sync", lambda s, src_d=src_d, dst=dst: s.dma_start(
            dst[...], src_d.ap()), inc="dAX", inc_n=16)
    al = p.emit("sync", lambda s: s.dma_start(
        CWc[:, :, :].rearrange("p a b -> p (a b)"),
        convw.ap().rearrange("p a b -> p (a b)")), inc="dAX", inc_n=16)
    dtwl = p.emit("gpsimd", lambda g: g.dma_start(
        DTW[:, :, :], ipw_dt.ap().rearrange("(kc kp) n -> kp kc n", kp=128)),
        inc="dDTW", inc_n=16)
    xl = p.emit("gpsimd", lambda g: g.dma_start(SB_X[:, :, :], frearr(x_in.ap())),
                inc="dX", inc_n=16)
    gwT1b = nc.dram_tensor("gwT1b", [D, 5632], BF16, kind="Internal")
    uwT1b = nc.dram_tensor("uwT1b", [D, 5632], BF16, kind="Internal")
    dwT1b = nc.dram_tensor("dwT1b", [5632, D], BF16, kind="Internal")
    pc = 0
    for srcw, dstw in ((gwT1, gwT1b), (uwT1, uwT1b), (dwT1, dwT1b)):
        pc = p.emit("gpsimd", lambda g, s=srcw, d=dstw: g.dma_start(d.ap(), s.ap()),
                    inc="dPC", inc_n=16)
    p.emit("tensor", None, waits=[CON["_ready"], ("dAX", al), ("dDTW", dtwl),
                                  ("dX", xl)])

    class DwT:
        def __init__(self, ap, igs):
            self.ap_ = ap
            self.igs = igs
        def rearr_rows(self, igi):
            r0 = self.igs[igi][0] * 128
            rows = self.igs[igi][1] * 128
            class V:
                def __init__(s2, ap, r0, rows):
                    s2.ap = ap
                    s2.r0 = r0
                    s2.rows = rows
                def __getitem__(s2, key):
                    _, colsl = key
                    return s2.ap[s2.r0:s2.r0 + s2.rows, colsl]
            return V(self.ap_, r0, rows)

    io = {"SB_X": SB_X, "CON": CON, "x_ready": ("dX", xl), "fused": True}
    for w in wd:
        io[w] = wd[w].ap()
    emit_l0_attn(p, io)
    block_barrier(p)
    io.update({"gwT": gwT0.ap(), "uwT": uwT0.ap(),
               "dwT": DwT(dwT0.ap(), ((0, 16), (16, 16), (32, 16), (48, 16))),
               "gs": SBgs0, "us": SBus0, "ds": SBds0})
    emit_l0_ffn(p, io)
    block_barrier(p)
    io.update({"ipw_xbc": ipw_xbc.ap(), "ipw_z": ipw_z.ap(), "opw": opw.ap(),
               "convw": CWc, "convb": CBc, "dtb": DTBc, "alog": ALOGc,
               "dcol": DCOLc, "nw": NWc, "SB_DTW": DTW,
               "aux_ready": ("dAX", al), "cw_ready": ("dAX", al),
               "dtw_ready": ("dDTW", dtwl)})
    emit_l1_attn(p, io)
    block_barrier(p)
    io.update({"gwT": gwT1b.ap(), "uwT": uwT1b.ap(),
               "dwT": DwT(dwT1b.ap(), L1FFN_IGS),
               "gs": SBgs1, "us": SBus1, "ds": SBds1,
               "sc_ready": ("dAX", al), "wpre": ("dPC", pc)})
    emit_l1_ffn(p, io)
    OUTF = io["OUTF"]
    xs, xv = io["x_ready"]
    out = p.emit("sync", lambda s: s.dma_start(
        x_out.ap().rearrange("kc kp n -> kp kc n"), OUTF[:, :, :]),
        waits=[(xs, xv)], inc="dIO", inc_n=16)
    p.emit("sync", None, waits=[("dIO", out)])
    p.finalize()
    return nc



_BUILD_CACHE = {}


def _get(name, fn):
    if name not in _BUILD_CACHE:
        _BUILD_CACHE[name] = fn()
    return _BUILD_CACHE[name]


def kernel(L0_attn_x, L0_attn_Wq, L0_attn_Wk, L0_attn_Wv, L0_attn_Wu, L0_attn_Wo,
           L0_ffn_gate_w, L0_ffn_gate_scale, L0_ffn_up_w, L0_ffn_up_scale,
           L0_ffn_down_w, L0_ffn_down_scale,
           L1_attn_in_proj_w, L1_attn_conv_w, L1_attn_conv_b, L1_attn_dt_bias,
           L1_attn_A_log, L1_attn_D, L1_attn_norm_w, L1_attn_out_proj_w,
           L1_ffn_gate_w, L1_ffn_up_w, L1_ffn_down_w, L1_ffn_gate_s, L1_ffn_up_s,
           L1_ffn_down_s):
    x = np.asarray(L0_attn_x)
    consts = make_consts()
    cores = list(range(8))
    nc = _get("fused", build_fused)

    Ws = {"Wq": np.asarray(L0_attn_Wq), "Wk": np.asarray(L0_attn_Wk),
          "Wv": np.asarray(L0_attn_Wv), "Wu": np.asarray(L0_attn_Wu),
          "Wo": np.asarray(L0_attn_Wo)}
    shared = {
        **Ws,
        "gwT0": np.ascontiguousarray(np.ascontiguousarray(
            np.asarray(L0_ffn_gate_w).T).view(np.uint8)[:, 0::4]),
        "uwT0": np.ascontiguousarray(np.ascontiguousarray(
            np.asarray(L0_ffn_up_w).T).view(np.uint8)[:, 0::4]),
        "dwT0": np.ascontiguousarray(np.ascontiguousarray(
            np.asarray(L0_ffn_down_w).T).view(np.uint8)[:, 0::4]),
        "gs0": np.ascontiguousarray(np.asarray(L0_ffn_gate_scale).reshape(64, 128).T),
        "us0": np.ascontiguousarray(np.asarray(L0_ffn_up_scale).reshape(64, 128).T),
        "ds0": np.ascontiguousarray(np.asarray(L0_ffn_down_scale).reshape(16, 128).T),
        "gwT1": np.ascontiguousarray(np.asarray(L1_ffn_gate_w).T),
        "uwT1": np.ascontiguousarray(np.asarray(L1_ffn_up_w).T),
        "dwT1": np.ascontiguousarray(np.asarray(L1_ffn_down_w).T),
        "gs1": np.ascontiguousarray(np.asarray(L1_ffn_gate_s).reshape(44, 128).T),
        "us1": np.ascontiguousarray(np.asarray(L1_ffn_up_s).reshape(44, 128).T),
        "ds1": np.ascontiguousarray(np.asarray(L1_ffn_down_s).reshape(16, 128).T),
        **consts,
    }
    ipT = np.ascontiguousarray(np.asarray(L1_attn_in_proj_w).T)
    opT = np.ascontiguousarray(np.asarray(L1_attn_out_proj_w).T)
    cw = np.asarray(L1_attn_conv_w)
    cb = np.asarray(L1_attn_conv_b)
    dtbv = np.asarray(L1_attn_dt_bias)
    alogv = np.asarray(L1_attn_A_log)
    Dv = np.asarray(L1_attn_D)
    nwv = np.asarray(L1_attn_norm_w)
    maps = []
    for c in cores:
        b, gh = c // 2, c % 2
        xcols = np.concatenate(
            [ipT[:, 4096 + gh * 2048:4096 + gh * 2048 + 2048],
             ipT[:, 8192 + gh * 512:8192 + gh * 512 + 512],
             ipT[:, 9216 + gh * 512:9216 + gh * 512 + 512]], axis=1)
        cwsl = np.concatenate([cw[gh * 2048:(gh + 1) * 2048],
                               cw[4096 + gh * 512:4096 + gh * 512 + 512],
                               cw[5120 + gh * 512:5120 + gh * 512 + 512]], axis=0)
        cbsl = np.concatenate([cb[gh * 2048:(gh + 1) * 2048],
                               cb[4096 + gh * 512:4096 + gh * 512 + 512],
                               cb[5120 + gh * 512:5120 + gh * 512 + 512]], axis=0)
        maps.append({
            "x_in": np.ascontiguousarray(x[b, gh * T:(gh + 1) * T, :].T),
            "ipw_xbc": np.ascontiguousarray(xcols),
            "ipw_dt": np.ascontiguousarray(
                ipT[:, 10240 + gh * 32:10240 + gh * 32 + 32]),
            "ipw_z": np.ascontiguousarray(ipT[:, gh * 2048:(gh + 1) * 2048]),
            "opw": np.ascontiguousarray(opT[gh * 2048:(gh + 1) * 2048, :]),
            "convw": np.ascontiguousarray(
                cwsl.reshape(24, 128, 4).transpose(1, 0, 2)),
            "convb": np.ascontiguousarray(cbsl.reshape(24, 128).T),
            "dtb": np.ascontiguousarray(dtbv[gh * 32:(gh + 1) * 32, None]),
            "alog": np.ascontiguousarray(alogv[gh * 32:(gh + 1) * 32, None]),
            "dcol": np.ascontiguousarray(
                np.repeat(Dv[gh * 32:(gh + 1) * 32], 64).reshape(16, 128).T),
            "nw": np.ascontiguousarray(
                nwv[gh * 2048:(gh + 1) * 2048].reshape(16, 128).T),
            **shared})
    r = run_bass_kernel_spmd(nc, maps, core_ids=cores)
    out = np.empty((4, 2048, 2048), np.float32)
    for c in cores:
        b, h = c // 2, c % 2
        out[b, h * T:(h + 1) * T, :] = np.asarray(
            r.results[c]["x_out"]).reshape(2048, T).T
    return out


# version 15 (final)
# speedup vs baseline: 1.1309x; 1.1309x over previous
import numpy as np
import ml_dtypes
from contextlib import ExitStack
import concourse.bass as bass
import concourse.mybir as mybir
from concourse.bass_utils import run_bass_kernel_spmd

F32 = mybir.dt.float32
BF16 = mybir.dt.bfloat16
I32 = mybir.dt.int32
AF = mybir.ActivationFunctionType
ALU = mybir.AluOpType
BF = ml_dtypes.bfloat16
PAIRS = [[0, 1], [2, 3], [4, 5], [6, 7]]
ENGINES = ("tensor", "vector", "scalar", "gpsimd", "sync")

D = 2048
KC = 16
T = 1024


class Prog:

    def __init__(self, nc):
        self.nc = nc
        self.es = ExitStack()
        self.cmds = {e: [] for e in ENGINES}
        self.sems = {}
        self.ctr = {}
        self._waited = {e: {} for e in ENGINES}
        self._selfinc = {e: {} for e in ENGINES}
        self._async = set()

    def sem(self, name):
        if name not in self.sems:
            self.sems[name] = self.es.enter_context(self.nc.semaphore(name))
            self.ctr[name] = 0
        return self.sems[name]

    def sb(self, name, shape, dtype):
        return self.es.enter_context(self.nc.sbuf_tensor(name, shape, dtype))

    def psum(self, name, shape, dtype=F32):
        return self.es.enter_context(self.nc.psum_tensor(name, shape, dtype))

    def emit(self, eng, fn, waits=(), inc=None, inc_n=1):
        w = []
        for s, v in waits:
            if v is None or v <= 0:
                continue
            self.sem(s)
            if self._waited[eng].get(s, 0) >= v:
                continue
            self._waited[eng][s] = v
            w.append((s, v))
        after = None
        if inc is not None:
            self.sem(inc)
            self.ctr[inc] += inc_n
            after = self.ctr[inc]
            if inc_n != 1 or inc.startswith("d"):
                self._async.add(inc)
        self.cmds[eng].append((fn, tuple(w), (inc, inc_n) if inc else None))
        return after

    def finalize(self):
        nc = self.nc

        def runner(eng_name):
            def run(engobj):
                for fn, waits, inc in self.cmds[eng_name]:
                    for s, v in waits:
                        engobj.wait_ge(self.sems[s], v)
                    if fn is None:
                        continue
                    inst = fn(engobj)
                    if inc is not None:
                        inst.then_inc(self.sems[inc[0]], inc[1])
            return run

        with nc.Block() as block:
            block.tensor(runner("tensor"))
            block.vector(runner("vector"))
            block.scalar(runner("scalar"))
            block.gpsimd(runner("gpsimd"))
            block.sync(runner("sync"))
        self.es.close()


class Bank:

    def __init__(self, p, name, shape=(128, 512), dtype=F32, scope=None):
        alloc = scope.enter_context if scope is not None else p.es.enter_context
        self.pt = alloc(p.nc.psum_tensor(name, list(shape), dtype))
        self.readers = []

    def acquire(self):
        w, self.readers = self.readers, []
        return w

    def reader(self, sem, val):
        self.readers.append((sem, val))



def frearr(ap):
    return ap.rearrange("(kc kp) n -> kp kc n", kp=128)


class WLoader:

    def __init__(self, p, name, nbuf=2, nk=KC, S=512, scope=None):
        self.p = p
        self.S = S
        self.nk = nk
        alloc = (scope.enter_context if scope is not None else p.es.enter_context)
        self.bufs = [alloc(p.nc.sbuf_tensor(f"{name}{i}", [128, nk, S], BF16))
                     for i in range(nbuf)]
        self.n = nbuf
        self.sem = [f"d_{name}{i}" for i in range(nbuf)]
        for s in self.sem:
            p.sem(s)
        self.last_reader = [None] * nbuf
        self.i = 0

    def load(self, dram_ap, col0, ncols=None, nk=None):
        ncols = ncols or self.S
        nk = nk or self.nk
        i = self.i
        self.i = (self.i + 1) % self.n
        buf = self.bufs[i]
        waits = [self.last_reader[i]] if self.last_reader[i] else []
        src = dram_ap[:, col0:col0 + ncols].rearrange("(kc kp) n -> kp kc n", kp=128)
        done = self.p.emit(
            "gpsimd",
            lambda g, buf=buf, src=src, nc_=ncols, nk=nk: g.dma_start(
                buf[:, 0:nk, 0:nc_], src),
            waits=waits, inc=self.sem[i], inc_n=16)
        return i, buf, (self.sem[i], done)

    def consumed(self, i, sem, val):
        self.last_reader[i] = (sem, val)



CONST_SPECS = (("onescol", [128, 1], BF16), ("onesrow", [1, 128], BF16),
               ("ident", [128, 128], BF16), ("onescol16", [128, 16, 16], BF16),
               ("rowsel16", [16, 16, 128], BF16), ("fconsts", [128, 8], F32),
               ("identf", [128, 128], F32), ("id32f", [32, 32], F32),
               ("id32b", [32, 32], BF16), ("ones32", [32, 256], F32),
               ("e32b", [32, 2048], BF16), ("mask01", [128, 2, 256], BF16))
FCONST_VALS = (1e-5, 1.0, 0.5, 2.0)


def make_consts():
    c = {}
    c["onescol"] = np.ones((128, 1), BF)
    c["onesrow"] = np.ones((1, 128), BF)
    c["ident"] = np.eye(128, dtype=np.float32).astype(BF)
    oc16 = np.zeros((128, 16, 16), np.float32)
    for h in range(16):
        oc16[:, h, h] = 1.0
    c["onescol16"] = oc16.astype(BF)
    rs16 = np.zeros((16, 16, 128), np.float32)
    for h in range(16):
        rs16[h, h, :] = 1.0
    c["rowsel16"] = rs16.astype(BF)
    fc = np.zeros((128, 8), np.float32)
    fc[:, :len(FCONST_VALS)] = np.asarray(FCONST_VALS)[None, :]
    c["fconsts"] = fc
    c["identf"] = np.eye(128, dtype=np.float32)
    c["id32f"] = np.eye(32, dtype=np.float32)
    c["id32b"] = np.eye(32, dtype=np.float32).astype(BF)
    c["ones32"] = np.ones((32, 256), np.float32)
    e32 = np.zeros((32, 2048), np.float32)
    for h in range(32):
        e32[h, h * 64:(h + 1) * 64] = 1.0
    c["e32b"] = e32.astype(BF)
    m01 = np.zeros((128, 2, 256), np.float32)
    for jt in range(2):
        for j in range(128):
            m01[j, jt, jt * 128 + j:] = 1.0
    c["mask01"] = m01.astype(BF)
    return c


_SCOPED_CONST_N = [0]


def scoped_consts(nc, p, CON, names, scope):
    _SCOPED_CONST_N[0] += 1
    sem = f"dC{_SCOPED_CONST_N[0]}"
    done = 0
    spec = {n: (s, d) for n, s, d in CONST_SPECS}
    for name in names:
        shape, dt = spec[name]
        dram = nc.dram_tensor(name, shape, dt, kind="ExternalInput")
        CON[name] = scope.enter_context(nc.sbuf_tensor("c_" + name, shape, dt))
        done = p.emit("sync",
                      lambda s, h=CON[name], d=dram: s.dma_start(h[...], d.ap()),
                      inc=sem, inc_n=16)
    CON["_ready"] = (sem, done)
    return CON


def setup_consts(nc, p, names=None):
    CON = {}
    done = 0
    for name, shape, dt in CONST_SPECS:
        if names is not None and name not in names:
            continue
        dram = nc.dram_tensor(name, shape, dt, kind="ExternalInput")
        CON[name] = p.sb("c_" + name, shape, dt)
        done = p.emit("sync",
                      lambda s, h=CON[name], d=dram: s.dma_start(h[...], d.ap()),
                      inc="dIO", inc_n=16)
    CON["_ready"] = ("dIO", done)
    if "fconsts" in CON:
        for i, val in enumerate(FCONST_VALS):
            nc.const_aps.register(F32, val, CON["fconsts"][:, i:i + 1])
    return CON



def emit_l0_attn(p, io, upto=99):
    nc = p.nc
    SB_X, CON = io["SB_X"], io["CON"]
    scope = ExitStack()
    sb = lambda n, s, d: scope.enter_context(nc.sbuf_tensor(n, s, d))
    wl = WLoader(p, "wl0a", scope=scope)
    scoped_consts(nc, p, CON, ("onescol16", "rowsel16"), scope)
    p.emit("tensor", None, waits=[CON["_ready"]])
    bank = [Bank(p, "l0_psA", scope=scope), Bank(p, "l0_psB", scope=scope)]
    xready = io["x_ready"]

    scope_kv = ExitStack()
    sbk = lambda n, s, d: scope_kv.enter_context(nc.sbuf_tensor(n, s, d))
    SB_KT = sbk("l0_kt", [128, 8, D], BF16)
    SB_VT = sbk("l0_vt", [128, 8, D], BF16)

    def tokmajor_proj(wdram, SB_OUT, relu_scale):
        for j in range(4):
            bi, wbuf, wready = wl.load(wdram, j * 512)
            last_mm = 0
            for tt in range(8):
                bk = bank[(j * 8 + tt) % 2]
                waits = [wready, xready] + bk.acquire()
                for kc in range(KC):
                    last_mm = p.emit(
                        "tensor",
                        lambda t, bk=bk, kc=kc, tt=tt, wbuf=wbuf: t.matmul(
                            bk.pt[:, :], SB_X[:, kc, tt * 128:(tt + 1) * 128],
                            wbuf[:, kc, :],
                            start=(kc == 0), stop=(kc == KC - 1)),
                        waits=waits if kc == 0 else (),
                        inc="sT" if kc == KC - 1 else None)
                ev = p.emit(
                    "scalar",
                    lambda s, bk=bk, tt=tt, j=j, SB_OUT=SB_OUT: s.activation(
                        SB_OUT[:, tt, j * 512:(j + 1) * 512], bk.pt[:, :],
                        AF.Relu, scale=relu_scale),
                    waits=[("sT", last_mm)], inc="sA")
                bk.reader("sA", ev)
            wl.consumed(bi, "sT", last_mm)
        return ev

    SCALE = 1.0 / np.sqrt(128.0)
    tokmajor_proj(io["Wk"], SB_KT, SCALE)
    if upto <= 0:
        scope_kv.close(); scope.close(); return

    def tokmajor_proj_plain(wdram, SB_OUT):
        for j in range(4):
            bi, wbuf, wready = wl.load(wdram, j * 512)
            last_mm = 0
            for tt in range(8):
                bk = bank[(j * 8 + tt) % 2]
                waits = [wready, xready] + bk.acquire()
                for kc in range(KC):
                    last_mm = p.emit(
                        "tensor",
                        lambda t, bk=bk, kc=kc, tt=tt, wbuf=wbuf: t.matmul(
                            bk.pt[:, :], SB_X[:, kc, tt * 128:(tt + 1) * 128],
                            wbuf[:, kc, :],
                            start=(kc == 0), stop=(kc == KC - 1)),
                        waits=waits if kc == 0 else (),
                        inc="sT" if kc == KC - 1 else None)
                ev = p.emit(
                    "vector",
                    lambda v, bk=bk, tt=tt, j=j, SB_OUT=SB_OUT: v.tensor_copy(
                        SB_OUT[:, tt, j * 512:(j + 1) * 512], bk.pt[:, :]),
                    waits=[("sT", last_mm)], inc="sV")
                bk.reader("sV", ev)
            wl.consumed(bi, "sT", last_mm)

    tokmajor_proj_plain(io["Wv"], SB_VT)
    if upto <= 1:
        scope_kv.close(); scope.close(); return

    SB_KV = sbk("l0_kv", [128, KC, 128], F32)
    kvdone = 0
    for hq in range(4):
        bk = bank[hq % 2]
        waits0 = bk.acquire()
        lastmm = 0
        for hh in range(4):
            h = hq * 4 + hh
            for tt in range(8):
                lastmm = p.emit(
                    "tensor",
                    lambda t, bk=bk, hh=hh, h=h, tt=tt: t.matmul(
                        bk.pt[:, hh * 128:(hh + 1) * 128],
                        SB_KT[:, tt, h * 128:(h + 1) * 128],
                        SB_VT[:, tt, h * 128:(h + 1) * 128],
                        start=(tt == 0), stop=(tt == 7)),
                    waits=waits0 if (hh == 0 and tt == 0) else (),
                    inc="sT" if (hh == 3 and tt == 7) else None)
        kvdone = p.emit(
            "vector",
            lambda v, bk=bk, hq=hq: v.tensor_copy(
                SB_KV[:, hq * 4:(hq + 1) * 4, :].rearrange("p h v -> p (h v)"),
                bk.pt[:, :]),
            waits=[("sT", lastmm)], inc="sV")
        bk.reader("sV", kvdone)

    cc_in = nc.dram_tensor("l0_kv_in", [128, D], F32, kind="Internal")
    cc_out = nc.dram_tensor("l0_kv_out", [128, D], F32, kind="Internal")
    st1 = p.emit("sync",
                 lambda s: s.dma_start(cc_in.ap(),
                                       SB_KV[:, :, :].rearrange("p h v -> p (h v)")),
                 waits=[("sV", kvdone)], inc="dCC", inc_n=16)
    ccd = p.emit("gpsimd",
                 lambda g: g.collective_compute(
                     "AllReduce", ALU.add, replica_groups=PAIRS,
                     ins=[cc_in.ap().opt()], outs=[cc_out.ap().opt()]),
                 waits=[("dCC", st1)], inc="dCC", inc_n=1)
    scope_kv.close()
    kv_staged = ("dCC", st1)
    if upto <= 2:
        scope.close(); return

    scope_qu = ExitStack()
    SB_Q = scope_qu.enter_context(nc.sbuf_tensor("l0_q", [128, KC, T], BF16, side="right"))
    SB_U = scope_qu.enter_context(nc.sbuf_tensor("l0_u", [128, KC, T], BF16, side="right"))

    def featmajor_proj(wdram, SB_OUT, act=None, act_scale=1.0):
        last = 0
        for j in range(4):
            bi, wbuf, wready = wl.load(wdram, j * 512)
            last_mm = 0
            for oc in range(4):
                for tc in range(2):
                    bk = bank[(oc * 2 + tc) % 2]
                    waits = [wready, xready, kv_staged] + bk.acquire()
                    for kc in range(KC):
                        last_mm = p.emit(
                            "tensor",
                            lambda t, bk=bk, kc=kc, oc=oc, tc=tc, wbuf=wbuf: t.matmul(
                                bk.pt[:, :], wbuf[:, kc, oc * 128:(oc + 1) * 128],
                                SB_X[:, kc, tc * 512:(tc + 1) * 512],
                                start=(kc == 0), stop=(kc == KC - 1)),
                            waits=waits if kc == 0 else (),
                            inc="sT" if kc == KC - 1 else None)
                    fc = j * 4 + oc
                    if act is None:
                        last = p.emit(
                            "vector",
                            lambda v, bk=bk, fc=fc, tc=tc, SB_OUT=SB_OUT: v.tensor_copy(
                                SB_OUT[:, fc, tc * 512:(tc + 1) * 512], bk.pt[:, :]),
                            waits=[("sT", last_mm)], inc="sV")
                        bk.reader("sV", last)
                    else:
                        last = p.emit(
                            "scalar",
                            lambda s, bk=bk, fc=fc, tc=tc, SB_OUT=SB_OUT, act=act,
                                   asc=act_scale: s.activation(
                                SB_OUT[:, fc, tc * 512:(tc + 1) * 512], bk.pt[:, :],
                                act, scale=asc),
                            waits=[("sT", last_mm)], inc="sA")
                        bk.reader("sA", last)
            wl.consumed(bi, "sT", last_mm)
        return last

    featmajor_proj(io["Wq"], SB_Q, act=AF.Relu, act_scale=SCALE)
    featmajor_proj(io["Wu"], SB_U)
    if upto <= 3:
        scope_qu.close(); scope.close(); return

    SB_KVb = sb("l0_kvb", [128, KC, 128], BF16)
    kvb_d = p.emit("gpsimd",
                   lambda g: g.dma_start(
                       SB_KVb[:, :, :].rearrange("p h v -> p (h v)"), cc_out.ap()),
                   waits=[("dCC", ccd)], inc="dKV", inc_n=16)

    SB_O = sb("l0_o", [128, KC, T], BF16)
    SB_SQ = sb("l0_sq", [128, T], BF16)
    psR = [Bank(p, "l0_psR0", (16, 512), scope=scope),
           Bank(p, "l0_psR1", (16, 512), scope=scope)]
    ssmm = [0, 0]
    for h in range(16):
        for tc in range(2):
            bk = bank[tc]
            waits = [("dKV", kvb_d)] + bk.acquire()
            mm = p.emit(
                "tensor",
                lambda t, bk=bk, h=h, tc=tc: t.matmul(
                    bk.pt[:, :], SB_KVb[:, h, :], SB_Q[:, h, tc * 512:(tc + 1) * 512],
                    start=True, stop=True),
                waits=waits, inc="sT")
            ev = p.emit(
                "vector",
                lambda v, bk=bk, h=h, tc=tc: v.tensor_copy(
                    SB_O[:, h, tc * 512:(tc + 1) * 512], bk.pt[:, :]),
                waits=[("sT", mm)], inc="sV")
            bk.reader("sV", ev)
            if upto <= 3.2:
                ssmm[tc] = 0
                continue
            sq = p.emit(
                "scalar",
                lambda s, h=h, tc=tc: s.activation(
                    SB_SQ[:, tc * 512:(tc + 1) * 512],
                    SB_O[:, h, tc * 512:(tc + 1) * 512], AF.Square),
                waits=[("sV", ev)], inc="sA")
            if upto <= 3.5:
                ssmm[tc] = 0
                continue
            ssmm[tc] = p.emit(
                "tensor",
                lambda t, h=h, tc=tc: t.matmul(
                    psR[tc].pt[:, :], CON["onescol16"][:, h, :],
                    SB_SQ[:, tc * 512:(tc + 1) * 512],
                    start=(h == 0), stop=(h == 15), skip_group_check=True),
                waits=[("sA", sq)], inc="sT")

    if upto <= 3.7:
        scope_qu.close(); scope.close(); return
    SB_SS = sb("l0_ss", [16, T], F32)
    s1 = p.emit("vector", lambda v: v.tensor_copy(SB_SS[:, 0:512], psR[0].pt[:, :]),
                waits=[("sT", ssmm[0])], inc="sV")
    s2 = p.emit("vector", lambda v: v.tensor_copy(SB_SS[:, 512:1024], psR[1].pt[:, :]),
                waits=[("sT", ssmm[1])], inc="sV")
    psR[0].reader("sV", s1)
    psR[1].reader("sV", s2)
    g1 = p.emit("vector", lambda v: v.tensor_scalar(
        SB_SS[:, :], SB_SS[:, :], 1.0 / 128.0, 1e-24, ALU.mult, ALU.max),
        waits=[("sV", s2)], inc="sV")
    g2 = p.emit("scalar", lambda s: s.activation(SB_SS[:, :], SB_SS[:, :], AF.Ln),
                waits=[("sV", g1)], inc="sA")
    g3 = p.emit("scalar", lambda s: s.activation(SB_SS[:, :], SB_SS[:, :], AF.Exp,
                                                 scale=-0.5),
                waits=[("sA", g2)], inc="sA")
    SB_SSb = sb("l0_ssb", [16, T], BF16)
    g4 = p.emit("vector", lambda v: v.tensor_copy(SB_SSb[:, :], SB_SS[:, :]),
                waits=[("sA", g3)], inc="sV")
    if upto <= 4:
        scope_qu.close(); scope.close(); return

    SB_Y = SB_O
    SB_T1 = sb("l0_t1", [128, T], BF16)
    for h in range(16):
        sil = p.emit(
            "scalar",
            lambda s, h=h: s.activation(SB_T1[:, :], SB_U[:, h, :], AF.Silu),
            inc="sA")
        t2 = p.emit(
            "vector",
            lambda v, h=h: v.tensor_mul(SB_T1[:, :], SB_O[:, h, :], SB_T1[:, :]),
            waits=[("sA", sil)], inc="sV")
        for tc in range(2):
            bk = bank[tc]
            waits = [("sV", g4)] + bk.acquire()
            bc = p.emit(
                "tensor",
                lambda t, bk=bk, h=h, tc=tc: t.matmul(
                    bk.pt[:, :], CON["rowsel16"][:, h, :],
                    SB_SSb[:, tc * 512:(tc + 1) * 512],
                    start=True, stop=True),
                waits=waits, inc="sT")
            y = p.emit(
                "vector",
                lambda v, bk=bk, h=h, tc=tc: v.tensor_mul(
                    SB_Y[:, h, tc * 512:(tc + 1) * 512],
                    SB_T1[:, tc * 512:(tc + 1) * 512], bk.pt[:, :]),
                waits=[("sT", bc), ("sV", t2)], inc="sV")
            bk.reader("sV", y)
        p.emit("scalar", None, waits=[("sV", y)])

    if upto <= 5:
        scope_qu.close(); scope.close(); return
    scope_qu.close()
    SB_S = scope.enter_context(nc.sbuf_tensor("l0_s", [128, KC, T], BF16, side="right"))
    sdone = 0
    for j in range(4):
        bi, wbuf, wready = wl.load(io["Wo"], j * 512)
        last_mm = 0
        for oc in range(4):
            for tc in range(2):
                bk = bank[(oc * 2 + tc) % 2]
                waits = [wready] + bk.acquire()
                for kc in range(KC):
                    last_mm = p.emit(
                        "tensor",
                        lambda t, bk=bk, kc=kc, oc=oc, tc=tc, wbuf=wbuf: t.matmul(
                            bk.pt[:, :], wbuf[:, kc, oc * 128:(oc + 1) * 128],
                            SB_Y[:, kc, tc * 512:(tc + 1) * 512],
                            start=(kc == 0), stop=(kc == KC - 1)),
                        waits=waits if kc == 0 else (),
                        inc="sT" if kc == KC - 1 else None)
                fc = j * 4 + oc
                sdone = p.emit(
                    "vector",
                    lambda v, bk=bk, fc=fc, tc=tc: v.tensor_add(
                        SB_S[:, fc, tc * 512:(tc + 1) * 512], bk.pt[:, :],
                        SB_X[:, fc, tc * 512:(tc + 1) * 512]),
                    waits=[("sT", last_mm)], inc="sV")
                bk.reader("sV", sdone)
        wl.consumed(bi, "sT", last_mm)

    if upto <= 6:
        io["x_ready"] = ("sV", sdone); scope.close(); return
    emit_ln(p, io, SB_S, SB_X, ("sV", sdone), scope, "ln0", bank)
    scope.close()


def emit_ln(p, io, SB_IN, SB_OUT, in_ready, scope, tag, bank, out_f32=None):
    nc = p.nc
    CON = io["CON"]
    sb = lambda n, s, d: scope.enter_context(nc.sbuf_tensor(n, s, d))
    rows = [Bank(p, f"{tag}_r{i}", (1, 512), scope=scope) for i in range(4)]
    SB_SQ = sb(f"{tag}_sq", [128, T], BF16)
    ROWS = sb(f"{tag}_rows", [1, 4 * T], F32)
    smm = [0, 0]
    qmm = [0, 0]
    sqd = 0
    prev_q = 0
    for fc in range(KC):
        sqd = p.emit("scalar",
                     lambda s, fc=fc: s.activation(SB_SQ[:, :], SB_IN[:, fc, :],
                                                   AF.Square),
                     waits=[in_ready, ("sT", prev_q)], inc="sA")
        for tc in range(2):
            smm[tc] = p.emit("tensor",
                             lambda t, fc=fc, tc=tc: t.matmul(
                                 rows[tc].pt[:, :], CON["onescol"][:, :],
                                 SB_IN[:, fc, tc * 512:(tc + 1) * 512],
                                 start=(fc == 0), stop=(fc == KC - 1),
                                 skip_group_check=True),
                             waits=[in_ready] if fc == 0 else (),
                             inc="sT")
            qmm[tc] = p.emit("tensor",
                             lambda t, fc=fc, tc=tc: t.matmul(
                                 rows[2 + tc].pt[:, :], CON["onescol"][:, :],
                                 SB_SQ[:, tc * 512:(tc + 1) * 512],
                                 start=(fc == 0), stop=(fc == KC - 1),
                                 skip_group_check=True),
                             waits=[("sA", sqd)],
                             inc="sT")
        prev_q = qmm[1]
    e1 = 0
    for i, off in ((0, 0), (1, 512), (2, 1024), (3, 1536)):
        e1 = p.emit("vector",
                    lambda v, i=i, off=off: v.tensor_copy(
                        ROWS[:, off:off + 512], rows[i].pt[:, :]),
                    waits=[("sT", max(smm + qmm))], inc="sV")
        rows[i].reader("sV", e1)
    e2 = p.emit("vector", lambda v: v.tensor_scalar(
        ROWS[:, 0:2048], ROWS[:, 0:2048], 1.0 / 2048.0, None, ALU.mult),
        waits=[("sV", e1)], inc="sV")
    e3 = p.emit("vector", lambda v: v.tensor_mul(
        ROWS[:, 2048:3072], ROWS[:, 0:1024], ROWS[:, 0:1024]),
        waits=[("sV", e2)], inc="sV")
    e4 = p.emit("vector", lambda v: v.tensor_sub(
        ROWS[:, 2048:3072], ROWS[:, 1024:2048], ROWS[:, 2048:3072]),
        waits=[("sV", e3)], inc="sV")
    e5 = p.emit("scalar", lambda s: s.activation(
        ROWS[:, 2048:3072], ROWS[:, 2048:3072], AF.Ln, bias=1e-5),
        waits=[("sV", e4)], inc="sA")
    e6 = p.emit("scalar", lambda s: s.activation(
        ROWS[:, 2048:3072], ROWS[:, 2048:3072], AF.Exp, scale=-0.5),
        waits=[("sA", e5)], inc="sA")
    e7 = p.emit("vector", lambda v: v.tensor_mul(
        ROWS[:, 3072:4096], ROWS[:, 0:1024], ROWS[:, 2048:3072]),
        waits=[("sA", e6)], inc="sV")
    e8 = p.emit("vector", lambda v: v.tensor_scalar(
        ROWS[:, 3072:4096], ROWS[:, 3072:4096], -1.0, None, ALU.mult),
        waits=[("sV", e7)], inc="sV")
    RB = sb(f"{tag}_rb", [1, 2 * T], BF16)
    e9 = p.emit("vector", lambda v: v.tensor_copy(RB[:, :], ROWS[:, 2048:4096]),
                waits=[("sV", e8)], inc="sV")
    BCR = sb(f"{tag}_bcr", [128, T], BF16)
    BCC = sb(f"{tag}_bcc", [128, T], BF16)
    bcdone = 0
    for ri, BC in ((0, BCR), (1, BCC)):
        for tc in range(2):
            bk = bank[tc]
            waits = [("sV", e9)] + bk.acquire()
            mm = p.emit("tensor",
                        lambda t, bk=bk, ri=ri, tc=tc: t.matmul(
                            bk.pt[:, :], CON["onesrow"][:, :],
                            RB[:, ri * T + tc * 512: ri * T + (tc + 1) * 512],
                            start=True, stop=True),
                        waits=waits, inc="sT")
            bcdone = p.emit("vector",
                            lambda v, bk=bk, BC=BC, tc=tc: v.tensor_copy(
                                BC[:, tc * 512:(tc + 1) * 512], bk.pt[:, :]),
                            waits=[("sT", mm)], inc="sV")
            bk.reader("sV", bcdone)
    last = 0
    for fc in range(KC):
        tgt = (SB_OUT if out_f32 is None else out_f32)
        last = p.emit("vector",
                      lambda v, fc=fc: v.tensor_mul(SB_SQ[:, :], SB_IN[:, fc, :],
                                                    BCR[:, :]),
                      waits=[("sV", bcdone), ("sV", last)], inc="sV")
        last = p.emit("vector",
                      lambda v, fc=fc, tgt=tgt: v.tensor_add(
                          tgt[:, fc, :], SB_SQ[:, :], BCC[:, :]),
                      waits=[("sV", last)], inc="sV")
    io["x_ready"] = ("sV", last)
    return last



def build_l0_attn(upto=99):
    nc = bass.Bass("TRN2", target_bir_lowering=False)
    x_in = nc.dram_tensor("x_in", [D, T], F32, kind="ExternalInput")
    wd = {}
    for w in ("Wq", "Wk", "Wv", "Wu", "Wo"):
        wd[w] = nc.dram_tensor(w, [D, D], F32, kind="ExternalInput")
    x_out = nc.dram_tensor("x_out", [KC, 128, T], BF16, kind="ExternalOutput")

    p = Prog(nc)
    for s in ("sT", "sV", "sA", "dIO", "dCC", "dX"):
        p.sem(s)
    CON = setup_consts(nc, p, names=("onescol", "onesrow", "fconsts"))
    SB_X = p.sb("sb_x", [128, KC, T], BF16)
    xl = p.emit("gpsimd", lambda g: g.dma_start(SB_X[:, :, :], frearr(x_in.ap())),
                inc="dX", inc_n=16)
    p.emit("tensor", None, waits=[CON["_ready"]])
    io = {"SB_X": SB_X, "CON": CON,
          "x_ready": ("dX", xl)}
    for w in wd:
        io[w] = wd[w].ap()
    emit_l0_attn(p, io, upto=upto)
    xs, xv = io["x_ready"]
    out = p.emit("sync", lambda s: s.dma_start(
        x_out.ap().rearrange("kc kp n -> kp kc n"), SB_X[:, :, :]),
        waits=[(xs, xv)], inc="dIO", inc_n=16)
    p.emit("sync", None, waits=[("dIO", out)])
    p.finalize()
    return nc





def emit_l0_ffn(p, io):
    nc = p.nc
    SB_X, CON = io["SB_X"], io["CON"]
    scope = ExitStack()
    sb = lambda n, s, d: scope.enter_context(nc.sbuf_tensor(n, s, d))
    wl = WLoader(p, "wl0f", scope=scope)
    GSC, USC, DSC = io["gs"], io["us"], io["ds"]
    bank = [Bank(p, f"ff0_ps{i}", scope=scope) for i in range(4)]
    scope_rows = ExitStack()
    rows = [Bank(p, f"ff0_r{i}", (1, 512), scope=scope_rows) for i in range(4)]
    xready = io["x_ready"]

    scope_acc = ExitStack()
    ACC = scope_acc.enter_context(nc.sbuf_tensor("ff0_acc", [128, KC, T], F32, side="right"))
    SB_H = sb("ff0_h", [128, KC, T], BF16)
    SB_G = sb("ff0_g", [128, 512], BF16)
    NR = sb("ff0_nr", [1, T], BF16)
    SHB = sb("ff0_shb", [128, T], BF16)

    smm = [0, 0]
    for fc in range(KC):
        for tc in range(2):
            smm[tc] = p.emit("tensor",
                             lambda t, fc=fc, tc=tc: t.matmul(
                                 rows[tc].pt[:, :], CON["onescol"][:, :],
                                 SB_X[:, fc, tc * 512:(tc + 1) * 512],
                                 start=(fc == 0), stop=(fc == KC - 1),
                                 skip_group_check=True),
                             waits=[xready] if fc == 0 else (),
                             inc="sT")
    nr = 0
    for tc in range(2):
        nr = p.emit("vector",
                    lambda v, tc=tc: v.tensor_scalar(
                        NR[:, tc * 512:(tc + 1) * 512], rows[tc].pt[:, :],
                        -128.0, None, ALU.mult),
                    waits=[("sT", smm[tc])], inc="sV")
        rows[tc].reader("sV", nr)

    IG = 4
    acc_last = [[0, 0] for _ in range(KC)]
    h_last = {}
    shmm = [0, 0]
    for ig in range(4):
        for isl in range(4):
            gb, gbuf, gready = wl.load(io["gwT"], ig * 2048 + isl * 512)
            ub, ubuf, uready = wl.load(io["uwT"], ig * 2048 + isl * 512)
            lg = lu = 0
            for tc in range(2):
                for oc in range(4):
                    ic = ig * 16 + isl * 4 + oc
                    hc = isl * 4 + oc
                    bkg, bku = bank[0 + (oc % 2)], bank[2 + (oc % 2)]
                    gmm = 0
                    wg = [gready, xready, ("sV", nr)] + bkg.acquire()
                    for kc in range(KC):
                        gmm = p.emit("tensor",
                                     lambda t, bkg=bkg, kc=kc, oc=oc, tc=tc, gbuf=gbuf: t.matmul(
                                         bkg.pt[:, :], gbuf[:, kc, oc * 128:(oc + 1) * 128],
                                         SB_X[:, kc, tc * 512:(tc + 1) * 512],
                                         start=(kc == 0), stop=False),
                                     waits=wg if kc == 0 else (), inc=None)
                    gmm = p.emit("tensor",
                                 lambda t, bkg=bkg, tc=tc: t.matmul(
                                     bkg.pt[:, :], CON["onesrow"][:, :],
                                     NR[:, tc * 512:(tc + 1) * 512],
                                     start=False, stop=True),
                                 inc="sT")
                    lg = gmm
                    gsil = p.emit("scalar",
                                  lambda s, bkg=bkg, ic=ic: s.activation(
                                      SB_G[:, :], bkg.pt[:, :], AF.Silu,
                                      scale=GSC[:, ic:ic + 1]),
                                  waits=[("sT", gmm)], inc="sA")
                    bkg.reader("sA", gsil)
                    umm = 0
                    wu = [uready, ("sV", nr)] + bku.acquire()
                    for kc in range(KC):
                        umm = p.emit("tensor",
                                     lambda t, bku=bku, kc=kc, oc=oc, tc=tc, ubuf=ubuf: t.matmul(
                                         bku.pt[:, :], ubuf[:, kc, oc * 128:(oc + 1) * 128],
                                         SB_X[:, kc, tc * 512:(tc + 1) * 512],
                                         start=(kc == 0), stop=False),
                                     waits=wu if kc == 0 else (), inc=None)
                    umm = p.emit("tensor",
                                 lambda t, bku=bku, tc=tc: t.matmul(
                                     bku.pt[:, :], CON["onesrow"][:, :],
                                     NR[:, tc * 512:(tc + 1) * 512],
                                     start=False, stop=True),
                                 inc="sT")
                    lu = umm
                    hv = p.emit("vector",
                                lambda v, bku=bku, hc=hc, tc=tc, ic=ic: v.scalar_tensor_tensor(
                                    SB_H[:, hc, tc * 512:(tc + 1) * 512],
                                    bku.pt[:, :], USC[:, ic:ic + 1], SB_G[:, :],
                                    ALU.mult, ALU.mult),
                                waits=[("sT", umm), ("sA", gsil)], inc="sV")
                    bku.reader("sV", hv)
                    h_last[(hc, tc)] = hv
                    p.emit("scalar", None, waits=[("sV", hv)])
                    shmm[tc] = p.emit("tensor",
                                      lambda t, hc=hc, tc=tc,
                                             st=(ig == 0 and isl == 0 and oc == 0),
                                             sp=(ig == 3 and isl == 3 and oc == 3): t.matmul(
                                          rows[2 + tc].pt[:, :], CON["onescol"][:, :],
                                          SB_H[:, hc, tc * 512:(tc + 1) * 512],
                                          start=st, stop=sp,
                                          skip_group_check=True),
                                      waits=[("sV", hv)], inc="sT")
            wl.consumed(gb, "sT", lg)
            wl.consumed(ub, "sT", lu)
        for jd in range(4):
            db, dbuf, dready = wl.load(io["dwT"].rearr_rows(ig), jd * 512)
            for oc in range(4):
                for tc in range(2):
                    fc = jd * 4 + oc
                    bk = bank[tc]
                    dmm = 0
                    wd = [dready] + bk.acquire() + [("sV", h_last[(kch, tc)]) for kch in range(KC)]
                    for kch in range(KC):
                        dmm = p.emit("tensor",
                                     lambda t, bk=bk, kch=kch, oc=oc, tc=tc, dbuf=dbuf: t.matmul(
                                         bk.pt[:, :], dbuf[:, kch, oc * 128:(oc + 1) * 128],
                                         SB_H[:, kch, tc * 512:(tc + 1) * 512],
                                         start=(kch == 0), stop=(kch == KC - 1)),
                                     waits=wd if kch == 0 else (), inc="sT" if kch == KC - 1 else None)
                    if ig == 0:
                        ae = p.emit("vector",
                                    lambda v, bk=bk, fc=fc, tc=tc: v.tensor_copy(
                                        ACC[:, fc, tc * 512:(tc + 1) * 512], bk.pt[:, :]),
                                    waits=[("sT", dmm), ("sV", acc_last[fc][tc])], inc="sV")
                    else:
                        ae = p.emit("vector",
                                    lambda v, bk=bk, fc=fc, tc=tc: v.tensor_add(
                                        ACC[:, fc, tc * 512:(tc + 1) * 512], bk.pt[:, :],
                                        ACC[:, fc, tc * 512:(tc + 1) * 512]),
                                    waits=[("sT", dmm), ("sV", acc_last[fc][tc])], inc="sV")
                    acc_last[fc][tc] = ae
                    bk.reader("sV", ae)
            wl.consumed(db, "sT", dmm)

    shb = 0
    for tc in range(2):
        shb = p.emit("vector",
                     lambda v, tc=tc: v.tensor_scalar(
                         NR[:, tc * 512:(tc + 1) * 512], rows[2 + tc].pt[:, :],
                         128.0, None, ALU.mult),
                     waits=[("sT", shmm[tc]), ("sT", p.ctr["sT"])], inc="sV")
        rows[2 + tc].reader("sV", shb)
    shbd = 0
    for tc in range(2):
        bk = bank[2 + tc]
        waits = [("sV", shb)] + bk.acquire()
        mm = p.emit("tensor",
                    lambda t, bk=bk, tc=tc: t.matmul(
                        bk.pt[:, :], CON["onesrow"][:, :],
                        NR[:, tc * 512:(tc + 1) * 512], start=True, stop=True),
                    waits=waits, inc="sT")
        shbd = p.emit("vector",
                      lambda v, bk=bk, tc=tc: v.tensor_copy(
                          SHB[:, tc * 512:(tc + 1) * 512], bk.pt[:, :]),
                      waits=[("sT", mm)], inc="sV")
        bk.reader("sV", shbd)

    SB_S = sb("ff0_s", [128, KC, T], BF16)
    sdone = 0
    for fc in range(KC):
        t1 = p.emit("vector",
                    lambda v, fc=fc: v.tensor_sub(ACC[:, fc, :], ACC[:, fc, :], SHB[:, :]),
                    waits=[("sV", max(shbd, acc_last[fc][0], acc_last[fc][1]))], inc="sV")
        sdone = p.emit("vector",
                       lambda v, fc=fc: v.scalar_tensor_tensor(
                           SB_S[:, fc, :], ACC[:, fc, :], DSC[:, fc:fc + 1],
                           SB_X[:, fc, :], ALU.mult, ALU.add),
                       waits=[("sV", t1)], inc="sV")
    scope_rows.close()
    scope_acc.close()
    emit_ln(p, io, SB_S, SB_X, ("sV", sdone), scope, "ln1", bank)
    scope.close()


class RowSlicedAP:
    def __init__(self, ap, r0, rows):
        self.ap = ap
        self.r0 = r0
        self.rows = rows

    def __getitem__(self, key):
        _, colsl = key
        return self.ap[self.r0:self.r0 + self.rows, colsl]


def build_l0_ffn():
    nc = bass.Bass("TRN2", target_bir_lowering=False)
    x_in = nc.dram_tensor("x_in", [KC, 128, T], BF16, kind="ExternalInput")
    gwT = nc.dram_tensor("gwT", [D, 8192], I32, kind="ExternalInput")
    uwT = nc.dram_tensor("uwT", [D, 8192], I32, kind="ExternalInput")
    dwT = nc.dram_tensor("dwT", [8192, D], I32, kind="ExternalInput")
    gs = nc.dram_tensor("gs", [128, 64], F32, kind="ExternalInput")
    us = nc.dram_tensor("us", [128, 64], F32, kind="ExternalInput")
    ds = nc.dram_tensor("ds", [128, 16], F32, kind="ExternalInput")
    x_out = nc.dram_tensor("x_out", [KC, 128, T], BF16, kind="ExternalOutput")

    p = Prog(nc)
    for s in ("sT", "sV", "sA", "dIO", "dX"):
        p.sem(s)
    CON = setup_consts(nc, p, names=("onescol", "onesrow", "fconsts"))
    SB_X = p.sb("sb_x", [128, KC, T], BF16)
    SBgs = p.sb("sb_gs", [128, 64], F32)
    SBus = p.sb("sb_us", [128, 64], F32)
    SBds = p.sb("sb_ds", [128, 16], F32)
    xl = p.emit("sync", lambda s: s.dma_start(
        SB_X[:, :, :], x_in.ap().rearrange("kc kp n -> kp kc n")), inc="dX", inc_n=16)
    for src_d, dst in ((gs, SBgs), (us, SBus), (ds, SBds)):
        xl = p.emit("sync", lambda s, src_d=src_d, dst=dst: s.dma_start(dst[...], src_d.ap()),
                    inc="dX", inc_n=16)
    p.emit("tensor", None, waits=[CON["_ready"], ("dX", xl)])

    class DwT:
        def __init__(self, ap):
            self.ap_ = ap
        def rearr_rows(self, ig):
            class V:
                def __init__(s2, ap, r0):
                    s2.ap = ap
                    s2.r0 = r0
                def __getitem__(s2, key):
                    _, colsl = key
                    return s2.ap[s2.r0:s2.r0 + 2048, colsl]
            return V(self.ap_, ig * 2048)

    io = {"SB_X": SB_X, "CON": CON, "x_ready": ("dX", xl),
          "gwT": gwT.ap(), "uwT": uwT.ap(), "dwT": DwT(dwT.ap()),
          "gs": SBgs, "us": SBus, "ds": SBds}
    emit_l0_ffn(p, io)
    xs, xv = io["x_ready"]
    out = p.emit("sync", lambda s: s.dma_start(
        x_out.ap().rearrange("kc kp n -> kp kc n"), SB_X[:, :, :]),
        waits=[(xs, xv)], inc="dIO", inc_n=16)
    p.emit("sync", None, waits=[("dIO", out)])
    p.finalize()
    return nc





def emit_l1_attn(p, io):
    nc = p.nc
    CON = io["CON"]
    SB_X = io["SB_X"]
    scope = ExitStack()
    sb = lambda n, s, d: scope.enter_context(nc.sbuf_tensor(n, s, d))
    wl = WLoader(p, "wl1a", S=256, scope=scope)
    scoped_consts(nc, p, CON, ("ident", "identf", "id32f", "id32b", "ones32",
                               "e32b", "mask01"), scope)
    scope_ps = ExitStack()

    bank = [Bank(p, "m_psA", scope=scope_ps), Bank(p, "m_psB", scope=scope_ps)]
    psY = Bank(p, "m_psY", (128, 1024), scope=scope_ps)
    psS = Bank(p, "m_psS", (128, 512), scope=scope_ps)
    psTB = Bank(p, "m_psTB", (128, 128), BF16, scope=scope_ps)
    psRow = Bank(p, "m_psRow", (32, 512), scope=scope_ps)
    psZ = Bank(p, "m_psZ", (128, 512), scope=scope_ps)

    S = sb("m_S", [128, 32, 64], F32)
    S_bf = sb("m_Sbf", [128, 32, 64], BF16)
    TL = sb("m_TL", [128, 24, 3], BF16)
    CDB = sb("m_cdbc", [32, 8], BF16)
    DT2 = sb("m_dt2", [32, T], F32)
    SB_DT = sb("m_dt", [32, T], F32)
    DACS = sb("m_dacs", [32, T], F32)
    WR = sb("m_wr", [32, T], F32)
    EDACS = sb("m_edacs", [32, T], BF16)
    STK = sb("m_stk", [96, 256], F32)
    TTf = sb("m_ttf", [128, 2, 96], F32)
    CBT = sb("m_cbt", [128, 2, 256], BF16)
    XT = sb("m_xt", [128, 2, 512], BF16)
    BTT = sb("m_btt", [128, 2, 128], BF16)
    XDb = sb("m_xd", [128, 2, 2, 64], BF16)
    XSb = sb("m_xs", [128, 2, 2, 64], BF16)
    EM = sb("m_em", [128, 2, 256], F32)
    EMB = sb("m_emb", [128, 2, 256], BF16)
    MTb = sb("m_mt", [128, 2, 2, 256], BF16)
    EBB = sb("m_ebb", [128, 2, 256], BF16)
    CTb = sb("m_ct", [128, 2, 256], BF16)
    DXb = sb("m_dxs", [128, 256], BF16)
    SQt = sb("m_sqt", [128, 2, 256], BF16)
    CDBT = sb("m_cdbt", [128, 8, 64], BF16)
    SSQ = sb("m_ssq", [1, T], BF16)
    CV = sb("m_cv", [128, 512], F32)
    RAW = sb("m_raw", [128, 2, 1027], BF16)
    ACOL = sb("m_acol", [32, 1], F32)
    XH = SB_X
    scope_xbc = ExitStack()
    XBC = scope_xbc.enter_context(nc.sbuf_tensor("m_xbc", [128, 24, T], BF16, side="right"))
    SB_Y = sb("m_yb", [128, KC, T], BF16)
    ZS = sb("m_zsil", [128, 512], BF16)
    OPT = sb("m_opt", [128, 512], BF16)

    ID32F, ID32B = CON["id32f"], CON["id32b"]
    contribA = nc.dram_tensor("m_contribA", [2, 1024, T], BF16, kind="Internal")
    contribB = nc.dram_tensor("m_contribB", [2, 1025, T], BF16, kind="Internal")
    rsoutA = nc.dram_tensor("m_rsoutA", [1024, T], BF16, kind="Internal")
    rsoutB = nc.dram_tensor("m_rsoutB", [1025, T], BF16, kind="Internal")

    if io.get("fused"):
        ag_in = nc.dram_tensor("m_ag_in", [D, T], BF16, kind="Internal")
        ag_out = nc.dram_tensor("m_ag_out", [2 * D, T], BF16, kind="Internal")
        xres_sc = nc.dram_tensor("m_xres_sc", [D, T], BF16, kind="Internal")
        xs, xv = io["x_ready"]
        sp1 = p.emit("sync", lambda s: s.dma_start(
            ag_in.ap().rearrange("(kc kp) n -> kp kc n", kp=128), SB_X[:, :, :]),
            waits=[(xs, xv)], inc="dOU", inc_n=16)
        sp2 = p.emit("sync", lambda s: s.dma_start(
            xres_sc.ap().rearrange("(kc kp) n -> kp kc n", kp=128), SB_X[:, :, :]),
            waits=[(xs, xv)], inc="dOU", inc_n=16)
        agd = p.emit("gpsimd", lambda g: g.collective_compute(
            "AllGather", ALU.bypass, replica_groups=PAIRS,
            ins=[ag_in.ap().opt()], outs=[ag_out.ap().opt()]),
            waits=[("dOU", sp2)], inc="dCC", inc_n=1)
        io["x_half_ap"] = lambda H: ag_out.ap()[H * D:(H + 1) * D, :].rearrange(
            "(kc kp) n -> kp kc n", kp=128)
        io["x_res"] = xres_sc.ap().rearrange("(kc kp) n -> kp kc n", kp=128)
        io["ag_ready"] = ("dCC", agd)
    else:
        io["x_half_ap"] = lambda H: io["x_full"][:, :, H * T:(H + 1) * T]
        io["ag_ready"] = ("sP", 0)
    z0 = p.emit("gpsimd", lambda g: g.memset(
        S[:, :, :].rearrange("p h q -> p (h q)"), 0.0), inc="sP")
    z0 = p.emit("gpsimd", lambda g: g.memset(
        S_bf[:, :, :].rearrange("p h q -> p (h q)"), 0.0), waits=[("sP", z0)], inc="sP")
    z0 = p.emit("gpsimd", lambda g: g.memset(
        TL[:, :, :].rearrange("p h q -> p (h q)"), 0.0), waits=[("sP", z0)], inc="sP")
    aready = io["aux_ready"]
    a1 = p.emit("scalar", lambda s: s.activation(ACOL[:, :], io["alog"][:, :], AF.Exp),
                waits=[aready, CON["_ready"]], inc="sA")
    a2 = p.emit("vector", lambda v: v.tensor_scalar(ACOL[:, :], ACOL[:, :], -1.0,
                                                    None, ALU.mult),
                waits=[("sA", a1)], inc="sV")
    initdone = ("sV", a2)
    sbf_last = ("sP", z0)
    io["_su_prev"] = 0
    s_init = ("sP", z0)
    op_dma = 0
    xds_prev = {}
    em_guard = {}
    emb_guard = {}

    for H in range(2):
        halfw = [("sT", p.ctr["sT"]), ("sV", p.ctr["sV"]), ("sA", p.ctr["sA"]),
                 ("dOU", p.ctr.get("dOU", 0))]
        xl = p.emit("sync", lambda s, H=H: s.dma_start(
            XH[:, :, :], io["x_half_ap"](H)),
            waits=halfw + [io["ag_ready"]], inc="dXH", inc_n=16)
        xready = ("dXH", xl)

        raw_ev = {0: 0, 1: 0}
        cs = 0
        for j in range(12):
            bi, wbuf, wready = wl.load(io["ipw_xbc"], j * 256)
            last_mm = 0
            for oc in range(2):
                xc = j * 2 + oc
                rb = xc % 2
                ev = 0
                for tc2 in range(2):
                    bk = bank[tc2]
                    waits = [wready, xready, ("sV", raw_ev[rb])] + bk.acquire() + halfw
                    for kc in range(KC):
                        last_mm = p.emit(
                            "tensor",
                            lambda t, bk=bk, kc=kc, oc=oc, tc2=tc2, wbuf=wbuf: t.matmul(
                                bk.pt[:, :], wbuf[:, kc, oc * 128:(oc + 1) * 128],
                                XH[:, kc, tc2 * 512:(tc2 + 1) * 512],
                                start=(kc == 0), stop=(kc == KC - 1)),
                            waits=waits if kc == 0 else (),
                            inc="sT" if kc == KC - 1 else None)
                    ev = p.emit("vector",
                                lambda v, bk=bk, rb=rb, tc2=tc2: v.tensor_copy(
                                    RAW[:, rb, 3 + tc2 * 512: 3 + (tc2 + 1) * 512],
                                    bk.pt[:, :]),
                                waits=[("sT", last_mm), ("sV", raw_ev[rb])], inc="sV")
                    bk.reader("sV", ev)
                c1 = p.emit("vector",
                            lambda v, rb=rb, xc=xc: v.tensor_copy(
                                RAW[:, rb, 0:3], TL[:, xc, :]),
                            waits=[("sV", ev)], inc="sV")
                for hh2 in range(2):
                    o0 = hh2 * 512
                    c1 = p.emit("vector",
                                lambda v, rb=rb, xc=xc, o0=o0: v.tensor_scalar(
                                    CV[:, :], RAW[:, rb, 3 + o0:515 + o0],
                                    io["convw"][:, xc, 3:4],
                                    io["convb"][:, xc:xc + 1], ALU.mult, ALU.add),
                                waits=[("sV", c1), ("sA", cs), io["cw_ready"]],
                                inc="sV")
                    for dd in range(3):
                        c1 = p.emit("vector",
                                    lambda v, rb=rb, xc=xc, dd=dd, o0=o0:
                                    v.scalar_tensor_tensor(
                                        CV[:, :], RAW[:, rb, dd + o0:512 + dd + o0],
                                        io["convw"][:, xc, dd:dd + 1], CV[:, :],
                                        ALU.mult, ALU.add),
                                    waits=[("sV", c1)], inc="sV")
                    cs = p.emit("scalar",
                                lambda s, xc=xc, o0=o0: s.activation(
                                    XBC[:, xc, o0:o0 + 512], CV[:, :], AF.Silu),
                                waits=[("sV", c1)], inc="sA")
                    c1 = p.emit("vector", None, waits=[("sA", cs)], inc=None) or c1
                c2 = p.emit("vector",
                            lambda v, rb=rb, xc=xc: v.tensor_copy(
                                TL[:, xc, :], RAW[:, rb, 1024:1027]),
                            waits=[("sV", c1)], inc="sV")
                raw_ev[rb] = c2
            wl.consumed(bi, "sT", last_mm)
        xbc_done = ("sA", cs)

        s2 = 0
        for tc2 in range(2):
            waits = [io["dtw_ready"], xready] + psRow.acquire() + halfw
            dt_mm = 0
            for kc in range(KC):
                dt_mm = p.emit("tensor",
                               lambda t, kc=kc, tc2=tc2: t.matmul(
                                   psRow.pt[:, :], io["SB_DTW"][:, kc, :],
                                   XH[:, kc, tc2 * 512:(tc2 + 1) * 512],
                                   start=(kc == 0), stop=(kc == KC - 1)),
                               waits=waits if kc == 0 else (),
                               inc="sT" if kc == KC - 1 else None)
            s1 = p.emit("scalar",
                        lambda s, tc2=tc2: s.activation(
                            SB_DT[:, tc2 * 512:(tc2 + 1) * 512], psRow.pt[:, :],
                            AF.Relu, bias=io["dtb"][:, :]),
                        waits=[("sT", dt_mm), aready] + halfw, inc="sA")
            s2 = p.emit("scalar",
                        lambda s, tc2=tc2: s.activation(
                            DT2[:, tc2 * 512:(tc2 + 1) * 512], psRow.pt[:, :],
                            AF.Abs, bias=io["dtb"][:, :]),
                        waits=[("sA", s1)], inc="sA")
            psRow.reader("sA", s2)
        s3 = p.emit("scalar", lambda s: s.activation(WR[:, :], DT2[:, :], AF.Exp,
                                                     scale=-1.0),
                    waits=[("sA", s2)] + halfw, inc="sA")
        s4 = p.emit("scalar", lambda s: s.activation(DT2[:, :], WR[:, :], AF.Ln,
                                                     bias=1.0),
                    waits=[("sA", s3)], inc="sA")
        s5 = p.emit("vector", lambda v: v.tensor_add(SB_DT[:, :], SB_DT[:, :],
                                                     DT2[:, :]),
                    waits=[("sA", s4)] + halfw, inc="sV")
        s6 = p.emit("vector", lambda v: v.tensor_scalar(DT2[:, :], SB_DT[:, :],
                                                        ACOL[:, :], None, ALU.mult),
                    waits=[("sV", s5), initdone], inc="sV")
        s7 = 0
        for z in range(4):
            s7 = p.emit("vector",
                        lambda v, z=z: v.tensor_tensor_scan(
                            DACS[:, z * 256:(z + 1) * 256], CON["ones32"][:, :],
                            DT2[:, z * 256:(z + 1) * 256], 0.0, ALU.mult, ALU.add),
                        waits=[("sV", s6)] + halfw, inc="sV")
        s8 = p.emit("scalar", lambda s: s.activation(EDACS[:, :], DACS[:, :], AF.Exp),
                    waits=[("sV", s7)] + halfw, inc="sA")
        wr_done = 0
        cd_done = 0
        for z in range(4):
            zz = H * 4 + z
            w1 = p.emit("vector",
                        lambda v, z=z: v.tensor_scalar(
                            DT2[:, z * 256:(z + 1) * 256],
                            DACS[:, z * 256:(z + 1) * 256],
                            DACS[:, z * 256 + 255:z * 256 + 256], None, ALU.subtract),
                        waits=[("sV", s7)], inc="sV")
            w2 = p.emit("scalar",
                        lambda s, z=z: s.activation(WR[:, z * 256:(z + 1) * 256],
                                                    DT2[:, z * 256:(z + 1) * 256],
                                                    AF.Exp, scale=-1.0),
                        waits=[("sV", w1)] + halfw, inc="sA")
            wr_done = p.emit("vector",
                             lambda v, z=z: v.tensor_mul(
                                 WR[:, z * 256:(z + 1) * 256],
                                 WR[:, z * 256:(z + 1) * 256],
                                 SB_DT[:, z * 256:(z + 1) * 256]),
                             waits=[("sA", w2)], inc="sV")
            cd_done = p.emit("scalar",
                             lambda s, z=z, zz=zz: s.activation(
                                 CDB[:, zz:zz + 1],
                                 DACS[:, z * 256 + 255:z * 256 + 256], AF.Exp),
                             waits=[("sV", s7), ("sA", w2)], inc="sA")

        NWCOL = io["nw"]
        IPWZ = io["ipw_z"]
        def ztask(j, sq_ready):
            bi, wbuf, wready = wl.load(IPWZ, j * 256)
            last_mm = 0
            gv = io.get("_gv", 0)
            for oc in range(2):
                zc = j * 2 + oc
                bk = psZ
                waits = [wready] + bk.acquire()
                for kc in range(KC):
                    last_mm = p.emit(
                        "tensor",
                        lambda t, bk=bk, kc=kc, oc=oc, wbuf=wbuf: t.matmul(
                            bk.pt[:, :], wbuf[:, kc, oc * 128:(oc + 1) * 128],
                            XH[:, kc, 0:512],
                            start=(kc == 0), stop=(kc == KC - 1)),
                        waits=waits if kc == 0 else (),
                        inc="sT" if kc == KC - 1 else None)
                zs = p.emit("scalar",
                            lambda s, bk=bk: s.activation(ZS[:, :], bk.pt[:, :],
                                                          AF.Silu),
                            waits=[("sT", last_mm), ("sV", gv)], inc="sA")
                bk.reader("sA", zs)
                gv = p.emit("vector",
                            lambda v, zc=zc: v.scalar_tensor_tensor(
                                SB_Y[:, zc, 0:512],
                                SB_Y[:, zc, 0:512],
                                NWCOL[:, zc:zc + 1], ZS[:, :],
                                ALU.mult, ALU.mult),
                            waits=[("sA", zs), ("sV", sq_ready), io["cw_ready"]],
                            inc="sV")
            wl.consumed(bi, "sT", last_mm)
            io["_gv"] = gv

        sq_mm = 0
        sqe_h01 = 0
        for z in range(4):
            zz = H * 4 + z
            k1 = p.emit("vector",
                        lambda v, z=z: v.tensor_scalar(
                            STK[0:32, :], DACS[:, z * 256:(z + 1) * 256], -1.0, None,
                            ALU.mult),
                        waits=[("sV", wr_done), ("sA", cd_done)], inc="sV")
            k2 = p.emit("vector",
                        lambda v, z=z: v.tensor_copy(
                            STK[32:64, :], SB_DT[:, z * 256:(z + 1) * 256]),
                        waits=[("sV", k1)], inc="sV")
            k3 = p.emit("vector",
                        lambda v, z=z: v.tensor_copy(
                            STK[64:96, :], WR[:, z * 256:(z + 1) * 256]),
                        waits=[("sV", k2)], inc="sV")
            tt_ev = 0
            for jt in range(2):
                w0 = [("sV", k3)] + bank[0].acquire()
                tmm = p.emit("tensor",
                             lambda t, jt=jt: t.transpose(
                                 bank[0].pt[0:128, 0:96],
                                 STK[:, jt * 128:(jt + 1) * 128],
                                 CON["identf"][0:96, 0:96]),
                             waits=w0, inc="sT")
                tt_ev = p.emit("vector",
                               lambda v, jt=jt: v.tensor_copy(
                                   TTf[:, jt, :], bank[0].pt[0:128, 0:96]),
                               waits=[("sT", tmm)], inc="sV")
                bank[0].reader("sV", tt_ev)
            last_ymm = 0
            last_smm = 0
            for g in range(4):
                bkc = bank[1]
                w0 = bkc.acquire() + [("sA", cd_done)]
                cmm = p.emit("tensor",
                             lambda t, g=g, zz=zz, bkc=bkc: t.matmul(
                                 bkc.pt[:, :],
                                 bass.AP(CDB, zz, [[8, 32], [0, 128]]),
                                 CON["e32b"][:, g * 512:(g + 1) * 512],
                                 start=True, stop=True),
                             waits=w0, inc="sT")
                cdt_ev = p.emit("vector",
                                lambda v, bkc=bkc: v.tensor_copy(
                                    CDBT[:, :, :].rearrange("p h q -> p (h q)"),
                                    bkc.pt[:, :]),
                                waits=[("sT", cmm), ("sV", io.get("_su_prev", 0))],
                                inc="sV")
                bkc.reader("sV", cdt_ev)
                cb_ev = 0
                for jt in range(2):
                    bk = bank[jt]
                    w0 = [xbc_done] + bk.acquire()
                    cmm = p.emit("tensor",
                                 lambda t, bk=bk, g=g, z=z, jt=jt: t.matmul(
                                     bk.pt[:, 0:256],
                                     XBC[:, 16 + g,
                                         z * 256 + jt * 128:z * 256 + (jt + 1) * 128],
                                     XBC[:, 20 + g, z * 256:(z + 1) * 256],
                                     start=True, stop=True),
                                 waits=w0, inc="sT")
                    cb_ev = p.emit("vector",
                                   lambda v, bk=bk, jt=jt: v.tensor_mul(
                                       CBT[:, jt, :], bk.pt[:, 0:256],
                                       CON["mask01"][:, jt, :]),
                                   waits=[("sT", cmm),
                                          ("sT", max(xds_prev.values() or [0]))],
                                   inc="sV")
                    bk.reader("sV", cb_ev)
                bt_ev = 0
                for jt in range(2):
                    w0 = psTB.acquire()
                    tmm = p.emit("tensor",
                                 lambda t, g=g, z=z, jt=jt: t.transpose(
                                     psTB.pt[:, :],
                                     XBC[:, 16 + g,
                                         z * 256 + jt * 128:z * 256 + (jt + 1) * 128],
                                     CON["ident"][:, :]),
                                 waits=w0 + [xbc_done], inc="sT")
                    bt_ev = p.emit("vector",
                                   lambda v, jt=jt: v.tensor_copy(
                                       BTT[:, jt, :], psTB.pt[:, :]),
                                   waits=[("sT", tmm)], inc="sV")
                    psTB.reader("sV", bt_ev)
                xt_ev = 0
                for q in range(4):
                    for jt in range(2):
                        w0 = psTB.acquire()
                        tmm = p.emit("tensor",
                                     lambda t, g=g, z=z, q=q, jt=jt: t.transpose(
                                         psTB.pt[:, :],
                                         XBC[:, 4 * g + q,
                                             z * 256 + jt * 128:z * 256 + (jt + 1) * 128],
                                         CON["ident"][:, :]),
                                     waits=w0 + [xbc_done], inc="sT")
                        xt_ev = p.emit("vector",
                                       lambda v, q=q, jt=jt: v.tensor_copy(
                                           XT[:, jt, q * 128:(q + 1) * 128],
                                           psTB.pt[:, :]),
                                       waits=[("sT", tmm)], inc="sV")
                        psTB.reader("sV", xt_ev)
                for rp in range(4):
                    prs = (2 * rp, 2 * rp + 1)
                    xd_ev = {}
                    for r in prs:
                        hl = g * 8 + r
                        pa = r % 2
                        ev = 0
                        for jt in range(2):
                            ev = p.emit("vector",
                                        lambda v, jt=jt, r=r, hl=hl, pa=pa: v.tensor_scalar(
                                            XDb[:, pa, jt, :],
                                            XT[:, jt, r * 64:(r + 1) * 64],
                                            TTf[:, jt, 32 + hl:32 + hl + 1], None,
                                            ALU.mult),
                                        waits=[("sV", xt_ev), ("sV", tt_ev),
                                               ("sT", xds_prev.get(pa, 0))], inc="sV")
                            ev = p.emit("vector",
                                        lambda v, jt=jt, r=r, hl=hl, pa=pa: v.tensor_scalar(
                                            XSb[:, pa, jt, :],
                                            XT[:, jt, r * 64:(r + 1) * 64],
                                            TTf[:, jt, 64 + hl:64 + hl + 1], None,
                                            ALU.mult),
                                        waits=[("sV", ev)], inc="sV")
                        xd_ev[r] = ev
                    d1 = {}
                    eb = {}
                    for r in prs:
                        hl = g * 8 + r
                        pa = r % 2
                        bkE = bank[pa]
                        w0 = bkE.acquire()
                        d1[r] = p.emit("tensor",
                                       lambda t, hl=hl, z=z, bkE=bkE: t.matmul(
                                           bkE.pt[:, 0:256],
                                           bass.AP(ID32F, hl, [[32, 32], [0, 128]]),
                                           DACS[:, z * 256:(z + 1) * 256],
                                           start=True, stop=True),
                                       waits=w0 + [("sV", s7)], inc="sT")
                        eb[r] = p.emit("tensor",
                                       lambda t, hl=hl, z=z, bkE=bkE: t.matmul(
                                           bkE.pt[:, 256:512],
                                           bass.AP(ID32B, hl, [[32, 32], [0, 128]]),
                                           EDACS[:, z * 256:(z + 1) * 256],
                                           start=True, stop=True),
                                       waits=[("sA", s8)], inc="sT")
                    mt_ev = {}
                    for jt in range(2):
                        e1 = {}
                        for r in prs:
                            hl = g * 8 + r
                            pa = r % 2
                            e1[r] = p.emit("vector",
                                           lambda v, jt=jt, hl=hl, pa=pa: v.tensor_scalar(
                                               EM[:, pa, :], bank[pa].pt[:, 0:256],
                                               TTf[:, jt, hl:hl + 1], 0.0,
                                               ALU.add, ALU.min),
                                           waits=[("sT", d1[r]), ("sV", tt_ev),
                                                  ("sA", em_guard.get(pa, 0))],
                                           inc="sV")
                        e2 = {}
                        for r in prs:
                            pa = r % 2
                            e2[r] = p.emit("scalar",
                                           lambda s, pa=pa: s.activation(
                                               EMB[:, pa, :], EM[:, pa, :], AF.Exp),
                                           waits=[("sV", e1[r]),
                                                  ("sV", emb_guard.get(pa, 0))],
                                           inc="sA")
                            em_guard[pa] = e2[r]
                        for r in prs:
                            pa = r % 2
                            mt_ev[r] = p.emit("vector",
                                              lambda v, jt=jt, pa=pa: v.tensor_mul(
                                                  MTb[:, pa, jt, :], EMB[:, pa, :],
                                                  CBT[:, jt, :]),
                                              waits=[("sA", e2[r]), ("sV", cb_ev)],
                                              inc="sV")
                            emb_guard[pa] = mt_ev[r]
                            bank[pa].reader("sV", e1[r])
                    ct = {}
                    for r in prs:
                        pa = r % 2
                        ebe = p.emit("vector",
                                     lambda v, pa=pa: v.tensor_copy(
                                         EBB[:, pa, :], bank[pa].pt[:, 256:512]),
                                     waits=[("sT", eb[r])], inc="sV")
                        bank[pa].reader("sV", ebe)
                        ct[r] = p.emit("vector",
                                       lambda v, g=g, z=z, pa=pa: v.tensor_mul(
                                           CTb[:, pa, :],
                                           XBC[:, 20 + g, z * 256:(z + 1) * 256],
                                           EBB[:, pa, :]),
                                       waits=[("sV", ebe)], inc="sV")
                    for r in prs:
                        hl = g * 8 + r
                        pa = r % 2
                        p2, base = r // 2, (r % 2) * 64
                        wY = psY.acquire() if (r == 0) else []
                        p.emit("tensor",
                               lambda t, p2=p2, base=base, pa=pa: t.matmul(
                                   psY.pt[base:base + 64, p2 * 256:(p2 + 1) * 256],
                                   XDb[:, pa, 0, :], MTb[:, pa, 0, :],
                                   start=True, stop=False),
                               waits=wY + [("sV", xd_ev[r]), ("sV", mt_ev[r])],
                               inc=None)
                        p.emit("tensor",
                               lambda t, p2=p2, base=base, pa=pa: t.matmul(
                                   psY.pt[base:base + 64, p2 * 256:(p2 + 1) * 256],
                                   XDb[:, pa, 1, :], MTb[:, pa, 1, :],
                                   start=False, stop=False),
                               inc=None)
                        last_ymm = p.emit("tensor",
                                          lambda t, p2=p2, base=base, hl=hl, pa=pa: t.matmul(
                                              psY.pt[base:base + 64,
                                                     p2 * 256:(p2 + 1) * 256],
                                              S_bf[:, hl, :], CTb[:, pa, :],
                                              start=False, stop=True),
                                          waits=[("sV", ct[r]), sbf_last], inc="sT")
                        wS = psS.acquire() if r == 0 else []
                        p.emit("tensor",
                               lambda t, r=r, pa=pa: t.matmul(
                                   psS.pt[:, r * 64:(r + 1) * 64],
                                   BTT[:, 0, :], XSb[:, pa, 0, :],
                                   start=True, stop=False),
                               waits=wS + [("sV", xd_ev[r]), ("sV", bt_ev)], inc=None)
                        last_smm = p.emit("tensor",
                                          lambda t, r=r, pa=pa: t.matmul(
                                              psS.pt[:, r * 64:(r + 1) * 64],
                                              BTT[:, 1, :], XSb[:, pa, 1, :],
                                              start=False, stop=True),
                                          inc="sT")
                        xds_prev[pa] = last_smm
                yv = 0
                dxe = 0
                for q in range(4):
                    dxe = p.emit("scalar",
                                 lambda s, g=g, q=q, z=z: s.activation(
                                     DXb[:, :],
                                     XBC[:, 4 * g + q, z * 256:(z + 1) * 256],
                                     AF.Copy,
                                     scale=io["dcol"][:, 4 * g + q:4 * g + q + 1]),
                                 waits=[xbc_done, ("sV", yv), io["cw_ready"]],
                                 inc="sA")
                    yv = p.emit("vector",
                                lambda v, g=g, q=q, z=z: v.tensor_add(
                                    SB_Y[:, 4 * g + q, z * 256:(z + 1) * 256],
                                    psY.pt[:, q * 256:(q + 1) * 256], DXb[:, :]),
                                waits=[("sT", last_ymm), ("sA", dxe)], inc="sV")
                psY.reader("sV", yv)
                su = p.emit("vector",
                            lambda v, g=g: v.tensor_mul(
                                S[:, g * 8:(g + 1) * 8, :].rearrange("p h q -> p (h q)"),
                                S[:, g * 8:(g + 1) * 8, :].rearrange("p h q -> p (h q)"),
                                CDBT[:, :, :].rearrange("p h q -> p (h q)")),
                            waits=[("sV", cdt_ev), ("sT", last_smm),
                                   ("sV", io.get("_su_prev", 0)), s_init], inc="sV")
                su = p.emit("vector",
                            lambda v, g=g: v.tensor_add(
                                S[:, g * 8:(g + 1) * 8, :].rearrange("p h q -> p (h q)"),
                                psS.pt[:, :],
                                S[:, g * 8:(g + 1) * 8, :].rearrange("p h q -> p (h q)")),
                            waits=[("sV", su)], inc="sV")
                psS.reader("sV", su)
                io["_su_prev"] = su
                if z >= 2:
                    ztask((z - 2) * 4 + g, sqe_h01)
            sbf = p.emit("vector",
                         lambda v: v.tensor_copy(
                             S_bf[:, :, :].rearrange("p h q -> p (h q)"),
                             S[:, :, :].rearrange("p h q -> p (h q)")),
                         waits=[("sV", io["_su_prev"]), ("sT", last_ymm)], inc="sV")
            sbf_last = ("sV", sbf)
            sq_mm = 0
            sq_mm_prev = {0: 0, 1: 0}
            wR = psRow.acquire()
            for fc in range(KC):
                qa = fc % 2
                sqd = p.emit("scalar",
                             lambda s, fc=fc, z=z, qa=qa: s.activation(
                                 SQt[:, qa, :], SB_Y[:, fc, z * 256:(z + 1) * 256],
                                 AF.Square),
                             waits=[("sV", yv), ("sT", sq_mm_prev[qa])], inc="sA")
                sq_mm = p.emit("tensor",
                               lambda t, fc=fc, qa=qa: t.matmul(
                                   psRow.pt[0:1, 0:256], CON["onescol"][:, :],
                                   SQt[:, qa, :],
                                   start=(fc == 0), stop=(fc == KC - 1),
                                   skip_group_check=True),
                               waits=[("sA", sqd)] + (wR if fc == 0 else []),
                               inc="sT")
                sq_mm_prev[qa] = sq_mm
            sqe = p.emit("vector",
                         lambda v, z=z: v.tensor_copy(
                             SSQ[:, z * 256:(z + 1) * 256], psRow.pt[0:1, 0:256]),
                         waits=[("sT", sq_mm)], inc="sV")
            psRow.reader("sV", sqe)
            if z == 1:
                sqe_h01 = sqe

        gv = io.get("_gv", 0)
        for j in range(8):
            bi, wbuf, wready = wl.load(io["ipw_z"], j * 256)
            last_mm = 0
            for oc in range(2):
                zc = j * 2 + oc
                for tc2 in (1,):
                    bk = bank[tc2]
                    waits = [wready, xready] + bk.acquire()
                    for kc in range(KC):
                        last_mm = p.emit(
                            "tensor",
                            lambda t, bk=bk, kc=kc, oc=oc, tc2=tc2, wbuf=wbuf: t.matmul(
                                bk.pt[:, :], wbuf[:, kc, oc * 128:(oc + 1) * 128],
                                XH[:, kc, tc2 * 512:(tc2 + 1) * 512],
                                start=(kc == 0), stop=(kc == KC - 1)),
                            waits=waits if kc == 0 else (),
                            inc="sT" if kc == KC - 1 else None)
                    zs = p.emit("scalar",
                                lambda s, bk=bk: s.activation(ZS[:, :], bk.pt[:, :],
                                                              AF.Silu),
                                waits=[("sT", last_mm), ("sV", gv)], inc="sA")
                    bk.reader("sA", zs)
                    gv = p.emit("vector",
                                lambda v, zc=zc, tc2=tc2: v.scalar_tensor_tensor(
                                    SB_Y[:, zc, tc2 * 512:(tc2 + 1) * 512],
                                    SB_Y[:, zc, tc2 * 512:(tc2 + 1) * 512],
                                    io["nw"][:, zc:zc + 1], ZS[:, :],
                                    ALU.mult, ALU.mult),
                                waits=[("sA", zs), ("sV", sqe), io["cw_ready"]],
                                inc="sV")
            wl.consumed(bi, "sT", last_mm)

        for j in range(8):
            bi, wbuf, wready = wl.load(io["opw"], j * 256)
            last_mm = 0
            for oc in range(2):
                for tc2 in range(2):
                    bk = bank[tc2]
                    waits = [wready, ("sV", gv)] + bk.acquire()
                    for kc in range(KC):
                        last_mm = p.emit(
                            "tensor",
                            lambda t, bk=bk, kc=kc, oc=oc, tc2=tc2, wbuf=wbuf: t.matmul(
                                bk.pt[:, :], wbuf[:, kc, oc * 128:(oc + 1) * 128],
                                SB_Y[:, kc, tc2 * 512:(tc2 + 1) * 512],
                                start=(kc == 0), stop=(kc == KC - 1)),
                            waits=waits if kc == 0 else (),
                            inc="sT" if kc == KC - 1 else None)
                    ope = p.emit("vector",
                                 lambda v, bk=bk: v.tensor_copy(OPT[:, :], bk.pt[:, :]),
                                 waits=[("sT", last_mm), ("dOU", op_dma)], inc="sV")
                    bk.reader("sV", ope)
                    row0 = j * 256 + oc * 128
                    tgt = contribA if row0 < 1024 else contribB
                    roff = row0 if row0 < 1024 else row0 - 1024
                    op_dma = p.emit("sync",
                                    lambda s, H=H, roff=roff, tc2=tc2, tgt=tgt:
                                    s.dma_start(
                                        tgt.ap()[H, roff:roff + 128,
                                                 tc2 * 512:(tc2 + 1) * 512],
                                        OPT[:, :]),
                                    waits=[("sV", ope)], inc="dOU", inc_n=16)
                    if H == 1 and row0 == 896 and tc2 == 1:
                        io["_rsA_ready"] = op_dma
            wl.consumed(bi, "sT", last_mm)
        op_dma = p.emit("sync",
                        lambda s, H=H: s.dma_start(contribB.ap()[H, 1024:1025, :],
                                                   SSQ[:, :]),
                        waits=[("sV", sqe)], inc="dOU", inc_n=16)

    ccA = p.emit("gpsimd",
                 lambda g: g.collective_compute(
                     "ReduceScatter", ALU.add, replica_groups=PAIRS,
                     ins=[contribA.ap().rearrange("a b n -> (a b) n").opt()],
                     outs=[rsoutA.ap().opt()]),
                 waits=[("dOU", io["_rsA_ready"])], inc="dCC", inc_n=1)
    cc = p.emit("gpsimd",
                lambda g: g.collective_compute(
                    "ReduceScatter", ALU.add, replica_groups=PAIRS,
                    ins=[contribB.ap().rearrange("a b n -> (a b) n").opt()],
                    outs=[rsoutB.ap().opt()]),
                waits=[("dOU", op_dma)], inc="dCC", inc_n=1)
    scope_xbc.close()
    O2T = sb("m_o2t", [128, 2, T], BF16)
    XRT = sb("m_xrt", [128, 2, T], BF16)
    RROW = sb("m_rrow", [1, T], F32)
    RROWB = sb("m_rrowb", [1, T], BF16)
    RB = sb("m_rb", [128, T], BF16)
    rb2 = p.emit("gpsimd",
                 lambda g: g.dma_start(RROW[:, :], rsoutB.ap()[1024:1025, :]),
                 waits=[("dCC", cc)], inc="dKV", inc_n=16)
    r1 = p.emit("scalar", lambda s: s.activation(RROW[:, :], RROW[:, :], AF.Ln,
                                                 bias=1e-5, scale=1.0 / 4096.0),
                waits=[("dKV", rb2)], inc="sA")
    r2 = p.emit("scalar", lambda s: s.activation(RROW[:, :], RROW[:, :], AF.Exp,
                                                 scale=-0.5),
                waits=[("sA", r1)], inc="sA")
    r3 = p.emit("vector", lambda v: v.tensor_copy(RROWB[:, :], RROW[:, :]),
                waits=[("sA", r2)], inc="sV")
    bcd = 0
    for tc2 in range(2):
        bk = bank[tc2]
        w0 = bk.acquire() + [("sV", r3)]
        mm = p.emit("tensor",
                    lambda t, bk=bk, tc2=tc2: t.matmul(
                        bk.pt[:, :], CON["onesrow"][:, :],
                        RROWB[:, tc2 * 512:(tc2 + 1) * 512], start=True, stop=True),
                    waits=w0, inc="sT")
        bcd = p.emit("vector",
                     lambda v, bk=bk, tc2=tc2: v.tensor_copy(
                         RB[:, tc2 * 512:(tc2 + 1) * 512], bk.pt[:, :]),
                     waits=[("sT", mm)], inc="sV")
        bk.reader("sV", bcd)
    SB_S = SB_Y
    sdone = 0
    tile_used = {0: 0, 1: 0}
    for fc in range(KC):
        pa = fc % 2
        rsrc = rsoutA if fc < 8 else rsoutB
        rsoff = fc * 128 if fc < 8 else (fc - 8) * 128
        ccw = ccA if fc < 8 else cc
        d1 = p.emit("sync",
                    lambda s, rsrc=rsrc, rsoff=rsoff, pa=pa: s.dma_start(
                        O2T[:, pa, :],
                        rsrc.ap()[rsoff:rsoff + 128, :]),
                    waits=[("dCC", ccw), ("sV", tile_used[pa])],
                    inc=f"dRB{pa}", inc_n=16)
        d2 = p.emit("sync",
                    lambda s, fc=fc, pa=pa: s.dma_start(
                        XRT[:, pa, :], io["x_res"][:, fc, :]),
                    inc=f"dRB{pa}", inc_n=16)
        sdone = p.emit("vector",
                       lambda v, fc=fc, pa=pa: v.tensor_mul(SB_S[:, fc, :],
                                                            O2T[:, pa, :], RB[:, :]),
                       waits=[("sV", bcd), (f"dRB{pa}", d2), ("sV", sdone)],
                       inc="sV")
        sdone = p.emit("vector",
                       lambda v, fc=fc, pa=pa: v.tensor_add(SB_S[:, fc, :],
                                                            SB_S[:, fc, :],
                                                            XRT[:, pa, :]),
                       waits=[("sV", sdone)], inc="sV")
        tile_used[pa] = sdone
    scope_ps.close()
    emit_ln(p, io, SB_S, SB_X, ("sV", sdone), scope, "ln2",
            [Bank(p, "ln2_bA", scope=scope), Bank(p, "ln2_bB", scope=scope)])
    scope.close()


def build_l1_attn():
    nc = bass.Bass("TRN2", target_bir_lowering=False)
    x_full = nc.dram_tensor("x_full", [KC, 128, 2 * T], BF16, kind="ExternalInput")
    x_res = nc.dram_tensor("x_res", [KC, 128, T], BF16, kind="ExternalInput")
    ipw_xbc = nc.dram_tensor("ipw_xbc", [D, 3072], F32, kind="ExternalInput")
    ipw_dt = nc.dram_tensor("ipw_dt", [D, 32], F32, kind="ExternalInput")
    ipw_z = nc.dram_tensor("ipw_z", [D, 2048], F32, kind="ExternalInput")
    opw = nc.dram_tensor("opw", [D, 2048], F32, kind="ExternalInput")
    convw = nc.dram_tensor("convw", [128, 24, 4], F32, kind="ExternalInput")
    convb = nc.dram_tensor("convb", [128, 24], F32, kind="ExternalInput")
    dtb = nc.dram_tensor("dtb", [32, 1], F32, kind="ExternalInput")
    alog = nc.dram_tensor("alog", [32, 1], F32, kind="ExternalInput")
    dcol = nc.dram_tensor("dcol", [128, 16], F32, kind="ExternalInput")
    nw = nc.dram_tensor("nw", [128, 16], F32, kind="ExternalInput")
    x_out = nc.dram_tensor("x_out", [KC, 128, T], BF16, kind="ExternalOutput")

    p = Prog(nc)
    for s in ("sT", "sV", "sA", "sP", "dIO", "dXH", "dKV", "dCC", "dOU", "dAX",
              "dDTW"):
        p.sem(s)
    CON = setup_consts(nc, p, names=("onescol", "onesrow", "fconsts"))
    SB_X = p.sb("sb_x", [128, KC, T], BF16)
    CWc = p.sb("sb_cw", [128, 24, 4], F32)
    CBc = p.sb("sb_cb", [128, 24], F32)
    DTBc = p.sb("sb_dtb", [32, 1], F32)
    ALOGc = p.sb("sb_alog", [32, 1], F32)
    DCOLc = p.sb("sb_dcol", [128, 16], F32)
    NWc = p.sb("sb_nw", [128, 16], F32)
    DTW = p.sb("sb_dtw", [128, KC, 32], BF16)
    al = p.emit("sync", lambda s: s.dma_start(
        CWc[:, :, :].rearrange("p a b -> p (a b)"),
        convw.ap().rearrange("p a b -> p (a b)")), inc="dAX", inc_n=16)
    al = p.emit("sync", lambda s: s.dma_start(CBc[:, :], convb.ap()),
                inc="dAX", inc_n=16)
    al = p.emit("sync", lambda s: s.dma_start(DCOLc[:, :], dcol.ap()),
                inc="dAX", inc_n=16)
    al = p.emit("sync", lambda s: s.dma_start(NWc[:, :], nw.ap()),
                inc="dAX", inc_n=16)
    dtwl = p.emit("gpsimd", lambda g: g.dma_start(
        DTW[:, :, :], ipw_dt.ap().rearrange("(kc kp) n -> kp kc n", kp=128)),
        inc="dDTW", inc_n=16)
    al = p.emit("sync", lambda s: s.dma_start(DTBc[:, :], dtb.ap()),
                inc="dAX", inc_n=16)
    al = p.emit("sync", lambda s: s.dma_start(ALOGc[:, :], alog.ap()),
                inc="dAX", inc_n=16)
    p.emit("tensor", None, waits=[CON["_ready"], ("dAX", al), ("dDTW", dtwl)])
    io = {"SB_X": SB_X, "CON": CON,
          "x_full": x_full.ap().rearrange("kc kp n -> kp kc n"),
          "x_res": x_res.ap().rearrange("kc kp n -> kp kc n"),
          "ipw_xbc": ipw_xbc.ap(), "ipw_z": ipw_z.ap(), "opw": opw.ap(),
          "convw": CWc, "convb": CBc, "dtb": DTBc, "alog": ALOGc,
          "dcol": DCOLc, "nw": NWc, "SB_DTW": DTW,
          "aux_ready": ("dAX", al), "cw_ready": ("dAX", al),
          "dtw_ready": ("dDTW", dtwl)}
    emit_l1_attn(p, io)
    xs, xv = io["x_ready"]
    out = p.emit("sync", lambda s: s.dma_start(
        x_out.ap().rearrange("kc kp n -> kp kc n"), SB_X[:, :, :]),
        waits=[(xs, xv)], inc="dIO", inc_n=16)
    p.emit("sync", None, waits=[("dIO", out)])
    p.finalize()
    return nc





L1FFN_IGS = ((0, 16), (16, 16), (32, 12))

def emit_l1_ffn(p, io):
    nc = p.nc
    SB_X, CON = io["SB_X"], io["CON"]
    scope = ExitStack()
    sb = lambda n, s, d: scope.enter_context(nc.sbuf_tensor(n, s, d))
    wl = WLoader(p, "wl1f", scope=scope)
    GSC, USC, DSC = io["gs"], io["us"], io["ds"]
    bank = [Bank(p, f"ff1_ps{i}", scope=scope) for i in range(4)]
    xready = io["x_ready"]

    scope_acc = ExitStack()
    ACC = scope_acc.enter_context(nc.sbuf_tensor("ff1_acc", [128, KC, T], F32,
                                                 side="right"))
    SB_H = scope_acc.enter_context(nc.sbuf_tensor("ff1_h", [128, KC, T], BF16,
                                                  side="right"))
    SB_G = sb("ff1_g", [128, 512], BF16)

    acc_last = [[0, 0] for _ in range(KC)]
    for igi, (c0, nch) in enumerate(L1FFN_IGS):
        h_last = {}
        for isl in range(nch // 4):
            gb, gbuf, gready = wl.load(io["gwT"], (c0 + isl * 4) * 128)
            ub, ubuf, uready = wl.load(io["uwT"], (c0 + isl * 4) * 128)
            lg = lu = 0
            for tc in range(2):
                for oc in range(4):
                    ic = c0 + isl * 4 + oc
                    hc = isl * 4 + oc
                    bkg, bku = bank[0 + (oc % 2)], bank[2 + (oc % 2)]
                    wg = [gready, xready] + bkg.acquire()
                    for kc in range(KC):
                        lg = p.emit("tensor",
                                    lambda t, bkg=bkg, kc=kc, oc=oc, tc=tc,
                                           gbuf=gbuf: t.matmul(
                                        bkg.pt[:, :],
                                        gbuf[:, kc, oc * 128:(oc + 1) * 128],
                                        SB_X[:, kc, tc * 512:(tc + 1) * 512],
                                        start=(kc == 0), stop=(kc == KC - 1)),
                                    waits=wg if kc == 0 else (),
                                    inc="sT" if kc == KC - 1 else None)
                    gsil = p.emit("scalar",
                                  lambda s, bkg=bkg, ic=ic: s.activation(
                                      SB_G[:, :], bkg.pt[:, :], AF.Silu,
                                      scale=GSC[:, ic:ic + 1]),
                                  waits=[("sT", lg), io["sc_ready"]], inc="sA")
                    bkg.reader("sA", gsil)
                    wu = [uready] + bku.acquire()
                    for kc in range(KC):
                        lu = p.emit("tensor",
                                    lambda t, bku=bku, kc=kc, oc=oc, tc=tc,
                                           ubuf=ubuf: t.matmul(
                                        bku.pt[:, :],
                                        ubuf[:, kc, oc * 128:(oc + 1) * 128],
                                        SB_X[:, kc, tc * 512:(tc + 1) * 512],
                                        start=(kc == 0), stop=(kc == KC - 1)),
                                    waits=wu if kc == 0 else (),
                                    inc="sT" if kc == KC - 1 else None)
                    hv = p.emit("vector",
                                lambda v, bku=bku, hc=hc, tc=tc, ic=ic:
                                v.scalar_tensor_tensor(
                                    SB_H[:, hc, tc * 512:(tc + 1) * 512],
                                    bku.pt[:, :], USC[:, ic:ic + 1],
                                    SB_G[:, :], ALU.mult, ALU.mult),
                                waits=[("sT", lu), ("sA", gsil)], inc="sV")
                    bku.reader("sV", hv)
                    h_last[(hc, tc)] = hv
                    p.emit("scalar", None, waits=[("sV", hv)])
            wl.consumed(gb, "sT", lg)
            wl.consumed(ub, "sT", lu)
        for jd in range(4):
            db, dbuf, dready = wl.load(io["dwT"].rearr_rows(igi), jd * 512, nk=nch)
            dmm = 0
            for oc in range(4):
                for tc in range(2):
                    fc = jd * 4 + oc
                    bk = bank[tc]
                    wd = [dready] + bk.acquire() + [("sV", h_last[(kch, tc)])
                                                    for kch in range(nch)]
                    for kch in range(nch):
                        dmm = p.emit("tensor",
                                     lambda t, bk=bk, kch=kch, oc=oc, tc=tc,
                                            dbuf=dbuf, nch=nch: t.matmul(
                                         bk.pt[:, :],
                                         dbuf[:, kch, oc * 128:(oc + 1) * 128],
                                         SB_H[:, kch, tc * 512:(tc + 1) * 512],
                                         start=(kch == 0), stop=(kch == nch - 1)),
                                     waits=wd if kch == 0 else (),
                                     inc="sT" if kch == nch - 1 else None)
                    if igi == 0:
                        ae = p.emit("vector",
                                    lambda v, bk=bk, fc=fc, tc=tc: v.tensor_copy(
                                        ACC[:, fc, tc * 512:(tc + 1) * 512],
                                        bk.pt[:, :]),
                                    waits=[("sT", dmm), ("sV", acc_last[fc][tc])],
                                    inc="sV")
                    else:
                        ae = p.emit("vector",
                                    lambda v, bk=bk, fc=fc, tc=tc: v.tensor_add(
                                        ACC[:, fc, tc * 512:(tc + 1) * 512],
                                        bk.pt[:, :],
                                        ACC[:, fc, tc * 512:(tc + 1) * 512]),
                                    waits=[("sT", dmm), ("sV", acc_last[fc][tc])],
                                    inc="sV")
                    acc_last[fc][tc] = ae
                    bk.reader("sV", ae)
            wl.consumed(db, "sT", dmm)

    SB_S = sb("ff1_s", [128, KC, T], BF16)
    sdone = 0
    for fc in range(KC):
        sdone = p.emit("vector",
                       lambda v, fc=fc: v.scalar_tensor_tensor(
                           SB_S[:, fc, :], ACC[:, fc, :], DSC[:, fc:fc + 1],
                           SB_X[:, fc, :], ALU.mult, ALU.add),
                       waits=[("sV", max(acc_last[fc])), ("sV", sdone),
                              io["sc_ready"]], inc="sV")
    scope_acc.close()
    OUTF = scope.enter_context(nc.sbuf_tensor("ff1_of", [128, KC, T], F32,
                                              side="right"))
    emit_ln(p, io, SB_S, SB_X, ("sV", sdone), scope, "ln3", bank[:2], out_f32=OUTF)
    io["OUTF"] = OUTF
    scope.close()


def build_l1_ffn():
    nc = bass.Bass("TRN2", target_bir_lowering=False)
    x_in = nc.dram_tensor("x_in", [KC, 128, T], BF16, kind="ExternalInput")
    gwT = nc.dram_tensor("gwT", [D, 5632], F32, kind="ExternalInput")
    uwT = nc.dram_tensor("uwT", [D, 5632], F32, kind="ExternalInput")
    dwT = nc.dram_tensor("dwT", [5632, D], F32, kind="ExternalInput")
    gs = nc.dram_tensor("gs", [128, 44], F32, kind="ExternalInput")
    us = nc.dram_tensor("us", [128, 44], F32, kind="ExternalInput")
    ds = nc.dram_tensor("ds", [128, 16], F32, kind="ExternalInput")
    x_out = nc.dram_tensor("x_out", [KC, 128, T], F32, kind="ExternalOutput")

    p = Prog(nc)
    for s in ("sT", "sV", "sA", "dIO", "dX"):
        p.sem(s)
    CON = setup_consts(nc, p, names=("onescol", "onesrow", "fconsts"))
    SB_X = p.sb("sb_x", [128, KC, T], BF16)
    SBgs = p.sb("sb_gs", [128, 44], F32)
    SBus = p.sb("sb_us", [128, 44], F32)
    SBds = p.sb("sb_ds", [128, 16], F32)
    xl = p.emit("sync", lambda s: s.dma_start(
        SB_X[:, :, :], x_in.ap().rearrange("kc kp n -> kp kc n")), inc="dX", inc_n=16)
    for src_d, dst in ((gs, SBgs), (us, SBus), (ds, SBds)):
        xl = p.emit("sync", lambda s, src_d=src_d, dst=dst: s.dma_start(
            dst[...], src_d.ap()), inc="dX", inc_n=16)
    p.emit("tensor", None, waits=[CON["_ready"], ("dX", xl)])

    class DwT:
        def __init__(self, ap):
            self.ap_ = ap
        def rearr_rows(self, igi):
            r0 = L1FFN_IGS[igi][0] * 128
            rows = L1FFN_IGS[igi][1] * 128
            class V:
                def __init__(s2, ap, r0, rows):
                    s2.ap = ap
                    s2.r0 = r0
                    s2.rows = rows
                def __getitem__(s2, key):
                    _, colsl = key
                    return s2.ap[s2.r0:s2.r0 + s2.rows, colsl]
            return V(self.ap_, r0, rows)

    io = {"SB_X": SB_X, "CON": CON, "x_ready": ("dX", xl),
          "gwT": gwT.ap(), "uwT": uwT.ap(), "dwT": DwT(dwT.ap()),
          "gs": SBgs, "us": SBus, "ds": SBds, "sc_ready": ("dX", xl)}
    emit_l1_ffn(p, io)
    OUTF = io["OUTF"]
    xs, xv = io["x_ready"]
    out = p.emit("sync", lambda s: s.dma_start(
        x_out.ap().rearrange("kc kp n -> kp kc n"), OUTF[:, :, :]),
        waits=[(xs, xv)], inc="dIO", inc_n=16)
    p.emit("sync", None, waits=[("dIO", out)])
    p.finalize()
    return nc





def block_barrier(p):
    snap = [(s, p.ctr.get(s, 0)) for s in ("sT", "sV", "sA", "sP")]
    p.emit("gpsimd", None, waits=snap)
    p.emit("sync", None, waits=snap)
    return snap


def build_fused():
    nc = bass.Bass("TRN2", target_bir_lowering=False)
    x_in = nc.dram_tensor("x_in", [D, T], F32, kind="ExternalInput")
    wd = {}
    for w in ("Wq", "Wk", "Wv", "Wu", "Wo"):
        wd[w] = nc.dram_tensor(w, [D, D], F32, kind="ExternalInput")
    U8 = mybir.dt.uint8
    gwT0 = nc.dram_tensor("gwT0", [D, 8192], U8, kind="ExternalInput")
    uwT0 = nc.dram_tensor("uwT0", [D, 8192], U8, kind="ExternalInput")
    dwT0 = nc.dram_tensor("dwT0", [8192, D], U8, kind="ExternalInput")
    gs0 = nc.dram_tensor("gs0", [128, 64], F32, kind="ExternalInput")
    us0 = nc.dram_tensor("us0", [128, 64], F32, kind="ExternalInput")
    ds0 = nc.dram_tensor("ds0", [128, 16], F32, kind="ExternalInput")
    ipw_xbc = nc.dram_tensor("ipw_xbc", [D, 3072], F32, kind="ExternalInput")
    ipw_dt = nc.dram_tensor("ipw_dt", [D, 32], F32, kind="ExternalInput")
    ipw_z = nc.dram_tensor("ipw_z", [D, 2048], F32, kind="ExternalInput")
    opw = nc.dram_tensor("opw", [D, 2048], F32, kind="ExternalInput")
    convw = nc.dram_tensor("convw", [128, 24, 4], F32, kind="ExternalInput")
    convb = nc.dram_tensor("convb", [128, 24], F32, kind="ExternalInput")
    dtb = nc.dram_tensor("dtb", [32, 1], F32, kind="ExternalInput")
    alog = nc.dram_tensor("alog", [32, 1], F32, kind="ExternalInput")
    dcol = nc.dram_tensor("dcol", [128, 16], F32, kind="ExternalInput")
    nw = nc.dram_tensor("nw", [128, 16], F32, kind="ExternalInput")
    gwT1 = nc.dram_tensor("gwT1", [D, 5632], F32, kind="ExternalInput")
    uwT1 = nc.dram_tensor("uwT1", [D, 5632], F32, kind="ExternalInput")
    dwT1 = nc.dram_tensor("dwT1", [5632, D], F32, kind="ExternalInput")
    gs1 = nc.dram_tensor("gs1", [128, 44], F32, kind="ExternalInput")
    us1 = nc.dram_tensor("us1", [128, 44], F32, kind="ExternalInput")
    ds1 = nc.dram_tensor("ds1", [128, 16], F32, kind="ExternalInput")
    x_out = nc.dram_tensor("x_out", [KC, 128, T], F32, kind="ExternalOutput")

    p = Prog(nc)
    for s in ("sT", "sV", "sA", "sP", "dIO", "dX", "dXH", "dKV", "dCC", "dOU",
              "dAX", "dDTW"):
        p.sem(s)
    CON = setup_consts(nc, p, names=("onescol", "onesrow", "fconsts"))
    SB_X = p.sb("sb_x", [128, KC, T], BF16)
    SBgs0 = p.sb("sb_gs0", [128, 64], F32)
    SBus0 = p.sb("sb_us0", [128, 64], F32)
    SBds0 = p.sb("sb_ds0", [128, 16], F32)
    CWc = p.sb("sb_cw", [128, 24, 4], F32)
    CBc = p.sb("sb_cb", [128, 24], F32)
    DTBc = p.sb("sb_dtb", [32, 1], F32)
    ALOGc = p.sb("sb_alog", [32, 1], F32)
    DCOLc = p.sb("sb_dcol", [128, 16], F32)
    NWc = p.sb("sb_nw", [128, 16], F32)
    DTW = p.sb("sb_dtw", [128, KC, 32], BF16)
    SBgs1 = p.sb("sb_gs1", [128, 44], F32)
    SBus1 = p.sb("sb_us1", [128, 44], F32)
    SBds1 = p.sb("sb_ds1", [128, 16], F32)
    al = 0
    for src_d, dst in ((gs0, SBgs0), (us0, SBus0), (ds0, SBds0),
                       (convb, CBc), (dcol, DCOLc), (nw, NWc),
                       (dtb, DTBc), (alog, ALOGc), (gs1, SBgs1), (us1, SBus1),
                       (ds1, SBds1)):
        al = p.emit("sync", lambda s, src_d=src_d, dst=dst: s.dma_start(
            dst[...], src_d.ap()), inc="dAX", inc_n=16)
    al = p.emit("sync", lambda s: s.dma_start(
        CWc[:, :, :].rearrange("p a b -> p (a b)"),
        convw.ap().rearrange("p a b -> p (a b)")), inc="dAX", inc_n=16)
    dtwl = p.emit("gpsimd", lambda g: g.dma_start(
        DTW[:, :, :], ipw_dt.ap().rearrange("(kc kp) n -> kp kc n", kp=128)),
        inc="dDTW", inc_n=16)
    xl = p.emit("gpsimd", lambda g: g.dma_start(SB_X[:, :, :], frearr(x_in.ap())),
                inc="dX", inc_n=16)
    p.emit("tensor", None, waits=[CON["_ready"], ("dAX", al), ("dDTW", dtwl),
                                  ("dX", xl)])

    class DwT:
        def __init__(self, ap, igs):
            self.ap_ = ap
            self.igs = igs
        def rearr_rows(self, igi):
            r0 = self.igs[igi][0] * 128
            rows = self.igs[igi][1] * 128
            class V:
                def __init__(s2, ap, r0, rows):
                    s2.ap = ap
                    s2.r0 = r0
                    s2.rows = rows
                def __getitem__(s2, key):
                    _, colsl = key
                    return s2.ap[s2.r0:s2.r0 + s2.rows, colsl]
            return V(self.ap_, r0, rows)

    io = {"SB_X": SB_X, "CON": CON, "x_ready": ("dX", xl), "fused": True}
    for w in wd:
        io[w] = wd[w].ap()
    emit_l0_attn(p, io)
    block_barrier(p)
    io.update({"gwT": gwT0.ap(), "uwT": uwT0.ap(),
               "dwT": DwT(dwT0.ap(), ((0, 16), (16, 16), (32, 16), (48, 16))),
               "gs": SBgs0, "us": SBus0, "ds": SBds0})
    emit_l0_ffn(p, io)
    block_barrier(p)
    io.update({"ipw_xbc": ipw_xbc.ap(), "ipw_z": ipw_z.ap(), "opw": opw.ap(),
               "convw": CWc, "convb": CBc, "dtb": DTBc, "alog": ALOGc,
               "dcol": DCOLc, "nw": NWc, "SB_DTW": DTW,
               "aux_ready": ("dAX", al), "cw_ready": ("dAX", al),
               "dtw_ready": ("dDTW", dtwl)})
    emit_l1_attn(p, io)
    block_barrier(p)
    io.update({"gwT": gwT1.ap(), "uwT": uwT1.ap(),
               "dwT": DwT(dwT1.ap(), L1FFN_IGS),
               "gs": SBgs1, "us": SBus1, "ds": SBds1,
               "sc_ready": ("dAX", al)})
    emit_l1_ffn(p, io)
    OUTF = io["OUTF"]
    xs, xv = io["x_ready"]
    out = p.emit("sync", lambda s: s.dma_start(
        x_out.ap().rearrange("kc kp n -> kp kc n"), OUTF[:, :, :]),
        waits=[(xs, xv)], inc="dIO", inc_n=16)
    p.emit("sync", None, waits=[("dIO", out)])
    p.finalize()
    return nc



_BUILD_CACHE = {}


def _get(name, fn):
    if name not in _BUILD_CACHE:
        _BUILD_CACHE[name] = fn()
    return _BUILD_CACHE[name]


def kernel(L0_attn_x, L0_attn_Wq, L0_attn_Wk, L0_attn_Wv, L0_attn_Wu, L0_attn_Wo,
           L0_ffn_gate_w, L0_ffn_gate_scale, L0_ffn_up_w, L0_ffn_up_scale,
           L0_ffn_down_w, L0_ffn_down_scale,
           L1_attn_in_proj_w, L1_attn_conv_w, L1_attn_conv_b, L1_attn_dt_bias,
           L1_attn_A_log, L1_attn_D, L1_attn_norm_w, L1_attn_out_proj_w,
           L1_ffn_gate_w, L1_ffn_up_w, L1_ffn_down_w, L1_ffn_gate_s, L1_ffn_up_s,
           L1_ffn_down_s):
    x = np.asarray(L0_attn_x)
    consts = make_consts()
    cores = list(range(8))
    nc = _get("fused", build_fused)

    Ws = {"Wq": np.asarray(L0_attn_Wq), "Wk": np.asarray(L0_attn_Wk),
          "Wv": np.asarray(L0_attn_Wv), "Wu": np.asarray(L0_attn_Wu),
          "Wo": np.asarray(L0_attn_Wo)}
    shared = {
        **Ws,
        "gwT0": np.ascontiguousarray(np.ascontiguousarray(
            np.asarray(L0_ffn_gate_w).T).view(np.uint8)[:, 0::4]),
        "uwT0": np.ascontiguousarray(np.ascontiguousarray(
            np.asarray(L0_ffn_up_w).T).view(np.uint8)[:, 0::4]),
        "dwT0": np.ascontiguousarray(np.ascontiguousarray(
            np.asarray(L0_ffn_down_w).T).view(np.uint8)[:, 0::4]),
        "gs0": np.ascontiguousarray(np.asarray(L0_ffn_gate_scale).reshape(64, 128).T),
        "us0": np.ascontiguousarray(np.asarray(L0_ffn_up_scale).reshape(64, 128).T),
        "ds0": np.ascontiguousarray(np.asarray(L0_ffn_down_scale).reshape(16, 128).T),
        "gwT1": np.ascontiguousarray(np.asarray(L1_ffn_gate_w).T),
        "uwT1": np.ascontiguousarray(np.asarray(L1_ffn_up_w).T),
        "dwT1": np.ascontiguousarray(np.asarray(L1_ffn_down_w).T),
        "gs1": np.ascontiguousarray(np.asarray(L1_ffn_gate_s).reshape(44, 128).T),
        "us1": np.ascontiguousarray(np.asarray(L1_ffn_up_s).reshape(44, 128).T),
        "ds1": np.ascontiguousarray(np.asarray(L1_ffn_down_s).reshape(16, 128).T),
        **consts,
    }
    ipT = np.ascontiguousarray(np.asarray(L1_attn_in_proj_w).T)
    opT = np.ascontiguousarray(np.asarray(L1_attn_out_proj_w).T)
    cw = np.asarray(L1_attn_conv_w)
    cb = np.asarray(L1_attn_conv_b)
    dtbv = np.asarray(L1_attn_dt_bias)
    alogv = np.asarray(L1_attn_A_log)
    Dv = np.asarray(L1_attn_D)
    nwv = np.asarray(L1_attn_norm_w)
    maps = []
    for c in cores:
        b, gh = c // 2, c % 2
        xcols = np.concatenate(
            [ipT[:, 4096 + gh * 2048:4096 + gh * 2048 + 2048],
             ipT[:, 8192 + gh * 512:8192 + gh * 512 + 512],
             ipT[:, 9216 + gh * 512:9216 + gh * 512 + 512]], axis=1)
        cwsl = np.concatenate([cw[gh * 2048:(gh + 1) * 2048],
                               cw[4096 + gh * 512:4096 + gh * 512 + 512],
                               cw[5120 + gh * 512:5120 + gh * 512 + 512]], axis=0)
        cbsl = np.concatenate([cb[gh * 2048:(gh + 1) * 2048],
                               cb[4096 + gh * 512:4096 + gh * 512 + 512],
                               cb[5120 + gh * 512:5120 + gh * 512 + 512]], axis=0)
        maps.append({
            "x_in": np.ascontiguousarray(x[b, gh * T:(gh + 1) * T, :].T),
            "ipw_xbc": np.ascontiguousarray(xcols),
            "ipw_dt": np.ascontiguousarray(
                ipT[:, 10240 + gh * 32:10240 + gh * 32 + 32]),
            "ipw_z": np.ascontiguousarray(ipT[:, gh * 2048:(gh + 1) * 2048]),
            "opw": np.ascontiguousarray(opT[gh * 2048:(gh + 1) * 2048, :]),
            "convw": np.ascontiguousarray(
                cwsl.reshape(24, 128, 4).transpose(1, 0, 2)),
            "convb": np.ascontiguousarray(cbsl.reshape(24, 128).T),
            "dtb": np.ascontiguousarray(dtbv[gh * 32:(gh + 1) * 32, None]),
            "alog": np.ascontiguousarray(alogv[gh * 32:(gh + 1) * 32, None]),
            "dcol": np.ascontiguousarray(
                np.repeat(Dv[gh * 32:(gh + 1) * 32], 64).reshape(16, 128).T),
            "nw": np.ascontiguousarray(
                nwv[gh * 2048:(gh + 1) * 2048].reshape(16, 128).T),
            **shared})
    r = run_bass_kernel_spmd(nc, maps, core_ids=cores)
    out = np.empty((4, 2048, 2048), np.float32)
    for c in cores:
        b, h = c // 2, c % 2
        out[b, h * T:(h + 1) * T, :] = np.asarray(
            r.results[c]["x_out"]).reshape(2048, T).T
    return out
